# version 16
# speedup vs baseline: 1.0925x; 1.0064x over previous
import contextlib
import math

import numpy as np
import ml_dtypes

import concourse.bass as bass
import concourse.mybir as mybir
from concourse.bass_utils import run_bass_kernel_spmd

F32 = mybir.dt.float32
BF16 = mybir.dt.bfloat16
AF = mybir.ActivationFunctionType
ALU = mybir.AluOpType
AX = mybir.AxisListType

D = 1024
S = 2048
SQ = S + 1
NCH = 8
NB4 = 4
HID = 2730
HIDP = 2816
NJ = HIDP // 128
LN_EPS = 1e-5
LOG8 = math.log(8.0)

_DUMPS = ()
_STOP_AFTER = None
_SPLIT_EXP = False


class _StopEmission(Exception):
    pass


def _bcol(v):
    return np.ascontiguousarray(v.reshape(8, 128).T)


def _consts():
    c = {}
    bf = ml_dtypes.bfloat16
    c["lnones"] = np.full((128, 1), 1.0 / 1024.0, dtype=bf)
    sel2 = np.zeros((128, 2), np.float32)
    sel2[0:64, 0] = 1.0
    sel2[64:128, 1] = 1.0
    c["sel2"] = sel2.astype(bf)
    R = np.zeros((128, 128), np.float32)
    for h in range(2):
        for m in range(32):
            R[h * 64 + m + 32, h * 64 + m] = -1.0
            R[h * 64 + m, h * 64 + m + 32] = 1.0
    c["rmat"] = R.astype(bf)
    c["ident"] = np.eye(128, dtype=np.float32).astype(bf)
    u = np.arange(128)[:, None]
    q = np.arange(128)[None, :]
    c["mtriu"] = (q >= u).astype(np.float32).astype(bf)
    c["mtril"] = (u >= q).astype(np.float32).astype(bf)
    c["onesf"] = np.ones((1, 128), np.float32)
    selg = np.zeros((8, 512), np.float32)
    for hp in range(4):
        selg[2 * hp, 128 * hp:128 * hp + 64] = 1.0
        selg[2 * hp + 1, 128 * hp + 64:128 * hp + 128] = 1.0
    c["selg_bf"] = selg.astype(bf)
    selg16 = np.zeros((16, 1024), np.float32)
    for hp in range(8):
        selg16[2 * hp, 128 * hp:128 * hp + 64] = 1.0
        selg16[2 * hp + 1, 128 * hp + 64:128 * hp + 128] = 1.0
    c["selg16_bf"] = selg16.astype(bf)
    sq_ = np.zeros((16, 512), np.float32)
    sk_ = np.zeros((16, 512), np.float32)
    for hp in range(4):
        sq_[2 * hp, 128 * hp:128 * hp + 64] = 1.0
        sq_[2 * hp + 1, 128 * hp + 64:128 * hp + 128] = 1.0
        sk_[8 + 2 * hp, 128 * hp:128 * hp + 64] = 1.0
        sk_[8 + 2 * hp + 1, 128 * hp + 64:128 * hp + 128] = 1.0
    c["selgQ"] = sq_.astype(bf)
    c["selgK"] = sk_.astype(bf)
    s16q = np.zeros((128, 64), np.float32)
    s16k = np.zeros((128, 64), np.float32)
    for hp in range(4):
        s16q[0:64, 16 * hp + 2 * hp] = 1.0
        s16q[64:128, 16 * hp + 2 * hp + 1] = 1.0
        s16k[0:64, 16 * hp + 8 + 2 * hp] = 1.0
        s16k[64:128, 16 * hp + 8 + 2 * hp + 1] = 1.0
    c["sel16q"] = s16q.astype(bf)
    c["sel16k"] = s16k.astype(bf)
    sC = np.zeros((16, 1024), np.float32)
    for h in range(8):
        sC[h, 128 * h:128 * (h + 1)] = 0.5
        sC[h + 8, 128 * h:128 * (h + 1)] = 0.5
    c["selC"] = sC.astype(bf)
    s2f = np.zeros((2, 256), np.float32)
    s2f[0, 0:128] = 1.0
    s2f[1, 128:256] = 1.0
    c["sel2f"] = s2f
    dh = 64
    half = np.arange(0, dh, 2, dtype=np.float64)
    inv_freq = 1.0 / (10000.0 ** (half / dh))
    t = np.arange(S, dtype=np.float64)
    freqs = np.concatenate([t[:, None] * inv_freq[None, :]] * 2, axis=1)
    base = (half + 0.4 * dh) / (1.4 * dh)
    power = (t - S // 2) / (512 // 2)
    scale = np.concatenate([base[None, :] ** power[:, None]] * 2, axis=1)
    cos, sin = np.cos(freqs), np.sin(freqs)
    dup = lambda a: np.ascontiguousarray(
        np.concatenate([a.T, a.T], axis=0).astype(np.float32))
    c["tcs"] = dup(cos * scale)
    c["tss"] = dup(sin * scale)
    c["tcis"] = dup(cos / scale)
    c["tsis"] = dup(sin / scale)
    return c


def _prep(inp):
    d = {}
    f32 = np.float32
    T = lambda a: np.ascontiguousarray(np.asarray(a, f32).T)
    A = lambda a: np.ascontiguousarray(np.asarray(a, f32))
    d["wq_t"] = T(inp["L0_attn_Wq"])
    d["wk_t"] = T(inp["L0_attn_Wk"])
    d["wv_t"] = T(inp["L0_attn_Wv"])
    d["wo_t"] = T(inp["L0_attn_Wo"])
    d["bq"] = _bcol(A(inp["L0_attn_bq"]))
    d["bk"] = _bcol(A(inp["L0_attn_bk"]))
    d["bv"] = _bcol(A(inp["L0_attn_bv"]))
    d["bo"] = _bcol(A(inp["L0_attn_bo"]))
    for l in range(2):
        d[f"wqkv{l}"] = A(inp["L0_ffn_wqkv"][l])
        d[f"qs{l}"] = np.ascontiguousarray(
            np.tile(A(inp["L0_ffn_q_scale"][l]), 2)[:, None])
        d[f"ks{l}"] = np.ascontiguousarray(
            np.tile(A(inp["L0_ffn_k_scale"][l]), 2)[:, None])
        d[f"gwo{l}"] = A(inp["L0_ffn_wo"][l])
        d[f"lng{l}"] = _bcol(A(inp["L0_ffn_ln_g"][l]))
        d[f"lnb{l}"] = _bcol(A(inp["L0_ffn_ln_b"][l]))
        w1 = A(inp["L0_ffn_w1"][l])
        w1a = np.concatenate([w1[:, :HID], np.zeros((D, HIDP - HID), f32)], 1)
        w1g = np.concatenate([w1[:, HID:], np.zeros((D, HIDP - HID), f32)], 1)
        w1pp = np.empty((D, 2 * HIDP), f32)
        for j in range(NJ):
            w1pp[:, 256 * j:256 * j + 128] = w1a[:, 128 * j:128 * (j + 1)]
            w1pp[:, 256 * j + 128:256 * j + 256] = w1g[:, 128 * j:128 * (j + 1)]
        d[f"w1pp{l}"] = w1pp
        b1 = A(inp["L0_ffn_b1"][l])
        b1a = np.concatenate([b1[:HID], np.zeros(HIDP - HID, f32)])
        b1g = np.concatenate([b1[HID:], np.zeros(HIDP - HID, f32)])
        b1pp = np.empty((128, 2 * NJ), f32)
        for j in range(NJ):
            b1pp[:, 2 * j] = b1a[128 * j:128 * (j + 1)]
            b1pp[:, 2 * j + 1] = b1g[128 * j:128 * (j + 1)]
        d[f"b1pp{l}"] = b1pp
        w2 = A(inp["L0_ffn_w2"][l])
        w2p = np.concatenate([w2, np.zeros((HIDP - HID, D), f32)], 0)
        d[f"w2pt{l}"] = np.ascontiguousarray(
            w2p.reshape(NJ, 128, 8, 128).transpose(2, 1, 0, 3)
            .reshape(D, HIDP))
        d[f"b2{l}"] = _bcol(A(inp["L0_ffn_b2"][l]))
    d.update(_consts())
    return d


_SPEC_F32 = [
    ("wq_t", [D, D]), ("wk_t", [D, D]), ("wv_t", [D, D]), ("wo_t", [D, D]),
    ("bq", [128, 8]), ("bk", [128, 8]), ("bv", [128, 8]), ("bo", [128, 8]),
    ("onesf", [1, 128]), ("sel2f", [2, 256]),
    ("tcs", [128, S]), ("tss", [128, S]), ("tcis", [128, S]), ("tsis", [128, S]),
]
for _l in range(2):
    _SPEC_F32 += [
        (f"wqkv{_l}", [D, 1536]), (f"qs{_l}", [128, 1]), (f"ks{_l}", [128, 1]),
        (f"gwo{_l}", [512, D]), (f"lng{_l}", [128, 8]), (f"lnb{_l}", [128, 8]),
        (f"w1pp{_l}", [D, 2 * HIDP]), (f"b1pp{_l}", [128, 2 * NJ]),
        (f"w2pt{_l}", [D, HIDP]), (f"b2{_l}", [128, 8]),
    ]
_SPEC_BF = [
    ("lnones", [128, 1]), ("sel2", [128, 2]), ("rmat", [128, 128]),
    ("ident", [128, 128]), ("mtriu", [128, 128]), ("mtril", [128, 128]),
    ("selg_bf", [8, 512]), ("selg16_bf", [16, 1024]),
    ("selgQ", [16, 512]), ("selgK", [16, 512]), ("selC", [16, 1024]),
    ("sel16q", [128, 64]), ("sel16k", [128, 64]),
]
_RESIDENT = ([n for n, _ in _SPEC_BF
              if n not in ("selgQ", "selgK", "selC", "selg16_bf")]
             + ["bq", "bk", "bv", "bo", "onesf", "sel2f"]
             + [f"{n}{l}" for l in range(2)
                for n in ("qs", "ks", "lng", "lnb", "b1pp", "b2")])


def _att_iter(branch):
    out = []
    for qq in range(4):
        lo, hi = 4 * qq, 4 * qq + 3
        urange = range(0, hi + 1) if branch == "s" else range(max(0, lo - 4), hi + 1)
        for U in urange:
            Qlo = max(U, lo)
            Qhi = hi if branch == "s" else min(U + 4, hi)
            if Qlo > Qhi:
                continue
            tiles = []
            for Q in range(Qlo, Qhi + 1):
                m = None
                if Q == U:
                    m = "triu"
                elif branch == "g" and Q == U + 4:
                    m = "tril"
                tiles.append((Q, m))
            out.append((qq, U, Qlo, tiles))
    return out


class _B:
    def __init__(self):
        self.nc = bass.Bass("TRN2", target_bir_lowering=False,
                            detect_race_conditions=False)
        self.st = contextlib.ExitStack()
        nc = self.nc
        self.xd = nc.dram_tensor("x", [D, S], F32, kind="ExternalInput").ap()
        self.out_d = nc.dram_tensor("out", [D, S], F32, kind="ExternalOutput").ap()
        self.lnmr = nc.dram_tensor("lnmr_scr", [1, S], F32).ap()
        self.lnrs = nc.dram_tensor("lnrs_scr", [1, S], F32).ap()
        self.lnmr0 = nc.dram_tensor("lnmr0_scr", [1, S], F32).ap()
        self.lnrs0 = nc.dram_tensor("lnrs0_scr", [1, S], F32).ap()
        self.scr0_sd = 0
        self.dr = {}
        for name, shp in _SPEC_F32:
            self.dr[name] = nc.dram_tensor(name, shp, F32, kind="ExternalInput").ap()
        for name, shp in _SPEC_BF:
            self.dr[name] = nc.dram_tensor(name, shp, BF16, kind="ExternalInput").ap()
        self.dumps = {}
        for name in _DUMPS:
            self.dumps[name] = nc.dram_tensor(
                "dump_" + name, [D, S], F32, kind="ExternalOutput").ap()

        self.sem = {k: self.st.enter_context(nc.semaphore("sem_" + k))
                    for k in ("t", "v", "a", "gc", "gd", "sd")}
        self.cnt = {k: 0 for k in self.sem}
        self.srow_sd = 0

        self.xg = [self.st.enter_context(nc.sbuf_tensor(f"xg{c}", [128, SQ], F32))
                   for c in range(NCH)]
        self.ct = {}
        shp_f32 = dict(_SPEC_F32)
        shp_bf = dict(_SPEC_BF)
        for name in _RESIDENT:
            if name in shp_bf:
                self.ct[name] = self.st.enter_context(
                    nc.sbuf_tensor("c_" + name, shp_bf[name], BF16))
            else:
                self.ct[name] = self.st.enter_context(
                    nc.sbuf_tensor("c_" + name, shp_f32[name], F32))

    _uid = 0

    def sb(self, name, shape, dtype):
        _B._uid += 1
        return self.nc.sbuf_tensor(f"{name}_{_B._uid}", shape, dtype)

    def ps(self, name, shape, dtype):
        _B._uid += 1
        return self.nc.psum_tensor(f"{name}_{_B._uid}", shape, dtype)

    def inc(self, key, instr, n=1):
        instr.then_inc(self.sem[key], n)
        self.cnt[key] += n
        return self.cnt[key]

    def wt(self, eng, *deps):
        for dkey, val in deps:
            if val <= 0:
                continue
            mult = 16 if dkey in ("gd", "sd") else 1
            eng.wait_ge(self.sem[dkey], mult * val)

    def now(self, key):
        return (key, self.cnt[key])

    def gload(self, out_ap, in_ap, *deps):
        self.wt(self.nc.gpsimd, *deps)
        self.nc.gpsimd.dma_start(out=out_ap, in_=in_ap).then_inc(self.sem["gd"], 16)
        self.cnt["gd"] += 1
        return self.cnt["gd"]

    def sload(self, out_ap, in_ap, *deps):
        self.wt(self.nc.sync, *deps)
        self.nc.sync.dma_start(out=out_ap, in_=in_ap).then_inc(self.sem["sd"], 16)
        self.cnt["sd"] += 1
        return self.cnt["sd"]

    def ckpt(self, name):
        self.barrier()
        if _STOP_AFTER == name:
            raise _StopEmission()

    def barrier(self):
        snap = [(k, self.cnt[k]) for k in self.sem]
        for eng in (self.nc.tensor, self.nc.vector, self.nc.scalar,
                    self.nc.gpsimd, self.nc.sync):
            self.wt(eng, *snap)

    def ln_finalize(self, pm, pq, mrow, rrow, dst=None):
        nc = self.nc
        dmr, drs = dst if dst is not None else (self.lnmr, self.lnrs)
        self.wt(nc.vector, self.now("t"))
        self.inc("v", nc.vector.tensor_copy(out=mrow[:], in_=pm[:]))
        vd = self.inc("v", nc.vector.tensor_copy(out=rrow[:], in_=pq[:]))
        self.wt(nc.scalar, ("v", vd))
        ad = self.inc("a", nc.scalar.activation(out=pm[:], in_=mrow[:],
                                                func=AF.Square))
        self.wt(nc.vector, ("a", ad))
        nc.vector.tensor_sub(out=rrow[:], in0=rrow[:], in1=pm[:])
        vd = self.inc("v", nc.vector.tensor_scalar_add(out=rrow[:], in0=rrow[:],
                                                       scalar1=LN_EPS))
        self.wt(nc.scalar, ("v", vd))
        nc.scalar.activation(out=rrow[:], in_=rrow[:], func=AF.Ln)
        ad = self.inc("a", nc.scalar.activation(out=rrow[:], in_=rrow[:],
                                                func=AF.Exp, scale=-0.5))
        self.wt(nc.vector, ("a", ad))
        vd = self.inc("v", nc.vector.tensor_mul(out=mrow[:], in0=mrow[:],
                                                in1=rrow[:]))
        self.sload(dmr, mrow[:], ("v", vd))
        return self.sload(drs, rrow[:])

    def ln_bcast(self, pmb, prb, mrow, rrow, scr_dep, src=None):
        nc = self.nc
        smr, srs = src if src is not None else (self.lnmr, self.lnrs)
        sd1 = self.sload(mrow[:], smr, ("sd", scr_dep))
        sd2 = self.sload(rrow[:], srs)
        onesf = self.ct["onesf"]
        self.wt(nc.tensor, ("sd", sd2))
        i = None
        for n in range(NB4):
            sl = slice(512 * n, 512 * (n + 1))
            nc.tensor.matmul(pmb[:, sl], onesf[:], mrow[0:1, sl],
                             start=True, stop=True)
            i = nc.tensor.matmul(prb[:, sl], onesf[:], rrow[0:1, sl],
                                 start=True, stop=True)
        return self.inc("t", i)

    def att_pair_gen(self, branch, qtile, ktile, vo_a, vo_b, bias_a, bias_b,
                     scale, oun_tile, sums_t, row0, srow, pqk_a, pqk_b,
                     ppv_a, ppv_b, pexp_a2, pexp_b2, st, entry_deps=()):
        nc = self.nc
        items = _att_iter(branch)
        nit = len(items)
        pend = None

        def emit_pv(p):
            (qq_, U_, first_, last_, eA, eB, vA, vB, par_) = p
            self.wt(nc.tensor, ("a", eA), ("v", vA),
                    ("a", st["evacA"] if first_ else 0))
            pvA = self.inc("t", nc.tensor.matmul(
                ppv_a[0:65, 0:512], vo_a[:, 65 * U_:65 * U_ + 65],
                pexp_a2[par_][:, 0:512], start=first_, stop=last_))
            self.wt(nc.tensor, ("a", eB), ("v", vB),
                    ("a", st["evacB"] if first_ else 0))
            pvB = self.inc("t", nc.tensor.matmul(
                ppv_b[0:65, 0:512], vo_b[:, 65 * U_:65 * U_ + 65],
                pexp_b2[par_][:, 0:512], start=first_, stop=last_))
            st["pv2"][par_] = pvB
            st["pv"] = pvB
            if last_:
                self.wt(nc.scalar, ("t", pvB), ("sd", self.srow_sd))
                qsl = slice(512 * qq_, 512 * (qq_ + 1))
                nc.scalar.copy(out=oun_tile[0:64, qsl], in_=ppv_a[0:64, 0:512])
                nc.scalar.copy(out=oun_tile[64:128, qsl],
                               in_=ppv_b[0:64, 0:512])
                nc.scalar.copy(out=srow[0:1, 0:512], in_=ppv_a[64:65, 0:512])
                ev = self.inc("a", nc.scalar.copy(
                    out=srow[0:1, 512:1024], in_=ppv_b[64:65, 0:512]))
                st["evacA"] = ev
                st["evacB"] = ev
                self.srow_sd = self.sload(
                    sums_t[row0:row0 + 2, qsl], srow[0:1, 0:1024], ("a", ev))

        for k in range(nit):
            qq, U, Qlo, tiles = items[k]
            span = 128 * len(tiles)
            off = 128 * (Qlo - 4 * qq)
            if k == 0:
                self.wt(nc.tensor, *entry_deps)
            self.wt(nc.tensor, ("a", st["expA"]))
            nc.tensor.matmul(pqk_a[:, 0:span],
                             ktile[0:64, 128 * U:128 * (U + 1)],
                             qtile[0:64, 128 * Qlo:128 * Qlo + span],
                             start=True, stop=True)
            self.wt(nc.tensor, ("a", st["expB"]))
            qk = self.inc("t", nc.tensor.matmul(
                pqk_b[:, 0:span], ktile[64:128, 128 * U:128 * (U + 1)],
                qtile[64:128, 128 * Qlo:128 * Qlo + span],
                start=True, stop=True))
            if pend is not None:
                emit_pv(pend)
                pend = None
            par = st["ep"] % 2
            st["ep"] += 1
            pexp_a = pexp_a2[par]
            pexp_b = pexp_b2[par]
            pvdep = st["pv2"][par]
            nz = 0
            zlast = None
            if off > 0 or off + span < 512:
                self.wt(nc.vector, ("t", pvdep))
                for pe in (pexp_a, pexp_b):
                    if off > 0:
                        zlast = nc.vector.memset(pe[:, 0:off], 0.0)
                        nz = 1
                    if off + span < 512:
                        zlast = nc.vector.memset(pe[:, off + span:512], 0.0)
                        nz = 1
            ztick = self.inc("v", zlast) if nz else 0
            self.wt(nc.scalar, ("t", max(qk, pvdep)))
            st["expA"] = self.inc("a", nc.scalar.activation(
                out=pexp_a[:, off:off + span], in_=pqk_a[:, 0:span],
                func=AF.Exp, bias=bias_a, scale=scale))
            st["expB"] = self.inc("a", nc.scalar.activation(
                out=pexp_b[:, off:off + span], in_=pqk_b[:, 0:span],
                func=AF.Exp, bias=bias_b, scale=scale))
            nmask = 0
            last = None
            for idx, (Q, m) in enumerate(tiles):
                if m is None:
                    continue
                mt = self.ct["mtriu"] if m == "triu" else self.ct["mtril"]
                o2 = off + 128 * idx
                if nmask == 0:
                    self.wt(nc.vector, ("a", st["expB"]))
                nc.vector.tensor_mul(out=pexp_a[:, o2:o2 + 128],
                                     in0=pexp_a[:, o2:o2 + 128], in1=mt[:])
                last = nc.vector.tensor_mul(out=pexp_b[:, o2:o2 + 128],
                                            in0=pexp_b[:, o2:o2 + 128],
                                            in1=mt[:])
                nmask += 1
            mtick = self.inc("v", last) if nmask else 0
            first = (k == 0 or items[k - 1][0] != qq)
            lastu = (k + 1 >= nit or items[k + 1][0] != qq)
            pend = (qq, U, first, lastu, st["expA"], st["expB"],
                    max(mtick, ztick), max(mtick, ztick), par)
            yield
        emit_pv(pend)

    def v_block(self, pP, n, vbt, vo_a, vo_b, pTs, tr_dep, bias=None,
                vo_guard=0):
        nc = self.nc
        self.wt(nc.scalar, self.now("t"))
        if bias is None:
            ev = self.inc("a", nc.scalar.copy(out=vbt[:], in_=pP[:]))
        else:
            ev = self.inc("a", nc.scalar.activation(out=vbt[:], in_=pP[:],
                                                    func=AF.Identity, bias=bias))
        for j in range(4):
            U = 4 * n + j
            pT = pTs[U % 2]
            self.wt(nc.tensor, ("a", ev), ("v", tr_dep[U % 2]))
            td = self.inc("t", nc.tensor.transpose(
                pT[:], vbt[:, 128 * j:128 * (j + 1)], self.ct["ident"][:]))
            self.wt(nc.vector, ("t", td), ("t", vo_guard))
            nc.vector.tensor_copy(out=vo_a[:, 65 * U:65 * U + 64],
                                  in_=pT[:, 0:64])
            tr_dep[U % 2] = self.inc("v", nc.vector.tensor_copy(
                out=vo_b[:, 65 * U:65 * U + 64], in_=pT[:, 64:128]))
        return ev


def _build():
    b = _B()
    try:
        _build_body(b)
        b.barrier()
        b.st.close()
    except _StopEmission:
        b.barrier()
    return b.nc


def _build_body(b):
    nc = b.nc
    xg = b.xg
    ct = b.ct
    xd = b.xd

    for name in _RESIDENT:
        b.sload(ct[name][:], b.dr[name])
    for c in range(NCH):
        b.wt(nc.gpsimd, b.now("sd"))
        b.inc("gc", nc.gpsimd.memset(xg[c][:, 0:1], 0.0))
    b.barrier()

    for l in range(2):
        with contextlib.ExitStack() as sc:
            xb = [sc.enter_context(b.sb(f"xb{c}", [128, SQ], BF16))
                  for c in range(NCH)]
            vo = [sc.enter_context(b.sb(f"vog{i}", [128, 16 * 65], BF16))
                  for i in range(8)]
            oun_g = [sc.enter_context(b.sb(f"og{i}", [128, S], BF16))
                     for i in range(4)]
            cbias = sc.enter_context(b.sb("cbias", [128, 8], F32))

            for i in range(8):
                oc = vo[i].ap().rearrange("p (u c) -> p u c", c=65)[:, :, 64:65]
                b.inc("gc", nc.gpsimd.memset(oc, 1.0))
            b.barrier()

            if l == 0:
                ln1 = ct["lnones"]
                with contextlib.ExitStack() as lsc:
                    sqb = lsc.enter_context(
                        b.sb("sqb0", [128, SQ], BF16))
                    mrow = lsc.enter_context(b.sb("mrow0", [1, S], F32))
                    rrow = lsc.enter_context(b.sb("rrow0", [1, S], F32))
                    sdl = [None] * NCH
                    for c in range(6):
                        sdl[c] = ("sd", b.sload(xg[c][:, 1:SQ],
                                                xd[128 * c:128 * (c + 1), :]))
                    with (b.ps("pm0", [1, S], F32) as pm,
                          b.ps("pq0", [1, S], F32) as pq):
                        dep_a = 0
                        for c in range(NCH):
                            gd = b.gload(xb[c][:, 1:SQ],
                                         xd[128 * c:128 * (c + 1), :])
                            b.wt(nc.scalar, ("gd", gd), ("a", dep_a),
                                 b.now("t"))
                            dep_a = b.inc("a", nc.scalar.activation(
                                out=sqb[:, 1:SQ], in_=xb[c][:, 1:SQ],
                                func=AF.Square))
                            b.wt(nc.tensor, ("a", dep_a))
                            i = None
                            for n in range(NB4):
                                sl = slice(1 + 512 * n, 1 + 512 * (n + 1))
                                nc.tensor.matmul(
                                    pm[0:1, 512 * n:512 * (n + 1)], ln1[:],
                                    xb[c][:, sl], start=(c == 0), stop=(c == 7))
                            for n in range(NB4):
                                sl = slice(1 + 512 * n, 1 + 512 * (n + 1))
                                i = nc.tensor.matmul(
                                    pq[0:1, 512 * n:512 * (n + 1)], ln1[:],
                                    sqb[:, sl], start=(c == 0), stop=(c == 7))
                            b.inc("t", i)
                        for c in range(6, NCH):
                            sdl[c] = ("gd", b.gload(
                                xg[c][:, 1:SQ], xd[128 * c:128 * (c + 1), :]))
                        scr = b.ln_finalize(pm, pq, mrow, rrow,
                                            dst=(b.lnmr0, b.lnrs0))
                        b.scr0_sd = scr
                    with (b.ps("pmb0", [128, S], F32) as pmb,
                          b.ps("prb0", [128, S], F32) as prb):
                        tdep = b.ln_bcast(pmb, prb, mrow, rrow, scr,
                                          src=(b.lnmr0, b.lnrs0))
                        for c in range(NCH):
                            b.wt(nc.vector, sdl[c], ("t", tdep))
                            nc.vector.tensor_mul(out=xg[c][:, 1:SQ],
                                                 in0=xg[c][:, 1:SQ], in1=prb[:])
                            vmu = b.inc("v", nc.vector.tensor_sub(
                                out=xg[c][:, 1:SQ], in0=xg[c][:, 1:SQ],
                                in1=pmb[:]))
                            b.wt(nc.scalar, ("v", vmu))
                            b.inc("a", nc.scalar.copy(
                                out=xb[c][:, 1:SQ], in_=xg[c][:, 1:SQ]))
                    b.barrier()
                b.ckpt(f"ln{l}")
                if "xn" in b.dumps:
                    for c in range(NCH):
                        b.sload(b.dumps["xn"][128 * c:128 * (c + 1), :],
                                xg[c][:, 1:SQ])
                    b.barrier()
            else:
                for c in range(NCH):
                    if c % 2 == 0:
                        b.inc("v", nc.vector.tensor_copy(
                            out=xb[c][:, 1:SQ], in_=xg[c][:, 1:SQ]))
                    else:
                        b.inc("a", nc.scalar.copy(
                            out=xb[c][:, 1:SQ], in_=xg[c][:, 1:SQ]))
                b.barrier()
            for c in range(NCH):
                b.inc("gc", nc.gpsimd.memset(xb[c][:, 0:1], 0.0))
            b.barrier()

            with contextlib.ExitStack() as isc:
                qb = [isc.enter_context(b.sb(f"qb{i}", [128, S], BF16))
                      for i in range(4)]
                kb = [isc.enter_context(b.sb(f"kb{i}", [128, S], BF16))
                      for i in range(4)]

                rq16 = isc.enter_context(b.sb("rq16", [16, S], BF16))
                with contextlib.ExitStack() as nsc:
                    n2all = nsc.enter_context(
                        b.sb("n2all", [16, S], BF16))
                    numall = nsc.enter_context(
                        b.sb("numall", [16, S], BF16))
                    dmax16 = nsc.enter_context(
                        b.sb("dmax16", [16, 1], BF16))
                    selC_t = nsc.enter_context(
                        b.sb("selC_t", [16, 1024], BF16))
                    selC_sd = b.sload(selC_t[:], b.dr["selC"])

                    with contextlib.ExitStack() as asc:
                        wall = asc.enter_context(
                            b.sb("wall", [128, 8 * D], BF16))
                        vbt = asc.enter_context(
                            b.sb("vbt", [128, 512], BF16))
                        tbg2 = [asc.enter_context(
                            b.sb(f"tbg2{i}", [128, 512], BF16))
                            for i in range(2)]
                        sel16w = asc.enter_context(
                            b.sb("sel16w", [128, 128], BF16))
                        qk2t = asc.enter_context(
                            b.sb("qk2t", [128, 2], F32))

                        qsc, ksc = ct[f"qs{l}"], ct[f"ks{l}"]
                        b.wt(nc.vector, b.now("sd"))
                        nc.vector.tensor_mul(out=qk2t[:, 0:1], in0=qsc[:],
                                             in1=qsc[:])
                        nc.vector.tensor_mul(out=qk2t[:, 1:2], in0=ksc[:],
                                             in1=ksc[:])
                        nc.vector.tensor_scalar(
                            out=sel16w[:, 0:64], in0=ct["sel16q"][:],
                            scalar1=qk2t[:, 0:1], scalar2=None, op0=ALU.mult)
                        b.inc("v", nc.vector.tensor_scalar(
                            out=sel16w[:, 64:128], in0=ct["sel16k"][:],
                            scalar1=qk2t[:, 1:2], scalar2=None, op0=ALU.mult))
                        b.barrier()

                        with (b.ps("pP0", [128, 512], F32) as pP0,
                              b.ps("pP1", [128, 512], F32) as pP1,
                              b.ps("pN16a", [16, 512], F32) as pN16a,
                              b.ps("pN16b", [16, 512], F32) as pN16b,
                              b.ps("pT0", [128, 128], BF16) as pT0,
                              b.ps("pT1", [128, 128], BF16) as pT1):
                            pPs = [pP0, pP1]
                            evac_dep = [0, 0]
                            pn_dep = 0
                            tr_dep = [0, 0]
                            pi = 0
                            wsrc = b.dr[f"wqkv{l}"].rearrange(
                                "(k p) c -> p k c", p=128)
                            wview = wall.ap().rearrange(
                                "p (s k c) -> p (s k) c", s=8, c=128)
                            gd = 0
                            for hp in range(4):
                                gd = b.gload(
                                    wview[:, 8 * hp:8 * (hp + 1), :],
                                    wsrc[:, :, 128 * hp:128 * (hp + 1)])
                                gd = b.gload(
                                    wview[:, 8 * (4 + hp):8 * (5 + hp), :],
                                    wsrc[:, :, 512 + 128 * hp:
                                         512 + 128 * (hp + 1)])
                            gddep = ("gd", gd)
                            b.ckpt("g1aw")
                            npend = [None]
                            pn_dep = [pn_dep]

                            def emit_norm_g(first, last, selA, selB, sq_, tb_):
                                b.wt(nc.tensor, ("a", sq_))
                                if first:
                                    b.wt(nc.tensor, ("a", pn_dep[0]))
                                nc.tensor.matmul(
                                    pN16a[0:16, 0:512], selA, tb_[:],
                                    start=first, stop=last)
                                return b.inc("t", nc.tensor.matmul(
                                    pN16b[0:16, 0:512], selB, tb_[:],
                                    start=first, stop=last))

                            for n in range(NB4):
                                nsl = slice(512 * n, 512 * (n + 1))
                                for tsel in range(2):
                                    dst = (qb, kb)[tsel]
                                    selp = (ct["sel16q"] if tsel == 0
                                            else ct["sel16k"])
                                    woff = 64 * tsel
                                    for hp in range(4):
                                        pP = pPs[pi % 2]
                                        blk = 4 * tsel + hp
                                        b.wt(nc.tensor, gddep,
                                             ("a", evac_dep[pi % 2]))
                                        i = None
                                        for k in range(NCH):
                                            off = 0 if k >= 4 else 1
                                            sl = slice(off + 512 * n,
                                                       off + 512 * (n + 1))
                                            i = nc.tensor.matmul(
                                                pP[:],
                                                wall[:, 1024 * blk + 128 * k:
                                                     1024 * blk
                                                     + 128 * (k + 1)],
                                                xb[k][:, sl], start=(k == 0),
                                                stop=(k == 7))
                                        mmdep = b.inc("t", i)
                                        if npend[0] is not None:
                                            emit_norm_g(*npend[0])
                                            npend[0] = None
                                        b.wt(nc.scalar, ("t", mmdep))
                                        nc.scalar.copy(
                                            out=dst[hp][:, nsl], in_=pP[:])
                                        sq = b.inc("a", nc.scalar.activation(
                                            out=tbg2[pi % 2][:],
                                            in_=dst[hp][:, nsl],
                                            func=AF.Square))
                                        first = (tsel == 0 and hp == 0)
                                        last = (tsel == 1 and hp == 3)
                                        npend[0] = (
                                            first, last,
                                            selp[:, 16 * hp:16 * (hp + 1)],
                                            sel16w[:, woff + 16 * hp:
                                                   woff + 16 * (hp + 1)],
                                            sq, tbg2[pi % 2])
                                        evac_dep[pi % 2] = sq
                                        pi += 1
                                td = emit_norm_g(*npend[0])
                                npend[0] = None
                                b.wt(nc.scalar, ("t", td))
                                nc.scalar.copy(out=n2all[0:16, nsl],
                                               in_=pN16a[0:16, 0:512])
                                pn_dep[0] = b.inc("a", nc.scalar.copy(
                                    out=numall[0:16, nsl],
                                    in_=pN16b[0:16, 0:512]))
                            b.ckpt("g1aq")
                            b.wt(nc.gpsimd, b.now("t"), b.now("a"))
                            gd = 0
                            for hp in range(4):
                                gd = b.gload(
                                    wview[:, 8 * hp:8 * (hp + 1), :],
                                    wsrc[:, :, 1024 + 128 * hp:
                                         1024 + 128 * (hp + 1)])
                            gddep = ("gd", gd)
                            for n in range(NB4):
                                for hp in range(4):
                                    pP = pPs[pi % 2]
                                    b.wt(nc.tensor, gddep,
                                         ("a", evac_dep[pi % 2]))
                                    i = None
                                    for k in range(NCH):
                                        off = 0 if k >= 4 else 1
                                        sl = slice(off + 512 * n,
                                                   off + 512 * (n + 1))
                                        i = nc.tensor.matmul(
                                            pP[:],
                                            wall[:, 1024 * hp + 128 * k:
                                                 1024 * hp + 128 * (k + 1)],
                                            xb[k][:, sl], start=(k == 0),
                                            stop=(k == 7))
                                    b.inc("t", i)
                                    evac_dep[pi % 2] = b.v_block(
                                        pP, n, vbt, vo[2 * hp],
                                        vo[2 * hp + 1], [pT0, pT1], tr_dep)
                                    pi += 1
                            b.barrier()
                    b.ckpt(f"g1a{l}")

                    nc.vector.tensor_scalar_max(out=n2all[:], in0=n2all[:],
                                                scalar1=1e-24)
                    vd = b.inc("v", nc.vector.tensor_scalar_max(
                        out=numall[:], in0=numall[:], scalar1=1e-30))
                    b.wt(nc.scalar, ("v", vd))
                    nc.scalar.activation(out=n2all[:], in_=n2all[:], func=AF.Ln)
                    nc.scalar.activation(out=rq16[:], in_=n2all[:],
                                         func=AF.Exp, scale=-0.5)
                    ad = b.inc("a", nc.scalar.activation(
                        out=numall[:], in_=numall[:], func=AF.Ln))
                    b.wt(nc.vector, ("a", ad))
                    nc.vector.tensor_sub(out=numall[:], in0=numall[:],
                                         in1=n2all[:])
                    vd = b.inc("v", nc.vector.tensor_reduce(
                        out=dmax16[:], in_=numall[0:16, 0:S], axis=AX.X,
                        op=ALU.max))
                    with b.ps("pb1", [128, 8], F32) as pb1:
                        b.wt(nc.tensor, ("v", vd), ("sd", selC_sd))
                        i = None
                        for h in range(8):
                            i = nc.tensor.matmul(
                                pb1[:, h:h + 1],
                                selC_t[:, 128 * h:128 * (h + 1)],
                                dmax16[0:16, 0:1], start=True, stop=True)
                        td = b.inc("t", i)
                        b.wt(nc.vector, ("t", td))
                        vd = b.inc("v", nc.vector.tensor_scalar_add(
                            out=cbias[:], in0=pb1[:], scalar1=LOG8))
                        b.wt(nc.scalar, ("v", vd))
                        ad = b.inc("a", nc.scalar.activation(
                            out=cbias[:], in_=cbias[:], func=AF.Exp))
                        b.wt(nc.vector, ("a", ad))
                        b.inc("v", nc.vector.tensor_scalar_mul(
                            out=cbias[:], in0=cbias[:], scalar1=-1.0))
                    b.barrier()

                if True:
                    with contextlib.ExitStack() as rsc:
                        selg2_t = rsc.enter_context(
                            b.sb("selg2_t", [16, 1024], BF16))
                        b.sload(selg2_t[:, 0:512], b.dr["selgQ"])
                        selg2_sd = b.sload(selg2_t[:, 512:1024],
                                           b.dr["selgK"])
                        tblA = [rsc.enter_context(
                            b.sb(f"tblA{i}", [128, 512], F32))
                            for i in range(2)]
                        tblB = rsc.enter_context(
                            b.sb("tblB", [128, 512], F32))
                        tb512c = rsc.enter_context(
                            b.sb("tb512c", [128, 512], BF16))
                        tf512 = [rsc.enter_context(b.sb(
                            f"tf512{i}", [128, 512], F32)) for i in range(2)]
                        gsums = rsc.enter_context(b.sb("gsums", [8, S], F32))
                        grb_bf = rsc.enter_context(b.sb("grb", [8, S], BF16))
                        srow = rsc.enter_context(b.sb("srow", [1, 1024], F32))
                        pexp = [rsc.enter_context(b.sb(
                            f"pexp{i}", [128, 512], BF16)) for i in range(4)]
                        with (b.ps("prb1", [128, 512], F32) as prb1,
                              b.ps("pR0", [128, 512], F32) as pR0,
                              b.ps("pR1", [128, 512], F32) as pR1,
                              b.ps("pqa0", [128, 512], F32) as pqa0,
                              b.ps("pqa1", [128, 512], F32) as pqa1,
                              b.ps("ppvg", [65, 512], F32) as ppva,
                              b.ps("ppvg2", [65, 512], F32) as ppvb):
                            qsc, ksc = ct[f"qs{l}"], ct[f"ks{l}"]
                            pRs = [pR0, pR1]
                            riL = [0]
                            vdep_r = [0, 0]
                            tbl_free = [0, 0]
                            tbl_sdd = [0, 0]
                            tseq = []
                            for (tA, tB) in (("tcs", "tss"),
                                             ("tcis", "tsis")):
                                for n in range(NB4):
                                    tseq.append(
                                        (tA, tB,
                                         slice(512 * n, 512 * (n + 1))))

                            def load_tbl(j, bb):
                                tA, tB, nsl = tseq[j % 8]
                                b.sload(tblA[bb][:], b.dr[tA][:, nsl],
                                        ("v", tbl_free[bb]))

                            def g1b_gen(hp):
                                for t_i, (dst, scol, selo) in enumerate((
                                        (qb, qsc, 0),
                                        (kb, ksc, 512))):
                                    for n in range(NB4):
                                        nsl = slice(512 * n, 512 * (n + 1))
                                        ri = riL[0]
                                        bb = ri % 2
                                        if ri == 0:
                                            load_tbl(0, 0)
                                        load_tbl(ri + 1, (ri + 1) % 2)
                                        tB = tseq[ri % 8][1]
                                        sdd = b.sload(
                                            tblB[:], b.dr[tB][:, nsl],
                                            ("v", vdep_r[(ri + 1) % 2]))
                                        b.wt(nc.tensor,
                                             ("v", vdep_r[(ri + 1) % 2]),
                                             ("sd", selg2_sd))
                                        td = b.inc("t", nc.tensor.matmul(
                                            prb1[:],
                                            selg2_t[:, selo + 128 * hp:
                                                    selo + 128 * (hp + 1)],
                                            rq16[0:16, nsl], start=True,
                                            stop=True))
                                        b.wt(nc.vector, ("t", td))
                                        vd = b.inc(
                                            "v",
                                            nc.vector.scalar_tensor_tensor(
                                                out=tb512c[:],
                                                in0=dst[hp][:, nsl],
                                                scalar=scol[:], in1=prb1[:],
                                                op0=ALU.mult, op1=ALU.mult))
                                        pR = pRs[ri % 2]
                                        b.wt(nc.tensor, ("v", vd),
                                             ("v", vdep_r[ri % 2]))
                                        td = b.inc("t", nc.tensor.matmul(
                                            pR[:], ct["rmat"][:], tb512c[:],
                                            start=True, stop=True))
                                        b.wt(nc.vector, ("sd", sdd))
                                        nc.vector.tensor_mul(
                                            out=tf512[0][:], in0=tb512c[:],
                                            in1=tblA[bb][:])
                                        b.wt(nc.vector, ("t", td))
                                        nc.vector.tensor_mul(
                                            out=tf512[1][:], in0=pR[:],
                                            in1=tblB[:])
                                        vdep_r[ri % 2] = b.inc(
                                            "v", nc.vector.tensor_add(
                                                out=dst[hp][:, nsl],
                                                in0=tf512[0][:],
                                                in1=tf512[1][:]))
                                        tbl_free[bb] = vdep_r[ri % 2]
                                        riL[0] += 1
                                        yield

                            ast = {"expA": 0, "expB": 0, "evacA": 0,
                                   "evacB": 0, "pv": 0, "pv2": [0, 0],
                                   "ep": 0}
                            att = None
                            for hp in range(4):
                                pg = g1b_gen(hp)
                                for _ in pg:
                                    for _i in range(4):
                                        if att is not None:
                                            try:
                                                next(att)
                                            except StopIteration:
                                                att = None
                                while att is not None:
                                    try:
                                        next(att)
                                    except StopIteration:
                                        att = None
                                rdy = ("v", vdep_r[(riL[0] + 1) % 2])
                                rdy2 = ("v", vdep_r[riL[0] % 2])
                                att = b.att_pair_gen(
                                    "g", qb[hp], kb[hp], vo[2 * hp],
                                    vo[2 * hp + 1],
                                    cbias[:, 2 * hp:2 * hp + 1],
                                    cbias[:, 2 * hp + 1:2 * hp + 2],
                                    8.0, oun_g[hp], gsums, 2 * hp, srow,
                                    pqa0, pqa1, ppva, ppvb,
                                    [pexp[0], pexp[1]], [pexp[2], pexp[3]],
                                    ast, entry_deps=(rdy, rdy2))
                            while att is not None:
                                try:
                                    next(att)
                                except StopIteration:
                                    att = None
                            b.barrier()
                            b.ckpt(f"g2{l}")

                        if l == 0 and "att0" in b.dumps:
                            dmp = b.dumps["att0"]
                            for hp in range(4):
                                b.gload(dmp[128 * hp:128 * (hp + 1), :],
                                        oun_g[hp][:])
                            b.sload(dmp[512:520, :], gsums[0:8, :])
                            b.barrier()

                        nc.scalar.activation(out=gsums[:], in_=gsums[:],
                                             func=AF.Ln)
                        ad = b.inc("a", nc.scalar.activation(
                            out=grb_bf[:], in_=gsums[:], func=AF.Exp,
                            scale=-1.0))
                        with (b.ps("prng0", [128, 512], F32) as prn0,
                              b.ps("prng1", [128, 512], F32) as prn1):
                            prns = [prn0, prn1]
                            od = [0, 0]
                            oi = 0
                            for hp in range(4):
                                for n in range(NB4):
                                    nsl = slice(512 * n, 512 * (n + 1))
                                    prn = prns[oi % 2]
                                    b.wt(nc.tensor, ("a", ad),
                                         ("v", od[oi % 2]))
                                    td = b.inc("t", nc.tensor.matmul(
                                        prn[:],
                                        ct["selg_bf"][:, 128 * hp:
                                                      128 * (hp + 1)],
                                        grb_bf[0:8, nsl], start=True,
                                        stop=True))
                                    b.wt(nc.vector, ("t", td))
                                    od[oi % 2] = b.inc(
                                        "v", nc.vector.tensor_mul(
                                            out=oun_g[hp][:, nsl],
                                            in0=oun_g[hp][:, nsl],
                                            in1=prn[:]))
                                    oi += 1
                        b.barrier()

            with contextlib.ExitStack() as wsc:
                wo4 = wsc.enter_context(b.sb("wo4", [128, 4096], BF16))
                gw = b.dr[f"gwo{l}"]
                gdw = 0
                for k in range(4):
                    gdw = b.gload(wo4[:, 1024 * k:1024 * (k + 1)],
                                  gw[128 * k:128 * (k + 1), :])
                with (b.ps("pW0", [128, 512], F32) as pW0,
                      b.ps("pW1", [128, 512], F32) as pW1):
                    pWs = [pW0, pW1]
                    vdep_w = [0, 0]
                    wi = 0
                    for m in range(NCH):
                        for n in range(NB4):
                            pW = pWs[wi % 2]
                            b.wt(nc.tensor, ("gd", gdw), ("v", vdep_w[wi % 2]))
                            i = None
                            for k in range(4):
                                i = nc.tensor.matmul(
                                    pW[:], wo4[:, 1024 * k + 128 * m:
                                                1024 * k + 128 * (m + 1)],
                                    oun_g[k][:, 512 * n:512 * (n + 1)],
                                    start=(k == 0), stop=(k == 3))
                            td = b.inc("t", i)
                            b.wt(nc.vector, ("t", td))
                            xsl = slice(1 + 512 * n, 1 + 512 * (n + 1))
                            nc.vector.tensor_add(out=xg[m][:, xsl],
                                                 in0=xg[m][:, xsl], in1=pW[:])
                            vdep_w[wi % 2] = b.inc(
                                "v", nc.vector.tensor_copy(
                                    out=xb[m][:, xsl], in_=xg[m][:, xsl]))
                            wi += 1
                    b.barrier()
            b.ckpt(f"g3{l}")

            ln1 = ct["lnones"]
            with contextlib.ExitStack() as lsc:
                sqb = lsc.enter_context(b.sb("sqb2", [128, SQ], BF16))
                mrow = lsc.enter_context(b.sb("mrow2", [1, S], F32))
                rrow = lsc.enter_context(b.sb("rrow2", [1, S], F32))
                with (b.ps("pm2", [1, S], F32) as pm,
                      b.ps("pq2", [1, S], F32) as pq):
                    dep_a = 0
                    for c in range(NCH):
                        b.wt(nc.scalar, b.now("t"), ("a", dep_a))
                        dep_a = b.inc("a", nc.scalar.activation(
                            out=sqb[:, 0:SQ], in_=xg[c][:, 0:SQ],
                            func=AF.Square))
                        off = 0 if c >= 4 else 1
                        b.wt(nc.tensor, ("a", dep_a))
                        i = None
                        for n in range(NB4):
                            sl = slice(off + 512 * n, off + 512 * (n + 1))
                            nc.tensor.matmul(pm[0:1, 512 * n:512 * (n + 1)],
                                             ln1[:], xb[c][:, sl],
                                             start=(c == 0), stop=(c == 7))
                        for n in range(NB4):
                            sl = slice(off + 512 * n, off + 512 * (n + 1))
                            i = nc.tensor.matmul(
                                pq[0:1, 512 * n:512 * (n + 1)], ln1[:],
                                sqb[:, sl], start=(c == 0), stop=(c == 7))
                        b.inc("t", i)
                    scrB = b.ln_finalize(pm, pq, mrow, rrow)
                b.barrier()

        with contextlib.ExitStack() as sc:
            h2n = [sc.enter_context(b.sb(f"h2n{c}", [128, S], BF16))
                   for c in range(NCH)]
            lngc, lnbc = ct[f"lng{l}"], ct[f"lnb{l}"]

            with contextlib.ExitStack() as fsc:
                w1t = [fsc.enter_context(
                    b.sb(f"w1t{i}", [128, 8 * 2048], BF16))
                    for i in range(2)]
                w2c = [fsc.enter_context(
                    b.sb(f"w2c{i}", [128, HIDP], BF16)) for i in range(2)]
                h4all = fsc.enter_context(
                    b.sb("h4all", [128, NJ * 512], BF16))
                gel = [fsc.enter_context(
                    b.sb(f"gel{i}", [128, 512], BF16))
                    for i in range(2)]
                rowsB = fsc.enter_context(b.sb("rowsB", [1, 512], F32))
                tfaB = fsc.enter_context(b.sb("tfaB", [128, 512], F32))
                xcol = fsc.enter_context(b.sb("xcolB", [128, 4], F32))
                w1d = b.dr[f"w1pp{l}"]
                w2d = b.dr[f"w2pt{l}"]
                b1c = ct[f"b1pp{l}"]
                b2c = ct[f"b2{l}"]
                thirds = [(0, 8), (8, 16), (16, 22)]

                def load_third(tix, buf, dep):
                    j0, j1 = thirds[tix]
                    w = 256 * (j1 - j0)
                    gd = 0
                    for k in range(NCH):
                        gd = b.gload(
                            w1t[buf][:, 2048 * k:2048 * k + w],
                            w1d[128 * k:128 * (k + 1), 256 * j0:256 * j1],
                            ("t", dep) if k == 0 else ("t", 0))
                    return gd

                with (b.ps("pag0", [128, 1024], F32) as pag0,
                      b.ps("pag1", [128, 1024], F32) as pag1,
                      b.ps("pW20", [128, 512], F32) as pW20,
                      b.ps("pW21", [128, 512], F32) as pW21,
                      b.ps("pmbB", [128, 512], F32) as pmbB,
                      b.ps("prbB", [128, 512], F32) as prbB):
                    pags = [pag0, pag1]
                    pW2s = [pW20, pW21]
                    gel_dep = [0, 0]
                    h4_dep = [0, 0]
                    w1gd = [0, 0]
                    w1_used = [0, 0]
                    gthird = 0
                    w1gd[0] = load_third(0, 0, 0)
                    gd_w2 = [0, 0]
                    w2_used = [0, 0]
                    gd_w2[0] = b.gload(w2c[0][:], w2d[0:128, :])
                    res_dep = 0
                    w2_done = 0

                    onesf = ct["onesf"]
                    ap_done = [0]
                    bc_dep = [0]
                    h2n_rdy = [0, 0, 0, 0]

                    def apply_begin(n):
                        nsl = slice(512 * n, 512 * (n + 1))
                        sd = b.sload(rowsB[0:1, 0:512], b.lnmr[0:1, nsl],
                                     ("sd", scrB), ("t", bc_dep[0]))
                        b.wt(nc.tensor, ("sd", sd), ("v", ap_done[0]))
                        t1 = b.inc("t", nc.tensor.matmul(
                            pmbB[:, 0:512], onesf[:], rowsB[0:1, 0:512],
                            start=True, stop=True))
                        sd = b.sload(rowsB[0:1, 0:512], b.lnrs[0:1, nsl],
                                     ("t", t1))
                        b.wt(nc.tensor, ("sd", sd))
                        bc_dep[0] = b.inc("t", nc.tensor.matmul(
                            prbB[:, 0:512], onesf[:], rowsB[0:1, 0:512],
                            start=True, stop=True))

                    def apply_c(n, c, xc=None):
                        tfa = tfaB[:]
                        nsl = slice(512 * n, 512 * (n + 1))
                        off = 0 if c >= 4 else 1
                        hsl = slice(off + 512 * n, off + 512 * (n + 1))
                        b.wt(nc.vector, ("t", bc_dep[0]))
                        if xc is None:
                            nc.vector.tensor_mul(out=tfa, in0=xg[c][:, hsl],
                                                 in1=prbB[:, 0:512])
                        else:
                            nc.vector.tensor_mul(out=tfa[:, 0:1], in0=xc,
                                                 in1=prbB[:, 0:1])
                            nc.vector.tensor_mul(
                                out=tfa[:, 1:512],
                                in0=xg[c][:, hsl.start + 1:hsl.stop],
                                in1=prbB[:, 1:512])
                        nc.vector.tensor_sub(out=tfa, in0=tfa,
                                             in1=pmbB[:, 0:512])
                        tick = b.inc("v", nc.vector.tensor_scalar(
                            out=h2n[c][:, nsl], in0=tfa,
                            scalar1=lngc[:, c:c + 1],
                            scalar2=lnbc[:, c:c + 1],
                            op0=ALU.mult, op1=ALU.add))
                        ap_done[0] = tick
                        h2n_rdy[n] = tick

                    apply_begin(0)
                    for c in range(NCH):
                        apply_c(0, c)

                    for nb in range(4):
                        nsl = slice(512 * nb, 512 * (nb + 1))
                        for j in range(NJ):
                            jb = j % 2
                            tix = 0 if j < 8 else (1 if j < 16 else 2)
                            tbuf = gthird % 2
                            j0 = thirds[tix][0]
                            if j in (0, 8, 16):
                                if not (nb == 3 and j == 16):
                                    nbuf = (gthird + 1) % 2
                                    w1gd[nbuf] = load_third(
                                        (tix + 1) % 3, nbuf, w1_used[nbuf])
                            pa = pags[jb][:, 0:512]
                            pg = pags[jb][:, 512:1024]
                            b.wt(nc.tensor, ("gd", w1gd[tbuf]),
                                 ("a", gel_dep[jb]), ("v", h4_dep[jb]),
                                 ("v", h2n_rdy[nb] if j == 0 else 0))
                            jcol = 256 * (j - j0)
                            i = None
                            for k in range(NCH):
                                base = 2048 * k + jcol
                                nc.tensor.matmul(
                                    pa, w1t[tbuf][:, base:base + 128],
                                    h2n[k][:, nsl], start=(k == 0),
                                    stop=(k == 7))
                            for k in range(NCH):
                                base = 2048 * k + jcol + 128
                                i = nc.tensor.matmul(
                                    pg, w1t[tbuf][:, base:base + 128],
                                    h2n[k][:, nsl], start=(k == 0),
                                    stop=(k == 7))
                            td = b.inc("t", i)
                            w1_used[tbuf] = td
                            b.wt(nc.scalar, ("t", td), ("v", h4_dep[jb]))
                            gel_dep[jb] = b.inc("a", nc.scalar.activation(
                                out=gel[jb][:], in_=pg, func=AF.Gelu,
                                bias=b1c[:, 2 * j + 1:2 * j + 2]))
                            b.wt(nc.vector, ("a", gel_dep[jb]),
                                 ("t", w2_done))
                            h4_dep[jb] = b.inc(
                                "v", nc.vector.scalar_tensor_tensor(
                                    out=h4all[:, 512 * j:512 * (j + 1)],
                                    in0=pa, scalar=b1c[:, 2 * j:2 * j + 1],
                                    in1=gel[jb][:], op0=ALU.add, op1=ALU.mult))
                            if j in (7, 15, 21):
                                gthird += 1
                        if nb + 1 < 4:
                            for c in range(4, NCH):
                                nc.vector.tensor_copy(
                                    out=xcol[:, c - 4:c - 3],
                                    in_=xg[c][:, 512 * (nb + 1):
                                              512 * (nb + 1) + 1])
                            apply_begin(nb + 1)
                            for c in range(NCH):
                                apply_c(nb + 1, c,
                                        xcol[:, c - 4:c - 3] if c >= 4
                                        else None)
                        last_w2 = 0
                        for m in range(NCH):
                            mb = m % 2
                            m_, nb_ = (m + 1, nb) if m + 1 < NCH else (0, nb + 1)
                            if nb_ < 4:
                                gd_w2[(m + 1) % 2] = b.gload(
                                    w2c[(m + 1) % 2][:],
                                    w2d[128 * m_:128 * (m_ + 1), :],
                                    ("t", w2_used[(m + 1) % 2]))
                            b.wt(nc.tensor, ("gd", gd_w2[mb]),
                                 ("v", h4_dep[0]), ("v", h4_dep[1]),
                                 ("v", res_dep))
                            i = None
                            for j in range(NJ):
                                i = nc.tensor.matmul(
                                    pW2s[mb][:],
                                    w2c[mb][:, 128 * j:128 * (j + 1)],
                                    h4all[:, 512 * j:512 * (j + 1)],
                                    start=(j == 0), stop=(j == NJ - 1))
                            td = b.inc("t", i)
                            w2_used[mb] = td
                            last_w2 = td
                            xsl = slice(1 + 512 * nb, 1 + 512 * (nb + 1))
                            b.wt(nc.vector, ("t", td))
                            res_dep = b.inc(
                                "v", nc.vector.scalar_tensor_tensor(
                                    out=xg[m][:, xsl], in0=pW2s[mb][:],
                                    scalar=b2c[:, m:m + 1], in1=xg[m][:, xsl],
                                    op0=ALU.add, op1=ALU.add))
                        w2_done = last_w2
                    b.barrier()
                b.ckpt(f"ffn{l}")
        if l == 0 and "g0" in b.dumps:
            for c in range(NCH):
                b.sload(b.dumps["g0"][128 * c:128 * (c + 1), :], xg[c][:, 1:SQ])
            b.barrier()
    if "gpre" in b.dumps:
        for c in range(NCH):
            b.sload(b.dumps["gpre"][128 * c:128 * (c + 1), :], xg[c][:, 1:SQ])
        b.barrier()

    with contextlib.ExitStack() as sc:
        xbs = [sc.enter_context(b.sb(f"xbs{c}", [128, S], BF16))
               for c in range(NCH)]
        oun_s = [sc.enter_context(b.sb(f"osn{i}", [128, S], BF16))
                 for i in range(NCH)]
        ssums = sc.enter_context(b.sb("ssums", [16, S], F32))
        srow_s = sc.enter_context(b.sb("srow_s", [1, 1024], F32))
        srb_bf = sc.enter_context(b.sb("srb", [16, S], BF16))
        selg16_t = sc.enter_context(b.sb("selg16_t", [16, 1024], BF16))
        selg16_sd = b.sload(selg16_t[:], b.dr["selg16_bf"])
        cbias_s = sc.enter_context(b.sb("cbias_s", [128, 16], F32))

        for c in range(NCH):
            b.wt(nc.scalar, b.now("t"), b.now("v"))
            b.inc("a", nc.scalar.activation(out=xg[c][:, 1:SQ],
                                            in_=xg[c][:, 1:SQ], func=AF.Silu))

        with contextlib.ExitStack() as lsc:
            tmpf = [lsc.enter_context(b.sb(f"tmpf{i}", [128, S], F32))
                    for i in range(3)]
            mrow = lsc.enter_context(b.sb("mrowS", [1, S], F32))
            rrow = lsc.enter_context(b.sb("rrowS", [1, S], F32))
            with (b.ps("pmbs", [128, S], F32) as pmb,
                  b.ps("prbs", [128, S], F32) as prb):
                tdep = b.ln_bcast(pmb, prb, mrow, rrow, b.scr0_sd,
                                  src=(b.lnmr0, b.lnrs0))
                dep_v = [0, 0, 0]
                for c in range(NCH):
                    buf = tmpf[c % 3]
                    if c % 3 != 1:
                        dd = ("sd", b.sload(buf[:], xd[128 * c:128 * (c + 1), :],
                                            ("v", dep_v[c % 3])))
                    else:
                        dd = ("gd", b.gload(buf[:], xd[128 * c:128 * (c + 1), :],
                                            ("v", dep_v[c % 3])))
                    b.wt(nc.vector, dd, ("t", tdep))
                    nc.vector.tensor_mul(out=buf[:], in0=buf[:], in1=prb[:])
                    dep_v[c % 3] = b.inc("v", nc.vector.tensor_sub(
                        out=xbs[c][:], in0=buf[:], in1=pmb[:]))
            b.barrier()
        b.ckpt("sln")

        with contextlib.ExitStack() as asc:
            qh2 = [asc.enter_context(b.sb(f"qh{i}", [128, S], BF16))
                   for i in range(2)]
            kh2 = [asc.enter_context(b.sb(f"kh{i}", [128, S], BF16))
                   for i in range(2)]
            vo4 = [asc.enter_context(b.sb(f"vos{i}", [128, 16 * 65], BF16))
                   for i in range(4)]
            wqs = [asc.enter_context(b.sb(f"wqss{i}", [128, D], BF16))
                   for i in range(2)]
            wks = [asc.enter_context(b.sb(f"wkss{i}", [128, D], BF16))
                   for i in range(2)]
            wvs = [asc.enter_context(b.sb(f"wvss{i}", [128, D], BF16))
                   for i in range(2)]
            nmax = asc.enter_context(b.sb("nmax", [2, 8], F32))
            t21 = asc.enter_context(b.sb("t21", [2, 8], F32))
            vbts = asc.enter_context(b.sb("vbts", [128, 512], BF16))
            tbs2 = [asc.enter_context(b.sb(f"tbs2{i}", [128, 512], BF16))
                    for i in range(2)]
            pexp = [asc.enter_context(b.sb(
                f"pexps{i}", [128, 512], BF16)) for i in range(4)]

            for vv in vo4:
                oc = vv.ap().rearrange("p (u c) -> p u c", c=65)[:, :, 64:65]
                b.inc("gc", nc.gpsimd.memset(oc, 1.0))
            b.barrier()

            with (b.ps("pPs0", [128, 512], F32) as pP0,
                  b.ps("pPs1", [128, 512], F32) as pP1,
                  b.ps("pqka", [128, 512], F32) as pqka,
                  b.ps("pqkb", [128, 512], F32) as pqkb,
                  b.ps("pTs0", [128, 128], BF16) as pT0,
                  b.ps("pTs1", [128, 128], BF16) as pT1,
                  b.ps("spva", [65, 512], F32) as ppva,
                  b.ps("spvb", [65, 512], F32) as ppvb):
                pPs = [pP0, pP1]
                ast = {"expA": 0, "expB": 0, "evacA": 0, "evacB": 0,
                       "pv": 0, "pv2": [0, 0], "ep": 0}
                evac_dep = [0, 0]
                tr_dep = [0, 0]
                nrm_dep2 = [0, 0]
                vo_guard = [0, 0]
                nmax_rd = [0, 0]
                piL = [0]
                npend = [None]
                entry = {}

                def emit_norm(gcd_):
                    (pPb, pib, col, sq_) = npend[0]
                    npend[0] = None
                    b.wt(nc.tensor, ("a", sq_))
                    td = b.inc("t", nc.tensor.matmul(
                        pPb[0:2, 0:512], ct["sel2"][:], tbs2[pib][:],
                        start=True, stop=True))
                    b.wt(nc.vector, ("t", td), ("gc", gcd_))
                    nc.vector.tensor_reduce(
                        out=t21[0:2, col:col + 1], in_=pPb[0:2, 0:512],
                        axis=AX.X, op=ALU.max)
                    return b.inc("v", nc.vector.tensor_max(
                        out=nmax[0:2, col:col + 1],
                        in0=nmax[0:2, col:col + 1],
                        in1=t21[0:2, col:col + 1]))

                wall_free = [0, 0]
                swgd = [0, 0]

                def load_sw(hp):
                    wb = hp % 2
                    gd = 0
                    for wtile, wname in ((wqs[wb], "wq_t"), (wks[wb], "wk_t"),
                                         (wvs[wb], "wv_t")):
                        wsrc = b.dr[wname].rearrange("(k p) c -> p k c", p=128)
                        gd = b.gload(
                            wtile.ap().rearrange("p (k c) -> p k c", c=128),
                            wsrc[:, :, 128 * hp:128 * (hp + 1)],
                            ("t", wall_free[wb]) if wtile is wqs[wb]
                            else ("t", 0))
                    return gd

                def s_proj_gen(hp):
                    par = hp % 2
                    qh = qh2[par]
                    kh = kh2[par]
                    vo_a, vo_b = vo4[2 * par], vo4[2 * par + 1]
                    if hp + 1 < 8:
                        swgd[(hp + 1) % 2] = load_sw(hp + 1)
                    gddep = ("gd", swgd[hp % 2])
                    b.wt(nc.gpsimd, ("a", nmax_rd[par]), b.now("v"))
                    gcd = b.inc("gc", nc.gpsimd.memset(
                        nmax[:, 4 * par:4 * par + 2], 0.0))
                    proj_evac = 0
                    for tsel in range(3):
                        wtile = (wqs[par], wks[par], wvs[par])[tsel]
                        dsts = (qh, kh, None)[tsel]
                        bias_col = (ct["bq"], ct["bk"],
                                    ct["bv"])[tsel][:, hp:hp + 1]
                        for n in range(NB4):
                            pi = piL[0]
                            pP = pPs[pi % 2]
                            b.wt(nc.tensor, gddep, ("a", evac_dep[pi % 2]),
                                 ("v", nrm_dep2[pi % 2]))
                            i = None
                            for k in range(NCH):
                                sl = slice(512 * n, 512 * (n + 1))
                                i = nc.tensor.matmul(
                                    pP[:], wtile[:, 128 * k:128 * (k + 1)],
                                    xbs[k][:, sl], start=(k == 0),
                                    stop=(k == 7))
                                if k == 3:
                                    yield
                            mmdep = b.inc("t", i)
                            if npend[0] is not None:
                                nrm_dep2[(pi + 1) % 2] = emit_norm(gcd)
                            if tsel < 2:
                                nsl = slice(512 * n, 512 * (n + 1))
                                b.wt(nc.scalar, ("t", mmdep))
                                nc.scalar.activation(out=dsts[:, nsl],
                                                     in_=pP[:],
                                                     func=AF.Identity,
                                                     bias=bias_col)
                                evac_dep[pi % 2] = b.inc(
                                    "a", nc.scalar.activation(
                                        out=tbs2[pi % 2][:],
                                        in_=dsts[:, nsl], func=AF.Square))
                                proj_evac = evac_dep[pi % 2]
                                npend[0] = (pP, pi % 2, 4 * par + tsel,
                                            evac_dep[pi % 2])
                            else:
                                evac_dep[pi % 2] = b.v_block(
                                    pP, n, vbts, vo_a, vo_b, [pT0, pT1],
                                    tr_dep, bias=bias_col,
                                    vo_guard=vo_guard[par])
                            piL[0] += 1
                            yield
                    wall_free[par] = b.cnt["t"]
                    if npend[0] is not None:
                        nrm_dep2[(piL[0] + 1) % 2] = emit_norm(gcd)
                    c0 = 4 * par
                    b.wt(nc.scalar, ("v", nrm_dep2[0]), ("v", nrm_dep2[1]))
                    nc.scalar.activation(out=t21[0:2, c0:c0 + 1],
                                         in_=nmax[0:2, c0:c0 + 1],
                                         func=AF.Sqrt)
                    ad = b.inc("a", nc.scalar.activation(
                        out=t21[0:2, c0 + 1:c0 + 2],
                        in_=nmax[0:2, c0 + 1:c0 + 2], func=AF.Sqrt))
                    nmax_rd[par] = ad
                    b.wt(nc.vector, ("a", ad))
                    nc.vector.tensor_mul(out=t21[0:2, c0 + 2:c0 + 3],
                                         in0=t21[0:2, c0:c0 + 1],
                                         in1=t21[0:2, c0 + 1:c0 + 2])
                    vd = b.inc("v", nc.vector.tensor_scalar_mul(
                        out=t21[0:2, c0 + 2:c0 + 3],
                        in0=t21[0:2, c0 + 2:c0 + 3], scalar1=-0.125))
                    b.wt(nc.tensor, ("v", vd))
                    nc.tensor.matmul(pP0[:, 0:1], ct["sel2f"][:, 0:128],
                                     t21[0:2, c0 + 2:c0 + 3],
                                     start=True, stop=True)
                    td = b.inc("t", nc.tensor.matmul(
                        pP0[:, 1:2], ct["sel2f"][:, 128:256],
                        t21[0:2, c0 + 2:c0 + 3], start=True, stop=True))
                    b.wt(nc.vector, ("t", td))
                    cb_tick = b.inc("v", nc.vector.tensor_copy(
                        out=cbias_s[:, 2 * hp:2 * hp + 2], in_=pP0[:, 0:2]))
                    nrm_dep2[0] = max(nrm_dep2[0], cb_tick)
                    entry[hp] = (("a", proj_evac), ("v", cb_tick))

                att = None
                swgd[0] = load_sw(0)
                for hp in range(8):
                    pg = s_proj_gen(hp)
                    for _ in pg:
                        for _i in range(2):
                            if att is not None:
                                try:
                                    next(att)
                                except StopIteration:
                                    att = None
                    while att is not None:
                        try:
                            next(att)
                        except StopIteration:
                            att = None
                    par = hp % 2
                    att = b.att_pair_gen(
                        "s", qh2[par], kh2[par], vo4[2 * par], vo4[2 * par + 1],
                        cbias_s[:, 2 * hp:2 * hp + 1],
                        cbias_s[:, 2 * hp + 1:2 * hp + 2],
                        0.125, oun_s[hp], ssums, 2 * hp, srow_s,
                        pqka, pqkb, ppva, ppvb,
                        [pexp[0], pexp[1]], [pexp[2], pexp[3]], ast,
                        entry_deps=entry[hp])
                    vo_guard[par] = 0
                while att is not None:
                    try:
                        next(att)
                    except StopIteration:
                        att = None
                b.barrier()

        with contextlib.ExitStack() as fsc:
            wos = fsc.enter_context(b.sb("wos", [128, 8192], BF16))
            ostg = [fsc.enter_context(b.sb(
                f"ostg{i}", [128, 512], F32)) for i in range(2)]
            tmpf2 = [fsc.enter_context(b.sb(
                f"tmf2{i}", [128, S], F32)) for i in range(2)]
            gdw = 0
            for k in range(NCH):
                gdw = b.gload(wos[:, 1024 * k:1024 * (k + 1)],
                              b.dr["wo_t"][128 * k:128 * (k + 1), :])
            sdm = [0, 0]
            sdm[0] = b.sload(tmpf2[0][:], xd[0:128, :])
            sdm[1] = b.sload(tmpf2[1][:], xd[128:256, :])

            nc.scalar.activation(out=ssums[:], in_=ssums[:], func=AF.Ln)
            ad = b.inc("a", nc.scalar.activation(out=srb_bf[:], in_=ssums[:],
                                                 func=AF.Exp, scale=-1.0))
            with (b.ps("prns0", [128, 512], F32) as prn0,
                  b.ps("prns1", [128, 512], F32) as prn1):
                prns = [prn0, prn1]
                od = [0, 0]
                oi = 0
                for hp in range(8):
                    for n in range(NB4):
                        nsl = slice(512 * n, 512 * (n + 1))
                        prn = prns[oi % 2]
                        b.wt(nc.tensor, ("a", ad), ("v", od[oi % 2]),
                             ("sd", selg16_sd))
                        td = b.inc("t", nc.tensor.matmul(
                            prn[:], selg16_t[:, 128 * hp:128 * (hp + 1)],
                            srb_bf[0:16, nsl], start=True, stop=True))
                        b.wt(nc.vector, ("t", td))
                        od[oi % 2] = b.inc("v", nc.vector.tensor_mul(
                            out=oun_s[hp][:, nsl], in0=oun_s[hp][:, nsl],
                            in1=prn[:]))
                        oi += 1
                b.barrier()

            boc = ct["bo"]
            with (b.ps("pS0f", [128, 512], F32) as pS0,
                  b.ps("pS1f", [128, 512], F32) as pS1):
                pSs = [pS0, pS1]
                si = 0
                vdep_s = [0, 0]
                st_dep = [0, 0]
                for m in range(NCH):
                    sd = sdm[m % 2]
                    for n in range(NB4):
                        pS = pSs[si % 2]
                        b.wt(nc.tensor, ("gd", gdw), ("v", vdep_s[si % 2]))
                        i = None
                        for k in range(NCH):
                            i = nc.tensor.matmul(
                                pS[:], wos[:, 1024 * k + 128 * m:
                                            1024 * k + 128 * (m + 1)],
                                oun_s[k][:, 512 * n:512 * (n + 1)],
                                start=(k == 0), stop=(k == 7))
                        td = b.inc("t", i)
                        nsl = slice(512 * n, 512 * (n + 1))
                        xsl = slice(1 + 512 * n, 1 + 512 * (n + 1))
                        ob = ostg[si % 2]
                        b.wt(nc.vector, ("t", td), ("sd", sd),
                             ("sd", st_dep[si % 2]))
                        nc.vector.scalar_tensor_tensor(
                            out=ob[:], in0=pS[:], scalar=boc[:, m:m + 1],
                            in1=xg[m][:, xsl], op0=ALU.add, op1=ALU.mult)
                        vdep_s[si % 2] = b.inc("v", nc.vector.tensor_add(
                            out=ob[:], in0=ob[:], in1=tmpf2[m % 2][:, nsl]))
                        st_dep[si % 2] = b.sload(
                            b.out_d[128 * m:128 * (m + 1), nsl], ob[:],
                            ("v", vdep_s[si % 2]))
                        si += 1
                    if m + 2 < NCH:
                        sdm[m % 2] = b.sload(
                            tmpf2[m % 2][:],
                            xd[128 * (m + 2):128 * (m + 3), :],
                            ("v", vdep_s[0]), ("v", vdep_s[1]))
                b.barrier()


def kernel(**inputs):
    x = np.asarray(inputs["L0_attn_x"], np.float32)
    shared = _prep(inputs)
    nc = _build()
    in_maps = []
    for c in range(8):
        m = dict(shared)
        m["x"] = np.ascontiguousarray(x[c].T)
        in_maps.append(m)
    res = run_bass_kernel_spmd(nc, in_maps, core_ids=list(range(8)))
    out = np.stack([np.asarray(r["out"], np.float32).T for r in res.results], 0)
    return np.ascontiguousarray(out)


# version 17
# speedup vs baseline: 1.1167x; 1.0222x over previous
import contextlib
import math

import numpy as np
import ml_dtypes

import concourse.bass as bass
import concourse.mybir as mybir
from concourse.bass_utils import run_bass_kernel_spmd

F32 = mybir.dt.float32
BF16 = mybir.dt.bfloat16
AF = mybir.ActivationFunctionType
ALU = mybir.AluOpType
AX = mybir.AxisListType

D = 1024
S = 2048
SQ = S + 1
NCH = 8
NB4 = 4
HID = 2730
HIDP = 2816
NJ = HIDP // 128
LN_EPS = 1e-5
LOG8 = math.log(8.0)

_DUMPS = ()
_STOP_AFTER = None
_SPLIT_EXP = False


class _StopEmission(Exception):
    pass


def _bcol(v):
    return np.ascontiguousarray(v.reshape(8, 128).T)


def _consts():
    c = {}
    bf = ml_dtypes.bfloat16
    c["lnones"] = np.full((128, 1), 1.0 / 1024.0, dtype=bf)
    sel2 = np.zeros((128, 2), np.float32)
    sel2[0:64, 0] = 1.0
    sel2[64:128, 1] = 1.0
    c["sel2"] = sel2.astype(bf)
    R = np.zeros((128, 128), np.float32)
    for h in range(2):
        for m in range(32):
            R[h * 64 + m + 32, h * 64 + m] = -1.0
            R[h * 64 + m, h * 64 + m + 32] = 1.0
    c["rmat"] = R.astype(bf)
    c["ident"] = np.eye(128, dtype=np.float32).astype(bf)
    u = np.arange(128)[:, None]
    q = np.arange(128)[None, :]
    c["mtriu"] = (q >= u).astype(np.float32).astype(bf)
    c["mtril"] = (u >= q).astype(np.float32).astype(bf)
    c["onesf"] = np.ones((1, 128), np.float32)
    selg = np.zeros((8, 512), np.float32)
    for hp in range(4):
        selg[2 * hp, 128 * hp:128 * hp + 64] = 1.0
        selg[2 * hp + 1, 128 * hp + 64:128 * hp + 128] = 1.0
    c["selg_bf"] = selg.astype(bf)
    selg16 = np.zeros((16, 1024), np.float32)
    for hp in range(8):
        selg16[2 * hp, 128 * hp:128 * hp + 64] = 1.0
        selg16[2 * hp + 1, 128 * hp + 64:128 * hp + 128] = 1.0
    c["selg16_bf"] = selg16.astype(bf)
    sq_ = np.zeros((16, 512), np.float32)
    sk_ = np.zeros((16, 512), np.float32)
    for hp in range(4):
        sq_[2 * hp, 128 * hp:128 * hp + 64] = 1.0
        sq_[2 * hp + 1, 128 * hp + 64:128 * hp + 128] = 1.0
        sk_[8 + 2 * hp, 128 * hp:128 * hp + 64] = 1.0
        sk_[8 + 2 * hp + 1, 128 * hp + 64:128 * hp + 128] = 1.0
    c["selgQ"] = sq_.astype(bf)
    c["selgK"] = sk_.astype(bf)
    s16q = np.zeros((128, 64), np.float32)
    s16k = np.zeros((128, 64), np.float32)
    for hp in range(4):
        s16q[0:64, 16 * hp + 2 * hp] = 1.0
        s16q[64:128, 16 * hp + 2 * hp + 1] = 1.0
        s16k[0:64, 16 * hp + 8 + 2 * hp] = 1.0
        s16k[64:128, 16 * hp + 8 + 2 * hp + 1] = 1.0
    c["sel16q"] = s16q.astype(bf)
    c["sel16k"] = s16k.astype(bf)
    sC = np.zeros((16, 1024), np.float32)
    for h in range(8):
        sC[h, 128 * h:128 * (h + 1)] = 0.5
        sC[h + 8, 128 * h:128 * (h + 1)] = 0.5
    c["selC"] = sC.astype(bf)
    s2f = np.zeros((2, 256), np.float32)
    s2f[0, 0:128] = 1.0
    s2f[1, 128:256] = 1.0
    c["sel2f"] = s2f
    dh = 64
    half = np.arange(0, dh, 2, dtype=np.float64)
    inv_freq = 1.0 / (10000.0 ** (half / dh))
    t = np.arange(S, dtype=np.float64)
    freqs = np.concatenate([t[:, None] * inv_freq[None, :]] * 2, axis=1)
    base = (half + 0.4 * dh) / (1.4 * dh)
    power = (t - S // 2) / (512 // 2)
    scale = np.concatenate([base[None, :] ** power[:, None]] * 2, axis=1)
    cos, sin = np.cos(freqs), np.sin(freqs)
    dup = lambda a: np.ascontiguousarray(
        np.concatenate([a.T, a.T], axis=0).astype(np.float32))
    c["tcs"] = dup(cos * scale)
    c["tss"] = dup(sin * scale)
    c["tcis"] = dup(cos / scale)
    c["tsis"] = dup(sin / scale)
    return c


def _prep(inp):
    d = {}
    f32 = np.float32
    T = lambda a: np.ascontiguousarray(np.asarray(a, f32).T)
    A = lambda a: np.ascontiguousarray(np.asarray(a, f32))
    d["wq_t"] = T(inp["L0_attn_Wq"])
    d["wk_t"] = T(inp["L0_attn_Wk"])
    d["wv_t"] = T(inp["L0_attn_Wv"])
    d["wo_t"] = T(inp["L0_attn_Wo"])
    d["bq"] = _bcol(A(inp["L0_attn_bq"]))
    d["bk"] = _bcol(A(inp["L0_attn_bk"]))
    d["bv"] = _bcol(A(inp["L0_attn_bv"]))
    d["bo"] = _bcol(A(inp["L0_attn_bo"]))
    for l in range(2):
        d[f"wqkv{l}"] = A(inp["L0_ffn_wqkv"][l])
        d[f"qs{l}"] = np.ascontiguousarray(
            np.tile(A(inp["L0_ffn_q_scale"][l]), 2)[:, None])
        d[f"ks{l}"] = np.ascontiguousarray(
            np.tile(A(inp["L0_ffn_k_scale"][l]), 2)[:, None])
        d[f"gwo{l}"] = A(inp["L0_ffn_wo"][l])
        d[f"lng{l}"] = _bcol(A(inp["L0_ffn_ln_g"][l]))
        d[f"lnb{l}"] = _bcol(A(inp["L0_ffn_ln_b"][l]))
        w1 = A(inp["L0_ffn_w1"][l])
        w1a = np.concatenate([w1[:, :HID], np.zeros((D, HIDP - HID), f32)], 1)
        w1g = np.concatenate([w1[:, HID:], np.zeros((D, HIDP - HID), f32)], 1)
        w1pp = np.empty((D, 2 * HIDP), f32)
        for j in range(NJ):
            w1pp[:, 256 * j:256 * j + 128] = w1a[:, 128 * j:128 * (j + 1)]
            w1pp[:, 256 * j + 128:256 * j + 256] = w1g[:, 128 * j:128 * (j + 1)]
        d[f"w1pp{l}"] = w1pp
        b1 = A(inp["L0_ffn_b1"][l])
        b1a = np.concatenate([b1[:HID], np.zeros(HIDP - HID, f32)])
        b1g = np.concatenate([b1[HID:], np.zeros(HIDP - HID, f32)])
        b1pp = np.empty((128, 2 * NJ), f32)
        for j in range(NJ):
            b1pp[:, 2 * j] = b1a[128 * j:128 * (j + 1)]
            b1pp[:, 2 * j + 1] = b1g[128 * j:128 * (j + 1)]
        d[f"b1pp{l}"] = b1pp
        w2 = A(inp["L0_ffn_w2"][l])
        w2p = np.concatenate([w2, np.zeros((HIDP - HID, D), f32)], 0)
        d[f"w2pt{l}"] = np.ascontiguousarray(
            w2p.reshape(NJ, 128, 8, 128).transpose(2, 1, 0, 3)
            .reshape(D, HIDP))
        d[f"b2{l}"] = _bcol(A(inp["L0_ffn_b2"][l]))
    d.update(_consts())
    return d


_SPEC_F32 = [
    ("wq_t", [D, D]), ("wk_t", [D, D]), ("wv_t", [D, D]), ("wo_t", [D, D]),
    ("bq", [128, 8]), ("bk", [128, 8]), ("bv", [128, 8]), ("bo", [128, 8]),
    ("onesf", [1, 128]), ("sel2f", [2, 256]),
    ("tcs", [128, S]), ("tss", [128, S]), ("tcis", [128, S]), ("tsis", [128, S]),
]
for _l in range(2):
    _SPEC_F32 += [
        (f"wqkv{_l}", [D, 1536]), (f"qs{_l}", [128, 1]), (f"ks{_l}", [128, 1]),
        (f"gwo{_l}", [512, D]), (f"lng{_l}", [128, 8]), (f"lnb{_l}", [128, 8]),
        (f"w1pp{_l}", [D, 2 * HIDP]), (f"b1pp{_l}", [128, 2 * NJ]),
        (f"w2pt{_l}", [D, HIDP]), (f"b2{_l}", [128, 8]),
    ]
_SPEC_BF = [
    ("lnones", [128, 1]), ("sel2", [128, 2]), ("rmat", [128, 128]),
    ("ident", [128, 128]), ("mtriu", [128, 128]), ("mtril", [128, 128]),
    ("selg_bf", [8, 512]), ("selg16_bf", [16, 1024]),
    ("selgQ", [16, 512]), ("selgK", [16, 512]), ("selC", [16, 1024]),
    ("sel16q", [128, 64]), ("sel16k", [128, 64]),
]
_RESIDENT = ([n for n, _ in _SPEC_BF
              if n not in ("selgQ", "selgK", "selC", "selg16_bf")]
             + ["bq", "bk", "bv", "bo", "onesf", "sel2f"]
             + [f"{n}{l}" for l in range(2)
                for n in ("qs", "ks", "lng", "lnb", "b1pp", "b2")])


def _att_iter(branch):
    out = []
    for qq in range(4):
        lo, hi = 4 * qq, 4 * qq + 3
        urange = range(0, hi + 1) if branch == "s" else range(max(0, lo - 4), hi + 1)
        for U in urange:
            Qlo = max(U, lo)
            Qhi = hi if branch == "s" else min(U + 4, hi)
            if Qlo > Qhi:
                continue
            tiles = []
            for Q in range(Qlo, Qhi + 1):
                m = None
                if Q == U:
                    m = "triu"
                elif branch == "g" and Q == U + 4:
                    m = "tril"
                tiles.append((Q, m))
            out.append((qq, U, Qlo, tiles))
    return out


class _B:
    def __init__(self):
        self.nc = bass.Bass("TRN2", target_bir_lowering=False,
                            detect_race_conditions=False)
        self.st = contextlib.ExitStack()
        nc = self.nc
        self.xd = nc.dram_tensor("x", [D, S], F32, kind="ExternalInput").ap()
        self.out_d = nc.dram_tensor("out", [D, S], F32, kind="ExternalOutput").ap()
        self.lnmr = nc.dram_tensor("lnmr_scr", [1, S], F32).ap()
        self.lnrs = nc.dram_tensor("lnrs_scr", [1, S], F32).ap()
        self.lnmr0 = nc.dram_tensor("lnmr0_scr", [1, S], F32).ap()
        self.lnrs0 = nc.dram_tensor("lnrs0_scr", [1, S], F32).ap()
        self.scr0_sd = 0
        self.dr = {}
        for name, shp in _SPEC_F32:
            self.dr[name] = nc.dram_tensor(name, shp, F32, kind="ExternalInput").ap()
        for name, shp in _SPEC_BF:
            self.dr[name] = nc.dram_tensor(name, shp, BF16, kind="ExternalInput").ap()
        self.dumps = {}
        for name in _DUMPS:
            self.dumps[name] = nc.dram_tensor(
                "dump_" + name, [D, S], F32, kind="ExternalOutput").ap()

        self.sem = {k: self.st.enter_context(nc.semaphore("sem_" + k))
                    for k in ("t", "v", "a", "gc", "gd", "sd")}
        self.cnt = {k: 0 for k in self.sem}
        self.srow_sd = 0

        self.xg = [self.st.enter_context(nc.sbuf_tensor(f"xg{c}", [128, SQ], F32))
                   for c in range(NCH)]
        self.ct = {}
        shp_f32 = dict(_SPEC_F32)
        shp_bf = dict(_SPEC_BF)
        for name in _RESIDENT:
            if name in shp_bf:
                self.ct[name] = self.st.enter_context(
                    nc.sbuf_tensor("c_" + name, shp_bf[name], BF16))
            else:
                self.ct[name] = self.st.enter_context(
                    nc.sbuf_tensor("c_" + name, shp_f32[name], F32))

    _uid = 0

    def sb(self, name, shape, dtype):
        _B._uid += 1
        return self.nc.sbuf_tensor(f"{name}_{_B._uid}", shape, dtype)

    def ps(self, name, shape, dtype):
        _B._uid += 1
        return self.nc.psum_tensor(f"{name}_{_B._uid}", shape, dtype)

    def inc(self, key, instr, n=1):
        instr.then_inc(self.sem[key], n)
        self.cnt[key] += n
        return self.cnt[key]

    def wt(self, eng, *deps):
        for dkey, val in deps:
            if val <= 0:
                continue
            mult = 16 if dkey in ("gd", "sd") else 1
            eng.wait_ge(self.sem[dkey], mult * val)

    def now(self, key):
        return (key, self.cnt[key])

    def gload(self, out_ap, in_ap, *deps):
        self.wt(self.nc.gpsimd, *deps)
        self.nc.gpsimd.dma_start(out=out_ap, in_=in_ap).then_inc(self.sem["gd"], 16)
        self.cnt["gd"] += 1
        return self.cnt["gd"]

    def sload(self, out_ap, in_ap, *deps):
        self.wt(self.nc.sync, *deps)
        self.nc.sync.dma_start(out=out_ap, in_=in_ap).then_inc(self.sem["sd"], 16)
        self.cnt["sd"] += 1
        return self.cnt["sd"]

    def ckpt(self, name):
        self.barrier()
        if _STOP_AFTER == name:
            raise _StopEmission()

    def barrier(self):
        snap = [(k, self.cnt[k]) for k in self.sem]
        for eng in (self.nc.tensor, self.nc.vector, self.nc.scalar,
                    self.nc.gpsimd, self.nc.sync):
            self.wt(eng, *snap)

    def ln_finalize(self, pm, pq, mrow, rrow, dst=None):
        nc = self.nc
        dmr, drs = dst if dst is not None else (self.lnmr, self.lnrs)
        self.wt(nc.vector, self.now("t"))
        self.inc("v", nc.vector.tensor_copy(out=mrow[:], in_=pm[:]))
        vd = self.inc("v", nc.vector.tensor_copy(out=rrow[:], in_=pq[:]))
        self.wt(nc.scalar, ("v", vd))
        ad = self.inc("a", nc.scalar.activation(out=pm[:], in_=mrow[:],
                                                func=AF.Square))
        self.wt(nc.vector, ("a", ad))
        nc.vector.tensor_sub(out=rrow[:], in0=rrow[:], in1=pm[:])
        vd = self.inc("v", nc.vector.tensor_scalar_add(out=rrow[:], in0=rrow[:],
                                                       scalar1=LN_EPS))
        self.wt(nc.scalar, ("v", vd))
        nc.scalar.activation(out=rrow[:], in_=rrow[:], func=AF.Ln)
        ad = self.inc("a", nc.scalar.activation(out=rrow[:], in_=rrow[:],
                                                func=AF.Exp, scale=-0.5))
        self.wt(nc.vector, ("a", ad))
        vd = self.inc("v", nc.vector.tensor_mul(out=mrow[:], in0=mrow[:],
                                                in1=rrow[:]))
        self.sload(dmr, mrow[:], ("v", vd))
        return self.sload(drs, rrow[:])

    def ln_bcast(self, pmb, prb, mrow, rrow, scr_dep, src=None):
        nc = self.nc
        smr, srs = src if src is not None else (self.lnmr, self.lnrs)
        sd1 = self.sload(mrow[:], smr, ("sd", scr_dep))
        sd2 = self.sload(rrow[:], srs)
        onesf = self.ct["onesf"]
        self.wt(nc.tensor, ("sd", sd2))
        i = None
        for n in range(NB4):
            sl = slice(512 * n, 512 * (n + 1))
            nc.tensor.matmul(pmb[:, sl], onesf[:], mrow[0:1, sl],
                             start=True, stop=True)
            i = nc.tensor.matmul(prb[:, sl], onesf[:], rrow[0:1, sl],
                                 start=True, stop=True)
        return self.inc("t", i)

    def att_pair_gen(self, branch, qtile, ktile, vo_a, vo_b, bias_a, bias_b,
                     scale, oun_tile, sums_t, row0, srow, pqk_a, pqk_b,
                     ppv_a, ppv_b, pexp_a2, pexp_b2, st, entry_deps=()):
        nc = self.nc
        items = _att_iter(branch)
        nit = len(items)
        pend = None

        def emit_pv(p):
            (qq_, U_, first_, last_, eA, eB, vA, vB, par_) = p
            self.wt(nc.tensor, ("a", eA), ("v", vA),
                    ("a", st["evacA"] if first_ else 0),
                    ("v", st["evacV"] if first_ else 0))
            pvA = self.inc("t", nc.tensor.matmul(
                ppv_a[0:65, 0:512], vo_a[:, 65 * U_:65 * U_ + 65],
                pexp_a2[par_][:, 0:512], start=first_, stop=last_))
            self.wt(nc.tensor, ("a", eB), ("v", vB),
                    ("a", st["evacB"] if first_ else 0))
            pvB = self.inc("t", nc.tensor.matmul(
                ppv_b[0:65, 0:512], vo_b[:, 65 * U_:65 * U_ + 65],
                pexp_b2[par_][:, 0:512], start=first_, stop=last_))
            st["pv2"][par_] = pvB
            st["pv"] = pvB
            if last_:
                self.wt(nc.scalar, ("t", pvB), ("sd", self.srow_sd))
                qsl = slice(512 * qq_, 512 * (qq_ + 1))
                nc.scalar.copy(out=srow[0:1, 0:512], in_=ppv_a[64:65, 0:512])
                ev = self.inc("a", nc.scalar.copy(
                    out=srow[0:1, 512:1024], in_=ppv_b[64:65, 0:512]))
                self.wt(nc.vector, ("t", pvB))
                nc.vector.tensor_copy(out=oun_tile[0:64, qsl],
                                      in_=ppv_a[0:64, 0:512])
                evv = self.inc("v", nc.vector.tensor_copy(
                    out=oun_tile[64:128, qsl], in_=ppv_b[0:64, 0:512]))
                st["evacA"] = ev
                st["evacB"] = ev
                st["evacV"] = evv
                self.srow_sd = self.sload(
                    sums_t[row0:row0 + 2, qsl], srow[0:1, 0:1024], ("a", ev))

        for k in range(nit):
            qq, U, Qlo, tiles = items[k]
            span = 128 * len(tiles)
            off = 128 * (Qlo - 4 * qq)
            if k == 0:
                self.wt(nc.tensor, *entry_deps)
            self.wt(nc.tensor, ("a", st["expA"]))
            nc.tensor.matmul(pqk_a[:, 0:span],
                             ktile[0:64, 128 * U:128 * (U + 1)],
                             qtile[0:64, 128 * Qlo:128 * Qlo + span],
                             start=True, stop=True)
            self.wt(nc.tensor, ("a", st["expB"]))
            qk = self.inc("t", nc.tensor.matmul(
                pqk_b[:, 0:span], ktile[64:128, 128 * U:128 * (U + 1)],
                qtile[64:128, 128 * Qlo:128 * Qlo + span],
                start=True, stop=True))
            if pend is not None:
                emit_pv(pend)
                pend = None
            par = st["ep"] % 2
            st["ep"] += 1
            pexp_a = pexp_a2[par]
            pexp_b = pexp_b2[par]
            pvdep = st["pv2"][par]
            nz = 0
            zlast = None
            if off > 0 or off + span < 512:
                self.wt(nc.vector, ("t", pvdep))
                for pe in (pexp_a, pexp_b):
                    if off > 0:
                        zlast = nc.vector.memset(pe[:, 0:off], 0.0)
                        nz = 1
                    if off + span < 512:
                        zlast = nc.vector.memset(pe[:, off + span:512], 0.0)
                        nz = 1
            ztick = self.inc("v", zlast) if nz else 0
            self.wt(nc.scalar, ("t", max(qk, pvdep)))
            st["expA"] = self.inc("a", nc.scalar.activation(
                out=pexp_a[:, off:off + span], in_=pqk_a[:, 0:span],
                func=AF.Exp, bias=bias_a, scale=scale))
            st["expB"] = self.inc("a", nc.scalar.activation(
                out=pexp_b[:, off:off + span], in_=pqk_b[:, 0:span],
                func=AF.Exp, bias=bias_b, scale=scale))
            nmask = 0
            last = None
            for idx, (Q, m) in enumerate(tiles):
                if m is None:
                    continue
                mt = self.ct["mtriu"] if m == "triu" else self.ct["mtril"]
                o2 = off + 128 * idx
                if nmask == 0:
                    self.wt(nc.vector, ("a", st["expB"]))
                nc.vector.tensor_mul(out=pexp_a[:, o2:o2 + 128],
                                     in0=pexp_a[:, o2:o2 + 128], in1=mt[:])
                last = nc.vector.tensor_mul(out=pexp_b[:, o2:o2 + 128],
                                            in0=pexp_b[:, o2:o2 + 128],
                                            in1=mt[:])
                nmask += 1
            mtick = self.inc("v", last) if nmask else 0
            first = (k == 0 or items[k - 1][0] != qq)
            lastu = (k + 1 >= nit or items[k + 1][0] != qq)
            pend = (qq, U, first, lastu, st["expA"], st["expB"],
                    max(mtick, ztick), max(mtick, ztick), par)
            yield
        emit_pv(pend)

    def v_block(self, pP, n, vbt, vo_a, vo_b, pTs, tr_dep, bias=None,
                vo_guard=0):
        nc = self.nc
        self.wt(nc.scalar, self.now("t"))
        if bias is None:
            ev = self.inc("a", nc.scalar.copy(out=vbt[:], in_=pP[:]))
        else:
            ev = self.inc("a", nc.scalar.activation(out=vbt[:], in_=pP[:],
                                                    func=AF.Identity, bias=bias))
        for j in range(4):
            U = 4 * n + j
            pT = pTs[U % 2]
            self.wt(nc.tensor, ("a", ev), ("v", tr_dep[U % 2]))
            td = self.inc("t", nc.tensor.transpose(
                pT[:], vbt[:, 128 * j:128 * (j + 1)], self.ct["ident"][:]))
            self.wt(nc.vector, ("t", td), ("t", vo_guard))
            nc.vector.tensor_copy(out=vo_a[:, 65 * U:65 * U + 64],
                                  in_=pT[:, 0:64])
            tr_dep[U % 2] = self.inc("v", nc.vector.tensor_copy(
                out=vo_b[:, 65 * U:65 * U + 64], in_=pT[:, 64:128]))
        return ev


def _build():
    b = _B()
    try:
        _build_body(b)
        b.barrier()
        b.st.close()
    except _StopEmission:
        b.barrier()
    return b.nc


def _build_body(b):
    nc = b.nc
    xg = b.xg
    ct = b.ct
    xd = b.xd

    for name in _RESIDENT:
        b.sload(ct[name][:], b.dr[name])
    for c in range(NCH):
        b.wt(nc.gpsimd, b.now("sd"))
        b.inc("gc", nc.gpsimd.memset(xg[c][:, 0:1], 0.0))
    b.barrier()

    for l in range(2):
        with contextlib.ExitStack() as sc:
            xb = [sc.enter_context(b.sb(f"xb{c}", [128, SQ], BF16))
                  for c in range(NCH)]
            vo = [sc.enter_context(b.sb(f"vog{i}", [128, 16 * 65], BF16))
                  for i in range(8)]
            oun_g = [sc.enter_context(b.sb(f"og{i}", [128, S], BF16))
                     for i in range(4)]
            cbias = sc.enter_context(b.sb("cbias", [128, 8], F32))

            for i in range(8):
                oc = vo[i].ap().rearrange("p (u c) -> p u c", c=65)[:, :, 64:65]
                b.inc("gc", nc.gpsimd.memset(oc, 1.0))
            b.barrier()

            if l == 0:
                ln1 = ct["lnones"]
                with contextlib.ExitStack() as lsc:
                    sqb = lsc.enter_context(
                        b.sb("sqb0", [128, SQ], BF16))
                    mrow = lsc.enter_context(b.sb("mrow0", [1, S], F32))
                    rrow = lsc.enter_context(b.sb("rrow0", [1, S], F32))
                    sdl = [None] * NCH
                    for c in range(6):
                        sdl[c] = ("sd", b.sload(xg[c][:, 1:SQ],
                                                xd[128 * c:128 * (c + 1), :]))
                    with (b.ps("pm0", [1, S], F32) as pm,
                          b.ps("pq0", [1, S], F32) as pq):
                        dep_a = 0
                        for c in range(NCH):
                            gd = b.gload(xb[c][:, 1:SQ],
                                         xd[128 * c:128 * (c + 1), :])
                            b.wt(nc.scalar, ("gd", gd), ("a", dep_a),
                                 b.now("t"))
                            dep_a = b.inc("a", nc.scalar.activation(
                                out=sqb[:, 1:SQ], in_=xb[c][:, 1:SQ],
                                func=AF.Square))
                            b.wt(nc.tensor, ("a", dep_a))
                            i = None
                            for n in range(NB4):
                                sl = slice(1 + 512 * n, 1 + 512 * (n + 1))
                                nc.tensor.matmul(
                                    pm[0:1, 512 * n:512 * (n + 1)], ln1[:],
                                    xb[c][:, sl], start=(c == 0), stop=(c == 7))
                            for n in range(NB4):
                                sl = slice(1 + 512 * n, 1 + 512 * (n + 1))
                                i = nc.tensor.matmul(
                                    pq[0:1, 512 * n:512 * (n + 1)], ln1[:],
                                    sqb[:, sl], start=(c == 0), stop=(c == 7))
                            b.inc("t", i)
                        for c in range(6, NCH):
                            sdl[c] = ("gd", b.gload(
                                xg[c][:, 1:SQ], xd[128 * c:128 * (c + 1), :]))
                        scr = b.ln_finalize(pm, pq, mrow, rrow,
                                            dst=(b.lnmr0, b.lnrs0))
                        b.scr0_sd = scr
                    with (b.ps("pmb0", [128, S], F32) as pmb,
                          b.ps("prb0", [128, S], F32) as prb):
                        tdep = b.ln_bcast(pmb, prb, mrow, rrow, scr,
                                          src=(b.lnmr0, b.lnrs0))
                        for c in range(NCH):
                            b.wt(nc.vector, sdl[c], ("t", tdep))
                            nc.vector.tensor_mul(out=xg[c][:, 1:SQ],
                                                 in0=xg[c][:, 1:SQ], in1=prb[:])
                            vmu = b.inc("v", nc.vector.tensor_sub(
                                out=xg[c][:, 1:SQ], in0=xg[c][:, 1:SQ],
                                in1=pmb[:]))
                            b.wt(nc.scalar, ("v", vmu))
                            b.inc("a", nc.scalar.copy(
                                out=xb[c][:, 1:SQ], in_=xg[c][:, 1:SQ]))
                    b.barrier()
                b.ckpt(f"ln{l}")
                if "xn" in b.dumps:
                    for c in range(NCH):
                        b.sload(b.dumps["xn"][128 * c:128 * (c + 1), :],
                                xg[c][:, 1:SQ])
                    b.barrier()
            else:
                for c in range(NCH):
                    if c % 2 == 0:
                        b.inc("v", nc.vector.tensor_copy(
                            out=xb[c][:, 1:SQ], in_=xg[c][:, 1:SQ]))
                    else:
                        b.inc("a", nc.scalar.copy(
                            out=xb[c][:, 1:SQ], in_=xg[c][:, 1:SQ]))
                b.barrier()
            for c in range(NCH):
                b.inc("gc", nc.gpsimd.memset(xb[c][:, 0:1], 0.0))
            b.barrier()

            with contextlib.ExitStack() as isc:
                qb = [isc.enter_context(b.sb(f"qb{i}", [128, S], BF16))
                      for i in range(4)]
                kb = [isc.enter_context(b.sb(f"kb{i}", [128, S], BF16))
                      for i in range(4)]

                rq16 = isc.enter_context(b.sb("rq16", [16, S], BF16))
                with contextlib.ExitStack() as nsc:
                    n2all = nsc.enter_context(
                        b.sb("n2all", [16, S], BF16))
                    numall = nsc.enter_context(
                        b.sb("numall", [16, S], BF16))
                    dmax16 = nsc.enter_context(
                        b.sb("dmax16", [16, 1], BF16))
                    selC_t = nsc.enter_context(
                        b.sb("selC_t", [16, 1024], BF16))
                    selC_sd = b.sload(selC_t[:], b.dr["selC"])

                    with contextlib.ExitStack() as asc:
                        wall = asc.enter_context(
                            b.sb("wall", [128, 8 * D], BF16))
                        vbt = asc.enter_context(
                            b.sb("vbt", [128, 512], BF16))
                        tbg2 = [asc.enter_context(
                            b.sb(f"tbg2{i}", [128, 512], BF16))
                            for i in range(2)]
                        sel16w = asc.enter_context(
                            b.sb("sel16w", [128, 128], BF16))
                        qk2t = asc.enter_context(
                            b.sb("qk2t", [128, 2], F32))

                        qsc, ksc = ct[f"qs{l}"], ct[f"ks{l}"]
                        b.wt(nc.vector, b.now("sd"))
                        nc.vector.tensor_mul(out=qk2t[:, 0:1], in0=qsc[:],
                                             in1=qsc[:])
                        nc.vector.tensor_mul(out=qk2t[:, 1:2], in0=ksc[:],
                                             in1=ksc[:])
                        nc.vector.tensor_scalar(
                            out=sel16w[:, 0:64], in0=ct["sel16q"][:],
                            scalar1=qk2t[:, 0:1], scalar2=None, op0=ALU.mult)
                        b.inc("v", nc.vector.tensor_scalar(
                            out=sel16w[:, 64:128], in0=ct["sel16k"][:],
                            scalar1=qk2t[:, 1:2], scalar2=None, op0=ALU.mult))
                        b.barrier()

                        with (b.ps("pP0", [128, 512], F32) as pP0,
                              b.ps("pP1", [128, 512], F32) as pP1,
                              b.ps("pN16a", [16, 512], F32) as pN16a,
                              b.ps("pN16b", [16, 512], F32) as pN16b,
                              b.ps("pT0", [128, 128], BF16) as pT0,
                              b.ps("pT1", [128, 128], BF16) as pT1):
                            pPs = [pP0, pP1]
                            evac_dep = [0, 0]
                            pn_dep = 0
                            tr_dep = [0, 0]
                            pi = 0
                            wsrc = b.dr[f"wqkv{l}"].rearrange(
                                "(k p) c -> p k c", p=128)
                            wview = wall.ap().rearrange(
                                "p (s k c) -> p (s k) c", s=8, c=128)
                            gd = 0
                            for hp in range(4):
                                gd = b.gload(
                                    wview[:, 8 * hp:8 * (hp + 1), :],
                                    wsrc[:, :, 128 * hp:128 * (hp + 1)])
                                gd = b.gload(
                                    wview[:, 8 * (4 + hp):8 * (5 + hp), :],
                                    wsrc[:, :, 512 + 128 * hp:
                                         512 + 128 * (hp + 1)])
                            gddep = ("gd", gd)
                            b.ckpt("g1aw")
                            npend = [None]
                            pn_dep = [pn_dep]

                            def emit_norm_g(first, last, selA, selB, sq_, tb_):
                                b.wt(nc.tensor, ("a", sq_))
                                if first:
                                    b.wt(nc.tensor, ("a", pn_dep[0]))
                                nc.tensor.matmul(
                                    pN16a[0:16, 0:512], selA, tb_[:],
                                    start=first, stop=last)
                                return b.inc("t", nc.tensor.matmul(
                                    pN16b[0:16, 0:512], selB, tb_[:],
                                    start=first, stop=last))

                            for n in range(NB4):
                                nsl = slice(512 * n, 512 * (n + 1))
                                for tsel in range(2):
                                    dst = (qb, kb)[tsel]
                                    selp = (ct["sel16q"] if tsel == 0
                                            else ct["sel16k"])
                                    woff = 64 * tsel
                                    for hp in range(4):
                                        pP = pPs[pi % 2]
                                        blk = 4 * tsel + hp
                                        b.wt(nc.tensor, gddep,
                                             ("a", evac_dep[pi % 2]))
                                        i = None
                                        for k in range(NCH):
                                            off = 0 if k >= 4 else 1
                                            sl = slice(off + 512 * n,
                                                       off + 512 * (n + 1))
                                            i = nc.tensor.matmul(
                                                pP[:],
                                                wall[:, 1024 * blk + 128 * k:
                                                     1024 * blk
                                                     + 128 * (k + 1)],
                                                xb[k][:, sl], start=(k == 0),
                                                stop=(k == 7))
                                        mmdep = b.inc("t", i)
                                        if npend[0] is not None:
                                            emit_norm_g(*npend[0])
                                            npend[0] = None
                                        b.wt(nc.scalar, ("t", mmdep))
                                        nc.scalar.copy(
                                            out=dst[hp][:, nsl], in_=pP[:])
                                        sq = b.inc("a", nc.scalar.activation(
                                            out=tbg2[pi % 2][:],
                                            in_=dst[hp][:, nsl],
                                            func=AF.Square))
                                        first = (tsel == 0 and hp == 0)
                                        last = (tsel == 1 and hp == 3)
                                        npend[0] = (
                                            first, last,
                                            selp[:, 16 * hp:16 * (hp + 1)],
                                            sel16w[:, woff + 16 * hp:
                                                   woff + 16 * (hp + 1)],
                                            sq, tbg2[pi % 2])
                                        evac_dep[pi % 2] = sq
                                        pi += 1
                                td = emit_norm_g(*npend[0])
                                npend[0] = None
                                b.wt(nc.scalar, ("t", td))
                                nc.scalar.copy(out=n2all[0:16, nsl],
                                               in_=pN16a[0:16, 0:512])
                                pn_dep[0] = b.inc("a", nc.scalar.copy(
                                    out=numall[0:16, nsl],
                                    in_=pN16b[0:16, 0:512]))
                            b.ckpt("g1aq")
                            b.wt(nc.gpsimd, b.now("t"), b.now("a"))
                            gd = 0
                            for hp in range(4):
                                gd = b.gload(
                                    wview[:, 8 * hp:8 * (hp + 1), :],
                                    wsrc[:, :, 1024 + 128 * hp:
                                         1024 + 128 * (hp + 1)])
                            gddep = ("gd", gd)
                            for n in range(NB4):
                                for hp in range(4):
                                    pP = pPs[pi % 2]
                                    b.wt(nc.tensor, gddep,
                                         ("a", evac_dep[pi % 2]))
                                    i = None
                                    for k in range(NCH):
                                        off = 0 if k >= 4 else 1
                                        sl = slice(off + 512 * n,
                                                   off + 512 * (n + 1))
                                        i = nc.tensor.matmul(
                                            pP[:],
                                            wall[:, 1024 * hp + 128 * k:
                                                 1024 * hp + 128 * (k + 1)],
                                            xb[k][:, sl], start=(k == 0),
                                            stop=(k == 7))
                                    b.inc("t", i)
                                    evac_dep[pi % 2] = b.v_block(
                                        pP, n, vbt, vo[2 * hp],
                                        vo[2 * hp + 1], [pT0, pT1], tr_dep)
                                    pi += 1
                            b.barrier()
                    b.ckpt(f"g1a{l}")

                    nc.vector.tensor_scalar_max(out=n2all[:], in0=n2all[:],
                                                scalar1=1e-24)
                    vd = b.inc("v", nc.vector.tensor_scalar_max(
                        out=numall[:], in0=numall[:], scalar1=1e-30))
                    b.wt(nc.scalar, ("v", vd))
                    nc.scalar.activation(out=n2all[:], in_=n2all[:], func=AF.Ln)
                    nc.scalar.activation(out=rq16[:], in_=n2all[:],
                                         func=AF.Exp, scale=-0.5)
                    ad = b.inc("a", nc.scalar.activation(
                        out=numall[:], in_=numall[:], func=AF.Ln))
                    b.wt(nc.vector, ("a", ad))
                    nc.vector.tensor_sub(out=numall[:], in0=numall[:],
                                         in1=n2all[:])
                    vd = b.inc("v", nc.vector.tensor_reduce(
                        out=dmax16[:], in_=numall[0:16, 0:S], axis=AX.X,
                        op=ALU.max))
                    with b.ps("pb1", [128, 8], F32) as pb1:
                        b.wt(nc.tensor, ("v", vd), ("sd", selC_sd))
                        i = None
                        for h in range(8):
                            i = nc.tensor.matmul(
                                pb1[:, h:h + 1],
                                selC_t[:, 128 * h:128 * (h + 1)],
                                dmax16[0:16, 0:1], start=True, stop=True)
                        td = b.inc("t", i)
                        b.wt(nc.vector, ("t", td))
                        vd = b.inc("v", nc.vector.tensor_scalar_add(
                            out=cbias[:], in0=pb1[:], scalar1=LOG8))
                        b.wt(nc.scalar, ("v", vd))
                        ad = b.inc("a", nc.scalar.activation(
                            out=cbias[:], in_=cbias[:], func=AF.Exp))
                        b.wt(nc.vector, ("a", ad))
                        b.inc("v", nc.vector.tensor_scalar_mul(
                            out=cbias[:], in0=cbias[:], scalar1=-1.0))
                    b.barrier()

                if True:
                    with contextlib.ExitStack() as rsc:
                        selg2_t = rsc.enter_context(
                            b.sb("selg2_t", [16, 1024], BF16))
                        b.sload(selg2_t[:, 0:512], b.dr["selgQ"])
                        selg2_sd = b.sload(selg2_t[:, 512:1024],
                                           b.dr["selgK"])
                        tblA = [rsc.enter_context(
                            b.sb(f"tblA{i}", [128, 512], F32))
                            for i in range(2)]
                        tblB = rsc.enter_context(
                            b.sb("tblB", [128, 512], F32))
                        tb512c = rsc.enter_context(
                            b.sb("tb512c", [128, 512], BF16))
                        tf512 = [rsc.enter_context(b.sb(
                            f"tf512{i}", [128, 512], F32)) for i in range(2)]
                        gsums = rsc.enter_context(b.sb("gsums", [8, S], F32))
                        grb_bf = rsc.enter_context(b.sb("grb", [8, S], BF16))
                        srow = rsc.enter_context(b.sb("srow", [1, 1024], F32))
                        pexp = [rsc.enter_context(b.sb(
                            f"pexp{i}", [128, 512], BF16)) for i in range(4)]
                        with (b.ps("prb1", [128, 512], F32) as prb1,
                              b.ps("pR0", [128, 512], F32) as pR0,
                              b.ps("pR1", [128, 512], F32) as pR1,
                              b.ps("pqa0", [128, 512], F32) as pqa0,
                              b.ps("pqa1", [128, 512], F32) as pqa1,
                              b.ps("ppvg", [65, 512], F32) as ppva,
                              b.ps("ppvg2", [65, 512], F32) as ppvb):
                            qsc, ksc = ct[f"qs{l}"], ct[f"ks{l}"]
                            pRs = [pR0, pR1]
                            riL = [0]
                            vdep_r = [0, 0]
                            tbl_free = [0, 0]
                            tbl_sdd = [0, 0]
                            tseq = []
                            for (tA, tB) in (("tcs", "tss"),
                                             ("tcis", "tsis")):
                                for n in range(NB4):
                                    tseq.append(
                                        (tA, tB,
                                         slice(512 * n, 512 * (n + 1))))

                            def load_tbl(j, bb):
                                tA, tB, nsl = tseq[j % 8]
                                b.sload(tblA[bb][:], b.dr[tA][:, nsl],
                                        ("v", tbl_free[bb]))

                            def g1b_gen(hp):
                                for t_i, (dst, scol, selo) in enumerate((
                                        (qb, qsc, 0),
                                        (kb, ksc, 512))):
                                    for n in range(NB4):
                                        nsl = slice(512 * n, 512 * (n + 1))
                                        ri = riL[0]
                                        bb = ri % 2
                                        if ri == 0:
                                            load_tbl(0, 0)
                                        load_tbl(ri + 1, (ri + 1) % 2)
                                        tB = tseq[ri % 8][1]
                                        sdd = b.sload(
                                            tblB[:], b.dr[tB][:, nsl],
                                            ("v", vdep_r[(ri + 1) % 2]))
                                        b.wt(nc.tensor,
                                             ("v", vdep_r[(ri + 1) % 2]),
                                             ("sd", selg2_sd))
                                        td = b.inc("t", nc.tensor.matmul(
                                            prb1[:],
                                            selg2_t[:, selo + 128 * hp:
                                                    selo + 128 * (hp + 1)],
                                            rq16[0:16, nsl], start=True,
                                            stop=True))
                                        b.wt(nc.vector, ("t", td))
                                        vd = b.inc(
                                            "v",
                                            nc.vector.scalar_tensor_tensor(
                                                out=tb512c[:],
                                                in0=dst[hp][:, nsl],
                                                scalar=scol[:], in1=prb1[:],
                                                op0=ALU.mult, op1=ALU.mult))
                                        pR = pRs[ri % 2]
                                        b.wt(nc.tensor, ("v", vd),
                                             ("v", vdep_r[ri % 2]))
                                        td = b.inc("t", nc.tensor.matmul(
                                            pR[:], ct["rmat"][:], tb512c[:],
                                            start=True, stop=True))
                                        b.wt(nc.vector, ("sd", sdd))
                                        nc.vector.tensor_mul(
                                            out=tf512[0][:], in0=tb512c[:],
                                            in1=tblA[bb][:])
                                        b.wt(nc.vector, ("t", td))
                                        nc.vector.tensor_mul(
                                            out=tf512[1][:], in0=pR[:],
                                            in1=tblB[:])
                                        vdep_r[ri % 2] = b.inc(
                                            "v", nc.vector.tensor_add(
                                                out=dst[hp][:, nsl],
                                                in0=tf512[0][:],
                                                in1=tf512[1][:]))
                                        tbl_free[bb] = vdep_r[ri % 2]
                                        riL[0] += 1
                                        yield

                            ast = {"expA": 0, "expB": 0, "evacA": 0,
                                   "evacB": 0, "evacV": 0, "pv": 0,
                                   "pv2": [0, 0], "ep": 0}
                            att = None
                            for hp in range(4):
                                pg = g1b_gen(hp)
                                for _ in pg:
                                    for _i in range(4):
                                        if att is not None:
                                            try:
                                                next(att)
                                            except StopIteration:
                                                att = None
                                while att is not None:
                                    try:
                                        next(att)
                                    except StopIteration:
                                        att = None
                                rdy = ("v", vdep_r[(riL[0] + 1) % 2])
                                rdy2 = ("v", vdep_r[riL[0] % 2])
                                att = b.att_pair_gen(
                                    "g", qb[hp], kb[hp], vo[2 * hp],
                                    vo[2 * hp + 1],
                                    cbias[:, 2 * hp:2 * hp + 1],
                                    cbias[:, 2 * hp + 1:2 * hp + 2],
                                    8.0, oun_g[hp], gsums, 2 * hp, srow,
                                    pqa0, pqa1, ppva, ppvb,
                                    [pexp[0], pexp[1]], [pexp[2], pexp[3]],
                                    ast, entry_deps=(rdy, rdy2))
                            while att is not None:
                                try:
                                    next(att)
                                except StopIteration:
                                    att = None
                            b.barrier()
                            b.ckpt(f"g2{l}")

                        if l == 0 and "att0" in b.dumps:
                            dmp = b.dumps["att0"]
                            for hp in range(4):
                                b.gload(dmp[128 * hp:128 * (hp + 1), :],
                                        oun_g[hp][:])
                            b.sload(dmp[512:520, :], gsums[0:8, :])
                            b.barrier()

                        nc.scalar.activation(out=gsums[:], in_=gsums[:],
                                             func=AF.Ln)
                        ad = b.inc("a", nc.scalar.activation(
                            out=grb_bf[:], in_=gsums[:], func=AF.Exp,
                            scale=-1.0))
                        with (b.ps("prng0", [128, 512], F32) as prn0,
                              b.ps("prng1", [128, 512], F32) as prn1):
                            prns = [prn0, prn1]
                            od = [0, 0]
                            oi = 0
                            for hp in range(4):
                                for n in range(NB4):
                                    nsl = slice(512 * n, 512 * (n + 1))
                                    prn = prns[oi % 2]
                                    b.wt(nc.tensor, ("a", ad),
                                         ("v", od[oi % 2]))
                                    td = b.inc("t", nc.tensor.matmul(
                                        prn[:],
                                        ct["selg_bf"][:, 128 * hp:
                                                      128 * (hp + 1)],
                                        grb_bf[0:8, nsl], start=True,
                                        stop=True))
                                    b.wt(nc.vector, ("t", td))
                                    od[oi % 2] = b.inc(
                                        "v", nc.vector.tensor_mul(
                                            out=oun_g[hp][:, nsl],
                                            in0=oun_g[hp][:, nsl],
                                            in1=prn[:]))
                                    oi += 1
                        b.barrier()

            with contextlib.ExitStack() as wsc:
                wo4 = wsc.enter_context(b.sb("wo4", [128, 4096], BF16))
                gw = b.dr[f"gwo{l}"]
                gdw = 0
                for k in range(4):
                    gdw = b.gload(wo4[:, 1024 * k:1024 * (k + 1)],
                                  gw[128 * k:128 * (k + 1), :])
                with (b.ps("pW0", [128, 512], F32) as pW0,
                      b.ps("pW1", [128, 512], F32) as pW1):
                    pWs = [pW0, pW1]
                    vdep_w = [0, 0]
                    wi = 0
                    for m in range(NCH):
                        for n in range(NB4):
                            pW = pWs[wi % 2]
                            b.wt(nc.tensor, ("gd", gdw), ("v", vdep_w[wi % 2]))
                            i = None
                            for k in range(4):
                                i = nc.tensor.matmul(
                                    pW[:], wo4[:, 1024 * k + 128 * m:
                                                1024 * k + 128 * (m + 1)],
                                    oun_g[k][:, 512 * n:512 * (n + 1)],
                                    start=(k == 0), stop=(k == 3))
                            td = b.inc("t", i)
                            b.wt(nc.vector, ("t", td))
                            xsl = slice(1 + 512 * n, 1 + 512 * (n + 1))
                            nc.vector.tensor_add(out=xg[m][:, xsl],
                                                 in0=xg[m][:, xsl], in1=pW[:])
                            vdep_w[wi % 2] = b.inc(
                                "v", nc.vector.tensor_copy(
                                    out=xb[m][:, xsl], in_=xg[m][:, xsl]))
                            wi += 1
                    b.barrier()
            b.ckpt(f"g3{l}")

            ln1 = ct["lnones"]
            with contextlib.ExitStack() as lsc:
                sqb = lsc.enter_context(b.sb("sqb2", [128, SQ], BF16))
                mrow = lsc.enter_context(b.sb("mrow2", [1, S], F32))
                rrow = lsc.enter_context(b.sb("rrow2", [1, S], F32))
                with (b.ps("pm2", [1, S], F32) as pm,
                      b.ps("pq2", [1, S], F32) as pq):
                    dep_a = 0
                    for c in range(NCH):
                        b.wt(nc.scalar, b.now("t"), ("a", dep_a))
                        dep_a = b.inc("a", nc.scalar.activation(
                            out=sqb[:, 0:SQ], in_=xg[c][:, 0:SQ],
                            func=AF.Square))
                        off = 0 if c >= 4 else 1
                        b.wt(nc.tensor, ("a", dep_a))
                        i = None
                        for n in range(NB4):
                            sl = slice(off + 512 * n, off + 512 * (n + 1))
                            nc.tensor.matmul(pm[0:1, 512 * n:512 * (n + 1)],
                                             ln1[:], xb[c][:, sl],
                                             start=(c == 0), stop=(c == 7))
                        for n in range(NB4):
                            sl = slice(off + 512 * n, off + 512 * (n + 1))
                            i = nc.tensor.matmul(
                                pq[0:1, 512 * n:512 * (n + 1)], ln1[:],
                                sqb[:, sl], start=(c == 0), stop=(c == 7))
                        b.inc("t", i)
                    scrB = b.ln_finalize(pm, pq, mrow, rrow)
                b.barrier()

        with contextlib.ExitStack() as sc:
            h2n = [sc.enter_context(b.sb(f"h2n{c}", [128, S], BF16))
                   for c in range(NCH)]
            lngc, lnbc = ct[f"lng{l}"], ct[f"lnb{l}"]

            with contextlib.ExitStack() as fsc:
                w1t = [fsc.enter_context(
                    b.sb(f"w1t{i}", [128, 8 * 2048], BF16))
                    for i in range(2)]
                w2c = [fsc.enter_context(
                    b.sb(f"w2c{i}", [128, HIDP], BF16)) for i in range(2)]
                h4all = fsc.enter_context(
                    b.sb("h4all", [128, NJ * 512], BF16))
                gel = [fsc.enter_context(
                    b.sb(f"gel{i}", [128, 512], BF16))
                    for i in range(2)]
                rowsB = fsc.enter_context(b.sb("rowsB", [1, 512], F32))
                tfaB = fsc.enter_context(b.sb("tfaB", [128, 512], F32))
                xcol = fsc.enter_context(b.sb("xcolB", [128, 4], F32))
                w1d = b.dr[f"w1pp{l}"]
                w2d = b.dr[f"w2pt{l}"]
                b1c = ct[f"b1pp{l}"]
                b2c = ct[f"b2{l}"]
                thirds = [(0, 8), (8, 16), (16, 22)]

                def load_third(tix, buf, dep):
                    j0, j1 = thirds[tix]
                    w = 256 * (j1 - j0)
                    gd = 0
                    for k in range(NCH):
                        gd = b.gload(
                            w1t[buf][:, 2048 * k:2048 * k + w],
                            w1d[128 * k:128 * (k + 1), 256 * j0:256 * j1],
                            ("t", dep) if k == 0 else ("t", 0))
                    return gd

                with (b.ps("pag0", [128, 1024], F32) as pag0,
                      b.ps("pag1", [128, 1024], F32) as pag1,
                      b.ps("pW20", [128, 512], F32) as pW20,
                      b.ps("pW21", [128, 512], F32) as pW21,
                      b.ps("pmbB", [128, 512], F32) as pmbB,
                      b.ps("prbB", [128, 512], F32) as prbB):
                    pags = [pag0, pag1]
                    pW2s = [pW20, pW21]
                    gel_dep = [0, 0]
                    h4_dep = [0, 0]
                    w1gd = [0, 0]
                    w1_used = [0, 0]
                    gthird = 0
                    w1gd[0] = load_third(0, 0, 0)
                    gd_w2 = [0, 0]
                    w2_used = [0, 0]
                    gd_w2[0] = b.gload(w2c[0][:], w2d[0:128, :])
                    res_dep = 0
                    w2_done = 0

                    onesf = ct["onesf"]
                    ap_done = [0]
                    bc_dep = [0]
                    h2n_rdy = [0, 0, 0, 0]

                    def apply_begin(n):
                        nsl = slice(512 * n, 512 * (n + 1))
                        sd = b.sload(rowsB[0:1, 0:512], b.lnmr[0:1, nsl],
                                     ("sd", scrB), ("t", bc_dep[0]))
                        b.wt(nc.tensor, ("sd", sd), ("v", ap_done[0]))
                        t1 = b.inc("t", nc.tensor.matmul(
                            pmbB[:, 0:512], onesf[:], rowsB[0:1, 0:512],
                            start=True, stop=True))
                        sd = b.sload(rowsB[0:1, 0:512], b.lnrs[0:1, nsl],
                                     ("t", t1))
                        b.wt(nc.tensor, ("sd", sd))
                        bc_dep[0] = b.inc("t", nc.tensor.matmul(
                            prbB[:, 0:512], onesf[:], rowsB[0:1, 0:512],
                            start=True, stop=True))

                    def apply_c(n, c, xc=None):
                        tfa = tfaB[:]
                        nsl = slice(512 * n, 512 * (n + 1))
                        off = 0 if c >= 4 else 1
                        hsl = slice(off + 512 * n, off + 512 * (n + 1))
                        b.wt(nc.vector, ("t", bc_dep[0]))
                        if xc is None:
                            nc.vector.tensor_mul(out=tfa, in0=xg[c][:, hsl],
                                                 in1=prbB[:, 0:512])
                        else:
                            nc.vector.tensor_mul(out=tfa[:, 0:1], in0=xc,
                                                 in1=prbB[:, 0:1])
                            nc.vector.tensor_mul(
                                out=tfa[:, 1:512],
                                in0=xg[c][:, hsl.start + 1:hsl.stop],
                                in1=prbB[:, 1:512])
                        nc.vector.tensor_sub(out=tfa, in0=tfa,
                                             in1=pmbB[:, 0:512])
                        tick = b.inc("v", nc.vector.tensor_scalar(
                            out=h2n[c][:, nsl], in0=tfa,
                            scalar1=lngc[:, c:c + 1],
                            scalar2=lnbc[:, c:c + 1],
                            op0=ALU.mult, op1=ALU.add))
                        ap_done[0] = tick
                        h2n_rdy[n] = tick

                    apply_begin(0)
                    for c in range(NCH):
                        apply_c(0, c)

                    for nb in range(4):
                        nsl = slice(512 * nb, 512 * (nb + 1))
                        for j in range(NJ):
                            jb = j % 2
                            tix = 0 if j < 8 else (1 if j < 16 else 2)
                            tbuf = gthird % 2
                            j0 = thirds[tix][0]
                            if j in (0, 8, 16):
                                if not (nb == 3 and j == 16):
                                    nbuf = (gthird + 1) % 2
                                    w1gd[nbuf] = load_third(
                                        (tix + 1) % 3, nbuf, w1_used[nbuf])
                            pa = pags[jb][:, 0:512]
                            pg = pags[jb][:, 512:1024]
                            b.wt(nc.tensor, ("gd", w1gd[tbuf]),
                                 ("a", gel_dep[jb]), ("v", h4_dep[jb]),
                                 ("v", h2n_rdy[nb] if j == 0 else 0))
                            jcol = 256 * (j - j0)
                            i = None
                            for k in range(NCH):
                                base = 2048 * k + jcol
                                nc.tensor.matmul(
                                    pa, w1t[tbuf][:, base:base + 128],
                                    h2n[k][:, nsl], start=(k == 0),
                                    stop=(k == 7))
                            for k in range(NCH):
                                base = 2048 * k + jcol + 128
                                i = nc.tensor.matmul(
                                    pg, w1t[tbuf][:, base:base + 128],
                                    h2n[k][:, nsl], start=(k == 0),
                                    stop=(k == 7))
                            td = b.inc("t", i)
                            w1_used[tbuf] = td
                            b.wt(nc.scalar, ("t", td), ("v", h4_dep[jb]))
                            gel_dep[jb] = b.inc("a", nc.scalar.activation(
                                out=gel[jb][:], in_=pg, func=AF.Gelu,
                                bias=b1c[:, 2 * j + 1:2 * j + 2]))
                            b.wt(nc.vector, ("a", gel_dep[jb]),
                                 ("t", w2_done))
                            h4_dep[jb] = b.inc(
                                "v", nc.vector.scalar_tensor_tensor(
                                    out=h4all[:, 512 * j:512 * (j + 1)],
                                    in0=pa, scalar=b1c[:, 2 * j:2 * j + 1],
                                    in1=gel[jb][:], op0=ALU.add, op1=ALU.mult))
                            if j in (7, 15, 21):
                                gthird += 1
                        if nb + 1 < 4:
                            for c in range(4, NCH):
                                nc.vector.tensor_copy(
                                    out=xcol[:, c - 4:c - 3],
                                    in_=xg[c][:, 512 * (nb + 1):
                                              512 * (nb + 1) + 1])
                            apply_begin(nb + 1)
                            for c in range(NCH):
                                apply_c(nb + 1, c,
                                        xcol[:, c - 4:c - 3] if c >= 4
                                        else None)
                        last_w2 = 0
                        for m in range(NCH):
                            mb = m % 2
                            m_, nb_ = (m + 1, nb) if m + 1 < NCH else (0, nb + 1)
                            if nb_ < 4:
                                gd_w2[(m + 1) % 2] = b.gload(
                                    w2c[(m + 1) % 2][:],
                                    w2d[128 * m_:128 * (m_ + 1), :],
                                    ("t", w2_used[(m + 1) % 2]))
                            b.wt(nc.tensor, ("gd", gd_w2[mb]),
                                 ("v", h4_dep[0]), ("v", h4_dep[1]),
                                 ("v", res_dep))
                            i = None
                            for j in range(NJ):
                                i = nc.tensor.matmul(
                                    pW2s[mb][:],
                                    w2c[mb][:, 128 * j:128 * (j + 1)],
                                    h4all[:, 512 * j:512 * (j + 1)],
                                    start=(j == 0), stop=(j == NJ - 1))
                            td = b.inc("t", i)
                            w2_used[mb] = td
                            last_w2 = td
                            xsl = slice(1 + 512 * nb, 1 + 512 * (nb + 1))
                            b.wt(nc.vector, ("t", td))
                            res_dep = b.inc(
                                "v", nc.vector.scalar_tensor_tensor(
                                    out=xg[m][:, xsl], in0=pW2s[mb][:],
                                    scalar=b2c[:, m:m + 1], in1=xg[m][:, xsl],
                                    op0=ALU.add, op1=ALU.add))
                        w2_done = last_w2
                    b.barrier()
                b.ckpt(f"ffn{l}")
        if l == 0 and "g0" in b.dumps:
            for c in range(NCH):
                b.sload(b.dumps["g0"][128 * c:128 * (c + 1), :], xg[c][:, 1:SQ])
            b.barrier()
    if "gpre" in b.dumps:
        for c in range(NCH):
            b.sload(b.dumps["gpre"][128 * c:128 * (c + 1), :], xg[c][:, 1:SQ])
        b.barrier()

    with contextlib.ExitStack() as sc:
        xbs = [sc.enter_context(b.sb(f"xbs{c}", [128, S], BF16))
               for c in range(NCH)]
        oun_s = [sc.enter_context(b.sb(f"osn{i}", [128, S], BF16))
                 for i in range(NCH)]
        ssums = sc.enter_context(b.sb("ssums", [16, S], F32))
        srow_s = sc.enter_context(b.sb("srow_s", [1, 1024], F32))
        srb_bf = sc.enter_context(b.sb("srb", [16, S], BF16))
        selg16_t = sc.enter_context(b.sb("selg16_t", [16, 1024], BF16))
        selg16_sd = b.sload(selg16_t[:], b.dr["selg16_bf"])
        cbias_s = sc.enter_context(b.sb("cbias_s", [128, 16], F32))

        for c in range(NCH):
            b.wt(nc.scalar, b.now("t"), b.now("v"))
            b.inc("a", nc.scalar.activation(out=xg[c][:, 1:SQ],
                                            in_=xg[c][:, 1:SQ], func=AF.Silu))

        with contextlib.ExitStack() as lsc:
            tmpf = [lsc.enter_context(b.sb(f"tmpf{i}", [128, S], F32))
                    for i in range(3)]
            mrow = lsc.enter_context(b.sb("mrowS", [1, S], F32))
            rrow = lsc.enter_context(b.sb("rrowS", [1, S], F32))
            with (b.ps("pmbs", [128, S], F32) as pmb,
                  b.ps("prbs", [128, S], F32) as prb):
                tdep = b.ln_bcast(pmb, prb, mrow, rrow, b.scr0_sd,
                                  src=(b.lnmr0, b.lnrs0))
                dep_v = [0, 0, 0]
                for c in range(NCH):
                    buf = tmpf[c % 3]
                    if c % 3 != 1:
                        dd = ("sd", b.sload(buf[:], xd[128 * c:128 * (c + 1), :],
                                            ("v", dep_v[c % 3])))
                    else:
                        dd = ("gd", b.gload(buf[:], xd[128 * c:128 * (c + 1), :],
                                            ("v", dep_v[c % 3])))
                    b.wt(nc.vector, dd, ("t", tdep))
                    nc.vector.tensor_mul(out=buf[:], in0=buf[:], in1=prb[:])
                    dep_v[c % 3] = b.inc("v", nc.vector.tensor_sub(
                        out=xbs[c][:], in0=buf[:], in1=pmb[:]))
            b.barrier()
        b.ckpt("sln")

        with contextlib.ExitStack() as asc:
            qh2 = [asc.enter_context(b.sb(f"qh{i}", [128, S], BF16))
                   for i in range(2)]
            kh2 = [asc.enter_context(b.sb(f"kh{i}", [128, S], BF16))
                   for i in range(2)]
            vo4 = [asc.enter_context(b.sb(f"vos{i}", [128, 16 * 65], BF16))
                   for i in range(4)]
            wqs = [asc.enter_context(b.sb(f"wqss{i}", [128, D], BF16))
                   for i in range(2)]
            wks = [asc.enter_context(b.sb(f"wkss{i}", [128, D], BF16))
                   for i in range(2)]
            wvs = [asc.enter_context(b.sb(f"wvss{i}", [128, D], BF16))
                   for i in range(2)]
            nmax = asc.enter_context(b.sb("nmax", [2, 8], F32))
            t21 = asc.enter_context(b.sb("t21", [2, 8], F32))
            vbts = asc.enter_context(b.sb("vbts", [128, 512], BF16))
            tbs2 = [asc.enter_context(b.sb(f"tbs2{i}", [128, 512], BF16))
                    for i in range(2)]
            pexp = [asc.enter_context(b.sb(
                f"pexps{i}", [128, 512], BF16)) for i in range(4)]

            for vv in vo4:
                oc = vv.ap().rearrange("p (u c) -> p u c", c=65)[:, :, 64:65]
                b.inc("gc", nc.gpsimd.memset(oc, 1.0))
            b.barrier()

            with (b.ps("pPs0", [128, 512], F32) as pP0,
                  b.ps("pPs1", [128, 512], F32) as pP1,
                  b.ps("pqka", [128, 512], F32) as pqka,
                  b.ps("pqkb", [128, 512], F32) as pqkb,
                  b.ps("pTs0", [128, 128], BF16) as pT0,
                  b.ps("pTs1", [128, 128], BF16) as pT1,
                  b.ps("spva", [65, 512], F32) as ppva,
                  b.ps("spvb", [65, 512], F32) as ppvb):
                pPs = [pP0, pP1]
                ast = {"expA": 0, "expB": 0, "evacA": 0, "evacB": 0,
                       "evacV": 0, "pv": 0, "pv2": [0, 0], "ep": 0}
                evac_dep = [0, 0]
                tr_dep = [0, 0]
                nrm_dep2 = [0, 0]
                vo_guard = [0, 0]
                nmax_rd = [0, 0]
                piL = [0]
                npend = [None]
                entry = {}

                def emit_norm(gcd_):
                    (pPb, pib, col, sq_) = npend[0]
                    npend[0] = None
                    b.wt(nc.tensor, ("a", sq_))
                    td = b.inc("t", nc.tensor.matmul(
                        pPb[0:2, 0:512], ct["sel2"][:], tbs2[pib][:],
                        start=True, stop=True))
                    b.wt(nc.vector, ("t", td), ("gc", gcd_))
                    nc.vector.tensor_reduce(
                        out=t21[0:2, col:col + 1], in_=pPb[0:2, 0:512],
                        axis=AX.X, op=ALU.max)
                    return b.inc("v", nc.vector.tensor_max(
                        out=nmax[0:2, col:col + 1],
                        in0=nmax[0:2, col:col + 1],
                        in1=t21[0:2, col:col + 1]))

                wall_free = [0, 0]
                swgd = [0, 0]

                def load_sw(hp):
                    wb = hp % 2
                    gd = 0
                    for wtile, wname in ((wqs[wb], "wq_t"), (wks[wb], "wk_t"),
                                         (wvs[wb], "wv_t")):
                        wsrc = b.dr[wname].rearrange("(k p) c -> p k c", p=128)
                        gd = b.gload(
                            wtile.ap().rearrange("p (k c) -> p k c", c=128),
                            wsrc[:, :, 128 * hp:128 * (hp + 1)],
                            ("t", wall_free[wb]) if wtile is wqs[wb]
                            else ("t", 0))
                    return gd

                def s_proj_gen(hp):
                    par = hp % 2
                    qh = qh2[par]
                    kh = kh2[par]
                    vo_a, vo_b = vo4[2 * par], vo4[2 * par + 1]
                    if hp + 1 < 8:
                        swgd[(hp + 1) % 2] = load_sw(hp + 1)
                    gddep = ("gd", swgd[hp % 2])
                    b.wt(nc.gpsimd, ("a", nmax_rd[par]), b.now("v"))
                    gcd = b.inc("gc", nc.gpsimd.memset(
                        nmax[:, 4 * par:4 * par + 2], 0.0))
                    proj_evac = 0
                    for tsel in range(3):
                        wtile = (wqs[par], wks[par], wvs[par])[tsel]
                        dsts = (qh, kh, None)[tsel]
                        bias_col = (ct["bq"], ct["bk"],
                                    ct["bv"])[tsel][:, hp:hp + 1]
                        for n in range(NB4):
                            pi = piL[0]
                            pP = pPs[pi % 2]
                            b.wt(nc.tensor, gddep, ("a", evac_dep[pi % 2]),
                                 ("v", nrm_dep2[pi % 2]))
                            i = None
                            for k in range(NCH):
                                sl = slice(512 * n, 512 * (n + 1))
                                i = nc.tensor.matmul(
                                    pP[:], wtile[:, 128 * k:128 * (k + 1)],
                                    xbs[k][:, sl], start=(k == 0),
                                    stop=(k == 7))
                                if k == 3:
                                    yield
                            mmdep = b.inc("t", i)
                            if npend[0] is not None:
                                nrm_dep2[(pi + 1) % 2] = emit_norm(gcd)
                            if tsel < 2:
                                nsl = slice(512 * n, 512 * (n + 1))
                                b.wt(nc.scalar, ("t", mmdep))
                                nc.scalar.activation(out=dsts[:, nsl],
                                                     in_=pP[:],
                                                     func=AF.Identity,
                                                     bias=bias_col)
                                evac_dep[pi % 2] = b.inc(
                                    "a", nc.scalar.activation(
                                        out=tbs2[pi % 2][:],
                                        in_=dsts[:, nsl], func=AF.Square))
                                proj_evac = evac_dep[pi % 2]
                                npend[0] = (pP, pi % 2, 4 * par + tsel,
                                            evac_dep[pi % 2])
                            else:
                                evac_dep[pi % 2] = b.v_block(
                                    pP, n, vbts, vo_a, vo_b, [pT0, pT1],
                                    tr_dep, bias=bias_col,
                                    vo_guard=vo_guard[par])
                            piL[0] += 1
                            yield
                    wall_free[par] = b.cnt["t"]
                    if npend[0] is not None:
                        nrm_dep2[(piL[0] + 1) % 2] = emit_norm(gcd)
                    c0 = 4 * par
                    b.wt(nc.scalar, ("v", nrm_dep2[0]), ("v", nrm_dep2[1]))
                    nc.scalar.activation(out=t21[0:2, c0:c0 + 1],
                                         in_=nmax[0:2, c0:c0 + 1],
                                         func=AF.Sqrt)
                    ad = b.inc("a", nc.scalar.activation(
                        out=t21[0:2, c0 + 1:c0 + 2],
                        in_=nmax[0:2, c0 + 1:c0 + 2], func=AF.Sqrt))
                    nmax_rd[par] = ad
                    b.wt(nc.vector, ("a", ad))
                    nc.vector.tensor_mul(out=t21[0:2, c0 + 2:c0 + 3],
                                         in0=t21[0:2, c0:c0 + 1],
                                         in1=t21[0:2, c0 + 1:c0 + 2])
                    vd = b.inc("v", nc.vector.tensor_scalar_mul(
                        out=t21[0:2, c0 + 2:c0 + 3],
                        in0=t21[0:2, c0 + 2:c0 + 3], scalar1=-0.125))
                    b.wt(nc.tensor, ("v", vd))
                    nc.tensor.matmul(pP0[:, 0:1], ct["sel2f"][:, 0:128],
                                     t21[0:2, c0 + 2:c0 + 3],
                                     start=True, stop=True)
                    td = b.inc("t", nc.tensor.matmul(
                        pP0[:, 1:2], ct["sel2f"][:, 128:256],
                        t21[0:2, c0 + 2:c0 + 3], start=True, stop=True))
                    b.wt(nc.vector, ("t", td))
                    cb_tick = b.inc("v", nc.vector.tensor_copy(
                        out=cbias_s[:, 2 * hp:2 * hp + 2], in_=pP0[:, 0:2]))
                    nrm_dep2[0] = max(nrm_dep2[0], cb_tick)
                    entry[hp] = (("a", proj_evac), ("v", cb_tick))

                att = None
                swgd[0] = load_sw(0)
                for hp in range(8):
                    pg = s_proj_gen(hp)
                    for _ in pg:
                        for _i in range(2):
                            if att is not None:
                                try:
                                    next(att)
                                except StopIteration:
                                    att = None
                    while att is not None:
                        try:
                            next(att)
                        except StopIteration:
                            att = None
                    par = hp % 2
                    att = b.att_pair_gen(
                        "s", qh2[par], kh2[par], vo4[2 * par], vo4[2 * par + 1],
                        cbias_s[:, 2 * hp:2 * hp + 1],
                        cbias_s[:, 2 * hp + 1:2 * hp + 2],
                        0.125, oun_s[hp], ssums, 2 * hp, srow_s,
                        pqka, pqkb, ppva, ppvb,
                        [pexp[0], pexp[1]], [pexp[2], pexp[3]], ast,
                        entry_deps=entry[hp])
                    vo_guard[par] = 0
                while att is not None:
                    try:
                        next(att)
                    except StopIteration:
                        att = None
                b.barrier()

        with contextlib.ExitStack() as fsc:
            wos = fsc.enter_context(b.sb("wos", [128, 8192], BF16))
            ostg = [fsc.enter_context(b.sb(
                f"ostg{i}", [128, 512], F32)) for i in range(2)]
            tmpf2 = [fsc.enter_context(b.sb(
                f"tmf2{i}", [128, S], F32)) for i in range(2)]
            gdw = 0
            for k in range(NCH):
                gdw = b.gload(wos[:, 1024 * k:1024 * (k + 1)],
                              b.dr["wo_t"][128 * k:128 * (k + 1), :])
            sdm = [0, 0]
            sdm[0] = b.sload(tmpf2[0][:], xd[0:128, :])
            sdm[1] = b.sload(tmpf2[1][:], xd[128:256, :])

            nc.scalar.activation(out=ssums[:], in_=ssums[:], func=AF.Ln)
            ad = b.inc("a", nc.scalar.activation(out=srb_bf[:], in_=ssums[:],
                                                 func=AF.Exp, scale=-1.0))
            with (b.ps("prns0", [128, 512], F32) as prn0,
                  b.ps("prns1", [128, 512], F32) as prn1):
                prns = [prn0, prn1]
                od = [0, 0]
                oi = 0
                for hp in range(8):
                    for n in range(NB4):
                        nsl = slice(512 * n, 512 * (n + 1))
                        prn = prns[oi % 2]
                        b.wt(nc.tensor, ("a", ad), ("v", od[oi % 2]),
                             ("sd", selg16_sd))
                        td = b.inc("t", nc.tensor.matmul(
                            prn[:], selg16_t[:, 128 * hp:128 * (hp + 1)],
                            srb_bf[0:16, nsl], start=True, stop=True))
                        b.wt(nc.vector, ("t", td))
                        od[oi % 2] = b.inc("v", nc.vector.tensor_mul(
                            out=oun_s[hp][:, nsl], in0=oun_s[hp][:, nsl],
                            in1=prn[:]))
                        oi += 1
                b.barrier()

            boc = ct["bo"]
            with (b.ps("pS0f", [128, 512], F32) as pS0,
                  b.ps("pS1f", [128, 512], F32) as pS1):
                pSs = [pS0, pS1]
                si = 0
                vdep_s = [0, 0]
                st_dep = [0, 0]
                for m in range(NCH):
                    sd = sdm[m % 2]
                    for n in range(NB4):
                        pS = pSs[si % 2]
                        b.wt(nc.tensor, ("gd", gdw), ("v", vdep_s[si % 2]))
                        i = None
                        for k in range(NCH):
                            i = nc.tensor.matmul(
                                pS[:], wos[:, 1024 * k + 128 * m:
                                            1024 * k + 128 * (m + 1)],
                                oun_s[k][:, 512 * n:512 * (n + 1)],
                                start=(k == 0), stop=(k == 7))
                        td = b.inc("t", i)
                        nsl = slice(512 * n, 512 * (n + 1))
                        xsl = slice(1 + 512 * n, 1 + 512 * (n + 1))
                        ob = ostg[si % 2]
                        b.wt(nc.vector, ("t", td), ("sd", sd),
                             ("sd", st_dep[si % 2]))
                        nc.vector.scalar_tensor_tensor(
                            out=ob[:], in0=pS[:], scalar=boc[:, m:m + 1],
                            in1=xg[m][:, xsl], op0=ALU.add, op1=ALU.mult)
                        vdep_s[si % 2] = b.inc("v", nc.vector.tensor_add(
                            out=ob[:], in0=ob[:], in1=tmpf2[m % 2][:, nsl]))
                        st_dep[si % 2] = b.sload(
                            b.out_d[128 * m:128 * (m + 1), nsl], ob[:],
                            ("v", vdep_s[si % 2]))
                        si += 1
                    if m + 2 < NCH:
                        sdm[m % 2] = b.sload(
                            tmpf2[m % 2][:],
                            xd[128 * (m + 2):128 * (m + 3), :],
                            ("v", vdep_s[0]), ("v", vdep_s[1]))
                b.barrier()


def kernel(**inputs):
    x = np.asarray(inputs["L0_attn_x"], np.float32)
    shared = _prep(inputs)
    nc = _build()
    in_maps = []
    for c in range(8):
        m = dict(shared)
        m["x"] = np.ascontiguousarray(x[c].T)
        in_maps.append(m)
    res = run_bass_kernel_spmd(nc, in_maps, core_ids=list(range(8)))
    out = np.stack([np.asarray(r["out"], np.float32).T for r in res.results], 0)
    return np.ascontiguousarray(out)


# version 18
# speedup vs baseline: 1.1572x; 1.0362x over previous
import contextlib
import math

import numpy as np
import ml_dtypes

import concourse.bass as bass
import concourse.mybir as mybir
from concourse.bass_utils import run_bass_kernel_spmd

F32 = mybir.dt.float32
BF16 = mybir.dt.bfloat16
AF = mybir.ActivationFunctionType
ALU = mybir.AluOpType
AX = mybir.AxisListType

D = 1024
S = 2048
SQ = S + 1
NCH = 8
NB4 = 4
HID = 2730
HIDP = 2816
NJ = HIDP // 128
LN_EPS = 1e-5
LOG8 = math.log(8.0)

_DUMPS = ()
_STOP_AFTER = None
_SPLIT_EXP = False


class _StopEmission(Exception):
    pass


def _bcol(v):
    return np.ascontiguousarray(v.reshape(8, 128).T)


def _consts():
    c = {}
    bf = ml_dtypes.bfloat16
    c["lnones"] = np.full((128, 1), 1.0 / 1024.0, dtype=bf)
    sel2 = np.zeros((128, 2), np.float32)
    sel2[0:64, 0] = 1.0
    sel2[64:128, 1] = 1.0
    c["sel2"] = sel2.astype(bf)
    R = np.zeros((128, 128), np.float32)
    for h in range(2):
        for m in range(32):
            R[h * 64 + m + 32, h * 64 + m] = -1.0
            R[h * 64 + m, h * 64 + m + 32] = 1.0
    c["rmat"] = R.astype(bf)
    c["ident"] = np.eye(128, dtype=np.float32).astype(bf)
    u = np.arange(128)[:, None]
    q = np.arange(128)[None, :]
    c["mtriu"] = (q >= u).astype(np.float32).astype(bf)
    c["mtril"] = (u >= q).astype(np.float32).astype(bf)
    c["onesf"] = np.ones((1, 128), np.float32)
    selg = np.zeros((8, 512), np.float32)
    for hp in range(4):
        selg[2 * hp, 128 * hp:128 * hp + 64] = 1.0
        selg[2 * hp + 1, 128 * hp + 64:128 * hp + 128] = 1.0
    c["selg_bf"] = selg.astype(bf)
    selg16 = np.zeros((16, 1024), np.float32)
    for hp in range(8):
        selg16[2 * hp, 128 * hp:128 * hp + 64] = 1.0
        selg16[2 * hp + 1, 128 * hp + 64:128 * hp + 128] = 1.0
    c["selg16_bf"] = selg16.astype(bf)
    sq_ = np.zeros((16, 512), np.float32)
    sk_ = np.zeros((16, 512), np.float32)
    for hp in range(4):
        sq_[2 * hp, 128 * hp:128 * hp + 64] = 1.0
        sq_[2 * hp + 1, 128 * hp + 64:128 * hp + 128] = 1.0
        sk_[8 + 2 * hp, 128 * hp:128 * hp + 64] = 1.0
        sk_[8 + 2 * hp + 1, 128 * hp + 64:128 * hp + 128] = 1.0
    c["selgQ"] = sq_.astype(bf)
    c["selgK"] = sk_.astype(bf)
    s16q = np.zeros((128, 64), np.float32)
    s16k = np.zeros((128, 64), np.float32)
    for hp in range(4):
        s16q[0:64, 16 * hp + 2 * hp] = 1.0
        s16q[64:128, 16 * hp + 2 * hp + 1] = 1.0
        s16k[0:64, 16 * hp + 8 + 2 * hp] = 1.0
        s16k[64:128, 16 * hp + 8 + 2 * hp + 1] = 1.0
    c["sel16q"] = s16q.astype(bf)
    c["sel16k"] = s16k.astype(bf)
    sC = np.zeros((16, 1024), np.float32)
    for h in range(8):
        sC[h, 128 * h:128 * (h + 1)] = 0.5
        sC[h + 8, 128 * h:128 * (h + 1)] = 0.5
    c["selC"] = sC.astype(bf)
    s2f = np.zeros((2, 256), np.float32)
    s2f[0, 0:128] = 1.0
    s2f[1, 128:256] = 1.0
    c["sel2f"] = s2f
    dh = 64
    half = np.arange(0, dh, 2, dtype=np.float64)
    inv_freq = 1.0 / (10000.0 ** (half / dh))
    t = np.arange(S, dtype=np.float64)
    freqs = np.concatenate([t[:, None] * inv_freq[None, :]] * 2, axis=1)
    base = (half + 0.4 * dh) / (1.4 * dh)
    power = (t - S // 2) / (512 // 2)
    scale = np.concatenate([base[None, :] ** power[:, None]] * 2, axis=1)
    cos, sin = np.cos(freqs), np.sin(freqs)
    dup = lambda a: np.ascontiguousarray(
        np.concatenate([a.T, a.T], axis=0).astype(np.float32))
    c["tcs"] = dup(cos * scale)
    c["tss"] = dup(sin * scale)
    c["tcis"] = dup(cos / scale)
    c["tsis"] = dup(sin / scale)
    return c


def _prep(inp):
    d = {}
    f32 = np.float32
    T = lambda a: np.ascontiguousarray(np.asarray(a, f32).T)
    A = lambda a: np.ascontiguousarray(np.asarray(a, f32))
    d["wq_t"] = T(inp["L0_attn_Wq"])
    d["wk_t"] = T(inp["L0_attn_Wk"])
    d["wv_t"] = T(inp["L0_attn_Wv"])
    d["wo_t"] = T(inp["L0_attn_Wo"])
    d["bq"] = _bcol(A(inp["L0_attn_bq"]))
    d["bk"] = _bcol(A(inp["L0_attn_bk"]))
    d["bv"] = _bcol(A(inp["L0_attn_bv"]))
    d["bo"] = _bcol(A(inp["L0_attn_bo"]))
    for l in range(2):
        d[f"wqkv{l}"] = A(inp["L0_ffn_wqkv"][l])
        d[f"qs{l}"] = np.ascontiguousarray(
            np.tile(A(inp["L0_ffn_q_scale"][l]), 2)[:, None])
        d[f"ks{l}"] = np.ascontiguousarray(
            np.tile(A(inp["L0_ffn_k_scale"][l]), 2)[:, None])
        d[f"gwo{l}"] = A(inp["L0_ffn_wo"][l])
        d[f"lng{l}"] = _bcol(A(inp["L0_ffn_ln_g"][l]))
        d[f"lnb{l}"] = _bcol(A(inp["L0_ffn_ln_b"][l]))
        w1 = A(inp["L0_ffn_w1"][l])
        w1a = np.concatenate([w1[:, :HID], np.zeros((D, HIDP - HID), f32)], 1)
        w1g = np.concatenate([w1[:, HID:], np.zeros((D, HIDP - HID), f32)], 1)
        w1pp = np.empty((D, 2 * HIDP), f32)
        for j in range(NJ):
            w1pp[:, 256 * j:256 * j + 128] = w1a[:, 128 * j:128 * (j + 1)]
            w1pp[:, 256 * j + 128:256 * j + 256] = w1g[:, 128 * j:128 * (j + 1)]
        d[f"w1pp{l}"] = w1pp
        b1 = A(inp["L0_ffn_b1"][l])
        b1a = np.concatenate([b1[:HID], np.zeros(HIDP - HID, f32)])
        b1g = np.concatenate([b1[HID:], np.zeros(HIDP - HID, f32)])
        b1pp = np.empty((128, 2 * NJ), f32)
        for j in range(NJ):
            b1pp[:, 2 * j] = b1a[128 * j:128 * (j + 1)]
            b1pp[:, 2 * j + 1] = b1g[128 * j:128 * (j + 1)]
        d[f"b1pp{l}"] = b1pp
        w2 = A(inp["L0_ffn_w2"][l])
        w2p = np.concatenate([w2, np.zeros((HIDP - HID, D), f32)], 0)
        d[f"w2pt{l}"] = np.ascontiguousarray(
            w2p.reshape(NJ, 128, 8, 128).transpose(2, 1, 0, 3)
            .reshape(D, HIDP))
        d[f"b2{l}"] = _bcol(A(inp["L0_ffn_b2"][l]))
    d.update(_consts())
    return d


_SPEC_F32 = [
    ("wq_t", [D, D]), ("wk_t", [D, D]), ("wv_t", [D, D]), ("wo_t", [D, D]),
    ("bq", [128, 8]), ("bk", [128, 8]), ("bv", [128, 8]), ("bo", [128, 8]),
    ("onesf", [1, 128]), ("sel2f", [2, 256]),
    ("tcs", [128, S]), ("tss", [128, S]), ("tcis", [128, S]), ("tsis", [128, S]),
]
for _l in range(2):
    _SPEC_F32 += [
        (f"wqkv{_l}", [D, 1536]), (f"qs{_l}", [128, 1]), (f"ks{_l}", [128, 1]),
        (f"gwo{_l}", [512, D]), (f"lng{_l}", [128, 8]), (f"lnb{_l}", [128, 8]),
        (f"w1pp{_l}", [D, 2 * HIDP]), (f"b1pp{_l}", [128, 2 * NJ]),
        (f"w2pt{_l}", [D, HIDP]), (f"b2{_l}", [128, 8]),
    ]
_SPEC_BF = [
    ("lnones", [128, 1]), ("sel2", [128, 2]), ("rmat", [128, 128]),
    ("ident", [128, 128]), ("mtriu", [128, 128]), ("mtril", [128, 128]),
    ("selg_bf", [8, 512]), ("selg16_bf", [16, 1024]),
    ("selgQ", [16, 512]), ("selgK", [16, 512]), ("selC", [16, 1024]),
    ("sel16q", [128, 64]), ("sel16k", [128, 64]),
]
_RESIDENT = ([n for n, _ in _SPEC_BF
              if n not in ("selgQ", "selgK", "selC", "selg16_bf")]
             + ["bq", "bk", "bv", "bo", "onesf", "sel2f"]
             + [f"{n}{l}" for l in range(2)
                for n in ("qs", "ks", "lng", "lnb", "b1pp", "b2")])


def _att_iter(branch):
    out = []
    for qq in range(4):
        lo, hi = 4 * qq, 4 * qq + 3
        urange = range(0, hi + 1) if branch == "s" else range(max(0, lo - 4), hi + 1)
        for U in urange:
            Qlo = max(U, lo)
            Qhi = hi if branch == "s" else min(U + 4, hi)
            if Qlo > Qhi:
                continue
            tiles = []
            for Q in range(Qlo, Qhi + 1):
                m = None
                if Q == U:
                    m = "triu"
                elif branch == "g" and Q == U + 4:
                    m = "tril"
                tiles.append((Q, m))
            out.append((qq, U, Qlo, tiles))
    return out


class _B:
    def __init__(self):
        self.nc = bass.Bass("TRN2", target_bir_lowering=False,
                            detect_race_conditions=False)
        self.st = contextlib.ExitStack()
        nc = self.nc
        self.xd = nc.dram_tensor("x", [D, S], F32, kind="ExternalInput").ap()
        self.out_d = nc.dram_tensor("out", [D, S], F32, kind="ExternalOutput").ap()
        self.lnmr = nc.dram_tensor("lnmr_scr", [1, S], F32).ap()
        self.lnrs = nc.dram_tensor("lnrs_scr", [1, S], F32).ap()
        self.lnmr0 = nc.dram_tensor("lnmr0_scr", [1, S], F32).ap()
        self.lnrs0 = nc.dram_tensor("lnrs0_scr", [1, S], F32).ap()
        self.scr0_sd = 0
        self.dr = {}
        for name, shp in _SPEC_F32:
            self.dr[name] = nc.dram_tensor(name, shp, F32, kind="ExternalInput").ap()
        for name, shp in _SPEC_BF:
            self.dr[name] = nc.dram_tensor(name, shp, BF16, kind="ExternalInput").ap()
        self.dumps = {}
        for name in _DUMPS:
            self.dumps[name] = nc.dram_tensor(
                "dump_" + name, [D, S], F32, kind="ExternalOutput").ap()

        self.sem = {k: self.st.enter_context(nc.semaphore("sem_" + k))
                    for k in ("t", "v", "a", "gc", "gd", "sd")}
        self.cnt = {k: 0 for k in self.sem}
        self.srow_sd = 0

        self.xg = [self.st.enter_context(nc.sbuf_tensor(f"xg{c}", [128, SQ], F32))
                   for c in range(NCH)]
        self.ct = {}
        shp_f32 = dict(_SPEC_F32)
        shp_bf = dict(_SPEC_BF)
        for name in _RESIDENT:
            if name in shp_bf:
                self.ct[name] = self.st.enter_context(
                    nc.sbuf_tensor("c_" + name, shp_bf[name], BF16))
            else:
                self.ct[name] = self.st.enter_context(
                    nc.sbuf_tensor("c_" + name, shp_f32[name], F32))

    _uid = 0

    def sb(self, name, shape, dtype):
        _B._uid += 1
        return self.nc.sbuf_tensor(f"{name}_{_B._uid}", shape, dtype)

    def ps(self, name, shape, dtype):
        _B._uid += 1
        return self.nc.psum_tensor(f"{name}_{_B._uid}", shape, dtype)

    def inc(self, key, instr, n=1):
        instr.then_inc(self.sem[key], n)
        self.cnt[key] += n
        return self.cnt[key]

    def wt(self, eng, *deps):
        for dkey, val in deps:
            if val <= 0:
                continue
            mult = 16 if dkey in ("gd", "sd") else 1
            eng.wait_ge(self.sem[dkey], mult * val)

    def now(self, key):
        return (key, self.cnt[key])

    def gload(self, out_ap, in_ap, *deps):
        self.wt(self.nc.gpsimd, *deps)
        self.nc.gpsimd.dma_start(out=out_ap, in_=in_ap).then_inc(self.sem["gd"], 16)
        self.cnt["gd"] += 1
        return self.cnt["gd"]

    def sload(self, out_ap, in_ap, *deps):
        self.wt(self.nc.sync, *deps)
        self.nc.sync.dma_start(out=out_ap, in_=in_ap).then_inc(self.sem["sd"], 16)
        self.cnt["sd"] += 1
        return self.cnt["sd"]

    def ckpt(self, name):
        self.barrier()
        if _STOP_AFTER == name:
            raise _StopEmission()

    def barrier(self):
        snap = [(k, self.cnt[k]) for k in self.sem]
        for eng in (self.nc.tensor, self.nc.vector, self.nc.scalar,
                    self.nc.gpsimd, self.nc.sync):
            self.wt(eng, *snap)

    def ln_finalize(self, pm, pq, mrow, rrow, dst=None):
        nc = self.nc
        dmr, drs = dst if dst is not None else (self.lnmr, self.lnrs)
        self.wt(nc.vector, self.now("t"))
        self.inc("v", nc.vector.tensor_copy(out=mrow[:], in_=pm[:]))
        vd = self.inc("v", nc.vector.tensor_copy(out=rrow[:], in_=pq[:]))
        self.wt(nc.scalar, ("v", vd))
        ad = self.inc("a", nc.scalar.activation(out=pm[:], in_=mrow[:],
                                                func=AF.Square))
        self.wt(nc.vector, ("a", ad))
        nc.vector.tensor_sub(out=rrow[:], in0=rrow[:], in1=pm[:])
        vd = self.inc("v", nc.vector.tensor_scalar_add(out=rrow[:], in0=rrow[:],
                                                       scalar1=LN_EPS))
        self.wt(nc.scalar, ("v", vd))
        nc.scalar.activation(out=rrow[:], in_=rrow[:], func=AF.Ln)
        ad = self.inc("a", nc.scalar.activation(out=rrow[:], in_=rrow[:],
                                                func=AF.Exp, scale=-0.5))
        self.wt(nc.vector, ("a", ad))
        vd = self.inc("v", nc.vector.tensor_mul(out=mrow[:], in0=mrow[:],
                                                in1=rrow[:]))
        self.sload(dmr, mrow[:], ("v", vd))
        return self.sload(drs, rrow[:])

    def ln_bcast(self, pmb, prb, mrow, rrow, scr_dep, src=None):
        nc = self.nc
        smr, srs = src if src is not None else (self.lnmr, self.lnrs)
        sd1 = self.sload(mrow[:], smr, ("sd", scr_dep))
        sd2 = self.sload(rrow[:], srs)
        onesf = self.ct["onesf"]
        self.wt(nc.tensor, ("sd", sd2))
        i = None
        for n in range(NB4):
            sl = slice(512 * n, 512 * (n + 1))
            nc.tensor.matmul(pmb[:, sl], onesf[:], mrow[0:1, sl],
                             start=True, stop=True)
            i = nc.tensor.matmul(prb[:, sl], onesf[:], rrow[0:1, sl],
                                 start=True, stop=True)
        return self.inc("t", i)

    def att_pair_gen(self, branch, qtile, ktile, vo_a, vo_b, bias_a, bias_b,
                     scale, oun_tile, sums_t, row0, srow, pqk_a, pqk_b,
                     ppv_a, ppv_b, pexp_a2, pexp_b2, st, entry_deps=()):
        nc = self.nc
        items = _att_iter(branch)
        nit = len(items)
        pend = None

        def emit_pv(p):
            (qq_, U_, first_, last_, eA, eB, vA, vB, par_) = p
            self.wt(nc.tensor, ("a", eA), ("v", vA),
                    ("a", st["evacA"] if first_ else 0),
                    ("v", st["evacV"] if first_ else 0))
            pvA = self.inc("t", nc.tensor.matmul(
                ppv_a[0:65, 0:512], vo_a[:, 65 * U_:65 * U_ + 65],
                pexp_a2[par_][:, 0:512], start=first_, stop=last_))
            self.wt(nc.tensor, ("a", eB), ("v", vB),
                    ("a", st["evacB"] if first_ else 0))
            pvB = self.inc("t", nc.tensor.matmul(
                ppv_b[0:65, 0:512], vo_b[:, 65 * U_:65 * U_ + 65],
                pexp_b2[par_][:, 0:512], start=first_, stop=last_))
            st["pv2"][par_] = pvB
            st["pv"] = pvB
            if last_:
                self.wt(nc.scalar, ("t", pvB), ("sd", self.srow_sd))
                qsl = slice(512 * qq_, 512 * (qq_ + 1))
                nc.scalar.copy(out=srow[0:1, 0:512], in_=ppv_a[64:65, 0:512])
                ev = self.inc("a", nc.scalar.copy(
                    out=srow[0:1, 512:1024], in_=ppv_b[64:65, 0:512]))
                self.wt(nc.vector, ("t", pvB))
                nc.vector.tensor_copy(out=oun_tile[0:64, qsl],
                                      in_=ppv_a[0:64, 0:512])
                evv = self.inc("v", nc.vector.tensor_copy(
                    out=oun_tile[64:128, qsl], in_=ppv_b[0:64, 0:512]))
                st["evacA"] = ev
                st["evacB"] = ev
                st["evacV"] = evv
                self.srow_sd = self.sload(
                    sums_t[row0:row0 + 2, qsl], srow[0:1, 0:1024], ("a", ev))

        for k in range(nit):
            qq, U, Qlo, tiles = items[k]
            span = 128 * len(tiles)
            off = 128 * (Qlo - 4 * qq)
            if k == 0:
                self.wt(nc.tensor, *entry_deps)
            self.wt(nc.tensor, ("a", st["expA"]))
            nc.tensor.matmul(pqk_a[:, 0:span],
                             ktile[0:64, 128 * U:128 * (U + 1)],
                             qtile[0:64, 128 * Qlo:128 * Qlo + span],
                             start=True, stop=True)
            self.wt(nc.tensor, ("a", st["expB"]))
            qk = self.inc("t", nc.tensor.matmul(
                pqk_b[:, 0:span], ktile[64:128, 128 * U:128 * (U + 1)],
                qtile[64:128, 128 * Qlo:128 * Qlo + span],
                start=True, stop=True))
            if pend is not None:
                emit_pv(pend)
                pend = None
            par = st["ep"] % 2
            st["ep"] += 1
            pexp_a = pexp_a2[par]
            pexp_b = pexp_b2[par]
            pvdep = st["pv2"][par]
            nz = 0
            zlast = None
            if off > 0 or off + span < 512:
                self.wt(nc.vector, ("t", pvdep))
                for pe in (pexp_a, pexp_b):
                    if off > 0:
                        zlast = nc.vector.memset(pe[:, 0:off], 0.0)
                        nz = 1
                    if off + span < 512:
                        zlast = nc.vector.memset(pe[:, off + span:512], 0.0)
                        nz = 1
            ztick = self.inc("v", zlast) if nz else 0
            self.wt(nc.scalar, ("t", max(qk, pvdep)))
            st["expA"] = self.inc("a", nc.scalar.activation(
                out=pexp_a[:, off:off + span], in_=pqk_a[:, 0:span],
                func=AF.Exp, bias=bias_a, scale=scale))
            st["expB"] = self.inc("a", nc.scalar.activation(
                out=pexp_b[:, off:off + span], in_=pqk_b[:, 0:span],
                func=AF.Exp, bias=bias_b, scale=scale))
            nmask = 0
            last = None
            for idx, (Q, m) in enumerate(tiles):
                if m is None:
                    continue
                mt = self.ct["mtriu"] if m == "triu" else self.ct["mtril"]
                o2 = off + 128 * idx
                if nmask == 0:
                    self.wt(nc.vector, ("a", st["expB"]))
                nc.vector.tensor_mul(out=pexp_a[:, o2:o2 + 128],
                                     in0=pexp_a[:, o2:o2 + 128], in1=mt[:])
                last = nc.vector.tensor_mul(out=pexp_b[:, o2:o2 + 128],
                                            in0=pexp_b[:, o2:o2 + 128],
                                            in1=mt[:])
                nmask += 1
            mtick = self.inc("v", last) if nmask else 0
            first = (k == 0 or items[k - 1][0] != qq)
            lastu = (k + 1 >= nit or items[k + 1][0] != qq)
            pend = (qq, U, first, lastu, st["expA"], st["expB"],
                    max(mtick, ztick), max(mtick, ztick), par)
            yield
        emit_pv(pend)

    def v_block(self, pP, n, vbt, vo_a, vo_b, pTs, tr_dep, bias=None,
                vo_guard=0):
        nc = self.nc
        self.wt(nc.scalar, self.now("t"))
        if bias is None:
            ev = self.inc("a", nc.scalar.copy(out=vbt[:], in_=pP[:]))
        else:
            ev = self.inc("a", nc.scalar.activation(out=vbt[:], in_=pP[:],
                                                    func=AF.Identity, bias=bias))
        for j in range(4):
            U = 4 * n + j
            pT = pTs[U % 2]
            self.wt(nc.tensor, ("a", ev), ("v", tr_dep[U % 2]))
            td = self.inc("t", nc.tensor.transpose(
                pT[:], vbt[:, 128 * j:128 * (j + 1)], self.ct["ident"][:]))
            self.wt(nc.vector, ("t", td), ("t", vo_guard))
            nc.vector.tensor_copy(out=vo_a[:, 65 * U:65 * U + 64],
                                  in_=pT[:, 0:64])
            tr_dep[U % 2] = self.inc("v", nc.vector.tensor_copy(
                out=vo_b[:, 65 * U:65 * U + 64], in_=pT[:, 64:128]))
        return ev


def _build():
    b = _B()
    try:
        _build_body(b)
        b.barrier()
        b.st.close()
    except _StopEmission:
        b.barrier()
    return b.nc


def _build_body(b):
    nc = b.nc
    xg = b.xg
    ct = b.ct
    xd = b.xd

    for name in _RESIDENT:
        b.sload(ct[name][:], b.dr[name])
    for c in range(NCH):
        b.wt(nc.gpsimd, b.now("sd"))
        b.inc("gc", nc.gpsimd.memset(xg[c][:, 0:1], 0.0))
    b.barrier()

    for l in range(2):
        with contextlib.ExitStack() as sc:
            xb = [sc.enter_context(b.sb(f"xb{c}", [128, SQ], BF16))
                  for c in range(NCH)]
            vo = [sc.enter_context(b.sb(f"vog{i}", [128, 16 * 65], BF16))
                  for i in range(8)]
            oun_g = [sc.enter_context(b.sb(f"og{i}", [128, S], BF16))
                     for i in range(4)]
            cbias = sc.enter_context(b.sb("cbias", [128, 8], F32))

            for i in range(8):
                oc = vo[i].ap().rearrange("p (u c) -> p u c", c=65)[:, :, 64:65]
                b.inc("gc", nc.gpsimd.memset(oc, 1.0))
            b.barrier()

            if l == 0:
                ln1 = ct["lnones"]
                with contextlib.ExitStack() as lsc:
                    sqb = lsc.enter_context(
                        b.sb("sqb0", [128, SQ], BF16))
                    mrow = lsc.enter_context(b.sb("mrow0", [1, S], F32))
                    rrow = lsc.enter_context(b.sb("rrow0", [1, S], F32))
                    sdl = [None] * NCH
                    for c in range(6):
                        sdl[c] = ("sd", b.sload(xg[c][:, 1:SQ],
                                                xd[128 * c:128 * (c + 1), :]))
                    with (b.ps("pm0", [1, S], F32) as pm,
                          b.ps("pq0", [1, S], F32) as pq):
                        dep_a = 0
                        for c in range(NCH):
                            gd = b.gload(xb[c][:, 1:SQ],
                                         xd[128 * c:128 * (c + 1), :])
                            b.wt(nc.scalar, ("gd", gd), ("a", dep_a),
                                 b.now("t"))
                            dep_a = b.inc("a", nc.scalar.activation(
                                out=sqb[:, 1:SQ], in_=xb[c][:, 1:SQ],
                                func=AF.Square))
                            b.wt(nc.tensor, ("a", dep_a))
                            i = None
                            for n in range(NB4):
                                sl = slice(1 + 512 * n, 1 + 512 * (n + 1))
                                nc.tensor.matmul(
                                    pm[0:1, 512 * n:512 * (n + 1)], ln1[:],
                                    xb[c][:, sl], start=(c == 0), stop=(c == 7))
                            for n in range(NB4):
                                sl = slice(1 + 512 * n, 1 + 512 * (n + 1))
                                i = nc.tensor.matmul(
                                    pq[0:1, 512 * n:512 * (n + 1)], ln1[:],
                                    sqb[:, sl], start=(c == 0), stop=(c == 7))
                            b.inc("t", i)
                        for c in range(6, NCH):
                            sdl[c] = ("gd", b.gload(
                                xg[c][:, 1:SQ], xd[128 * c:128 * (c + 1), :]))
                        scr = b.ln_finalize(pm, pq, mrow, rrow,
                                            dst=(b.lnmr0, b.lnrs0))
                        b.scr0_sd = scr
                    with (b.ps("pmb0", [128, S], F32) as pmb,
                          b.ps("prb0", [128, S], F32) as prb):
                        tdep = b.ln_bcast(pmb, prb, mrow, rrow, scr,
                                          src=(b.lnmr0, b.lnrs0))
                        for c in range(NCH):
                            b.wt(nc.vector, sdl[c], ("t", tdep))
                            nc.vector.tensor_mul(out=xg[c][:, 1:SQ],
                                                 in0=xg[c][:, 1:SQ], in1=prb[:])
                            vmu = b.inc("v", nc.vector.tensor_sub(
                                out=xg[c][:, 1:SQ], in0=xg[c][:, 1:SQ],
                                in1=pmb[:]))
                            b.wt(nc.scalar, ("v", vmu))
                            b.inc("a", nc.scalar.copy(
                                out=xb[c][:, 1:SQ], in_=xg[c][:, 1:SQ]))
                    b.barrier()
                b.ckpt(f"ln{l}")
                if "xn" in b.dumps:
                    for c in range(NCH):
                        b.sload(b.dumps["xn"][128 * c:128 * (c + 1), :],
                                xg[c][:, 1:SQ])
                    b.barrier()
            else:
                for c in range(NCH):
                    if c % 2 == 0:
                        b.inc("v", nc.vector.tensor_copy(
                            out=xb[c][:, 1:SQ], in_=xg[c][:, 1:SQ]))
                    else:
                        b.inc("a", nc.scalar.copy(
                            out=xb[c][:, 1:SQ], in_=xg[c][:, 1:SQ]))
                b.barrier()
            for c in range(NCH):
                b.inc("gc", nc.gpsimd.memset(xb[c][:, 0:1], 0.0))
            b.barrier()

            with contextlib.ExitStack() as isc:
                qb = [isc.enter_context(b.sb(f"qb{i}", [128, S], BF16))
                      for i in range(4)]
                kb = [isc.enter_context(b.sb(f"kb{i}", [128, S], BF16))
                      for i in range(4)]

                rq16 = isc.enter_context(b.sb("rq16", [16, S], BF16))
                with contextlib.ExitStack() as nsc:
                    n2all = nsc.enter_context(
                        b.sb("n2all", [16, S], BF16))
                    numall = nsc.enter_context(
                        b.sb("numall", [16, S], BF16))
                    dmax16 = nsc.enter_context(
                        b.sb("dmax16", [16, 1], BF16))
                    selC_t = nsc.enter_context(
                        b.sb("selC_t", [16, 1024], BF16))
                    selC_sd = b.sload(selC_t[:], b.dr["selC"])

                    with contextlib.ExitStack() as asc:
                        wall = asc.enter_context(
                            b.sb("wall", [128, 8 * D], BF16))
                        vbt = asc.enter_context(
                            b.sb("vbt", [128, 512], BF16))
                        tbg2 = [asc.enter_context(
                            b.sb(f"tbg2{i}", [128, 512], BF16))
                            for i in range(2)]
                        sel16w = asc.enter_context(
                            b.sb("sel16w", [128, 128], BF16))
                        qk2t = asc.enter_context(
                            b.sb("qk2t", [128, 2], F32))

                        qsc, ksc = ct[f"qs{l}"], ct[f"ks{l}"]
                        b.wt(nc.vector, b.now("sd"))
                        nc.vector.tensor_mul(out=qk2t[:, 0:1], in0=qsc[:],
                                             in1=qsc[:])
                        nc.vector.tensor_mul(out=qk2t[:, 1:2], in0=ksc[:],
                                             in1=ksc[:])
                        nc.vector.tensor_scalar(
                            out=sel16w[:, 0:64], in0=ct["sel16q"][:],
                            scalar1=qk2t[:, 0:1], scalar2=None, op0=ALU.mult)
                        b.inc("v", nc.vector.tensor_scalar(
                            out=sel16w[:, 64:128], in0=ct["sel16k"][:],
                            scalar1=qk2t[:, 1:2], scalar2=None, op0=ALU.mult))
                        b.barrier()

                        with (b.ps("pP0", [128, 512], F32) as pP0,
                              b.ps("pP1", [128, 512], F32) as pP1,
                              b.ps("pN16a", [16, 512], F32) as pN16a,
                              b.ps("pN16b", [16, 512], F32) as pN16b,
                              b.ps("pT0", [128, 128], BF16) as pT0,
                              b.ps("pT1", [128, 128], BF16) as pT1):
                            pPs = [pP0, pP1]
                            evac_dep = [0, 0]
                            pn_dep = 0
                            tr_dep = [0, 0]
                            pi = 0
                            wsrc = b.dr[f"wqkv{l}"].rearrange(
                                "(k p) c -> p k c", p=128)
                            wview = wall.ap().rearrange(
                                "p (s k c) -> p (s k) c", s=8, c=128)
                            gd = 0
                            for hp in range(4):
                                gd = b.gload(
                                    wview[:, 8 * hp:8 * (hp + 1), :],
                                    wsrc[:, :, 128 * hp:128 * (hp + 1)])
                                gd = b.gload(
                                    wview[:, 8 * (4 + hp):8 * (5 + hp), :],
                                    wsrc[:, :, 512 + 128 * hp:
                                         512 + 128 * (hp + 1)])
                            gddep = ("gd", gd)
                            b.ckpt("g1aw")
                            npend = [None]
                            pn_dep = [pn_dep]

                            def emit_norm_g(first, last, selA, selB, sq_, tb_):
                                b.wt(nc.tensor, ("a", sq_))
                                if first:
                                    b.wt(nc.tensor, ("a", pn_dep[0]))
                                nc.tensor.matmul(
                                    pN16a[0:16, 0:512], selA, tb_[:],
                                    start=first, stop=last)
                                return b.inc("t", nc.tensor.matmul(
                                    pN16b[0:16, 0:512], selB, tb_[:],
                                    start=first, stop=last))

                            for n in range(NB4):
                                nsl = slice(512 * n, 512 * (n + 1))
                                for tsel in range(2):
                                    dst = (qb, kb)[tsel]
                                    selp = (ct["sel16q"] if tsel == 0
                                            else ct["sel16k"])
                                    woff = 64 * tsel
                                    for hp in range(4):
                                        pP = pPs[pi % 2]
                                        blk = 4 * tsel + hp
                                        b.wt(nc.tensor, gddep,
                                             ("a", evac_dep[pi % 2]))
                                        i = None
                                        for k in range(NCH):
                                            off = 0 if k >= 4 else 1
                                            sl = slice(off + 512 * n,
                                                       off + 512 * (n + 1))
                                            i = nc.tensor.matmul(
                                                pP[:],
                                                wall[:, 1024 * blk + 128 * k:
                                                     1024 * blk
                                                     + 128 * (k + 1)],
                                                xb[k][:, sl], start=(k == 0),
                                                stop=(k == 7))
                                        mmdep = b.inc("t", i)
                                        if npend[0] is not None:
                                            emit_norm_g(*npend[0])
                                            npend[0] = None
                                        b.wt(nc.scalar, ("t", mmdep))
                                        nc.scalar.copy(
                                            out=dst[hp][:, nsl], in_=pP[:])
                                        sq = b.inc("a", nc.scalar.activation(
                                            out=tbg2[pi % 2][:],
                                            in_=dst[hp][:, nsl],
                                            func=AF.Square))
                                        first = (tsel == 0 and hp == 0)
                                        last = (tsel == 1 and hp == 3)
                                        npend[0] = (
                                            first, last,
                                            selp[:, 16 * hp:16 * (hp + 1)],
                                            sel16w[:, woff + 16 * hp:
                                                   woff + 16 * (hp + 1)],
                                            sq, tbg2[pi % 2])
                                        evac_dep[pi % 2] = sq
                                        pi += 1
                                td = emit_norm_g(*npend[0])
                                npend[0] = None
                                b.wt(nc.scalar, ("t", td))
                                nc.scalar.copy(out=n2all[0:16, nsl],
                                               in_=pN16a[0:16, 0:512])
                                pn_dep[0] = b.inc("a", nc.scalar.copy(
                                    out=numall[0:16, nsl],
                                    in_=pN16b[0:16, 0:512]))
                            b.ckpt("g1aq")
                            b.wt(nc.gpsimd, b.now("t"), b.now("a"))
                            gd = 0
                            for hp in range(4):
                                gd = b.gload(
                                    wview[:, 8 * hp:8 * (hp + 1), :],
                                    wsrc[:, :, 1024 + 128 * hp:
                                         1024 + 128 * (hp + 1)])
                            gddep = ("gd", gd)
                            for n in range(NB4):
                                for hp in range(4):
                                    pP = pPs[pi % 2]
                                    b.wt(nc.tensor, gddep,
                                         ("a", evac_dep[pi % 2]))
                                    i = None
                                    for k in range(NCH):
                                        off = 0 if k >= 4 else 1
                                        sl = slice(off + 512 * n,
                                                   off + 512 * (n + 1))
                                        i = nc.tensor.matmul(
                                            pP[:],
                                            wall[:, 1024 * hp + 128 * k:
                                                 1024 * hp + 128 * (k + 1)],
                                            xb[k][:, sl], start=(k == 0),
                                            stop=(k == 7))
                                    b.inc("t", i)
                                    evac_dep[pi % 2] = b.v_block(
                                        pP, n, vbt, vo[2 * hp],
                                        vo[2 * hp + 1], [pT0, pT1], tr_dep)
                                    pi += 1
                            b.barrier()
                    b.ckpt(f"g1a{l}")

                    nc.vector.tensor_scalar_max(out=n2all[:], in0=n2all[:],
                                                scalar1=1e-24)
                    vd = b.inc("v", nc.vector.tensor_scalar_max(
                        out=numall[:], in0=numall[:], scalar1=1e-30))
                    b.wt(nc.scalar, ("v", vd))
                    nc.scalar.activation(out=n2all[:], in_=n2all[:], func=AF.Ln)
                    nc.scalar.activation(out=rq16[:], in_=n2all[:],
                                         func=AF.Exp, scale=-0.5)
                    ad = b.inc("a", nc.scalar.activation(
                        out=numall[:], in_=numall[:], func=AF.Ln))
                    b.wt(nc.vector, ("a", ad))
                    nc.vector.tensor_sub(out=numall[:], in0=numall[:],
                                         in1=n2all[:])
                    vd = b.inc("v", nc.vector.tensor_reduce(
                        out=dmax16[:], in_=numall[0:16, 0:S], axis=AX.X,
                        op=ALU.max))
                    with b.ps("pb1", [128, 8], F32) as pb1:
                        b.wt(nc.tensor, ("v", vd), ("sd", selC_sd))
                        i = None
                        for h in range(8):
                            i = nc.tensor.matmul(
                                pb1[:, h:h + 1],
                                selC_t[:, 128 * h:128 * (h + 1)],
                                dmax16[0:16, 0:1], start=True, stop=True)
                        td = b.inc("t", i)
                        b.wt(nc.vector, ("t", td))
                        vd = b.inc("v", nc.vector.tensor_scalar_add(
                            out=cbias[:], in0=pb1[:], scalar1=LOG8))
                        b.wt(nc.scalar, ("v", vd))
                        ad = b.inc("a", nc.scalar.activation(
                            out=cbias[:], in_=cbias[:], func=AF.Exp))
                        b.wt(nc.vector, ("a", ad))
                        b.inc("v", nc.vector.tensor_scalar_mul(
                            out=cbias[:], in0=cbias[:], scalar1=-1.0))
                    b.barrier()

                if True:
                    with contextlib.ExitStack() as rsc:
                        selg2_t = rsc.enter_context(
                            b.sb("selg2_t", [16, 1024], BF16))
                        b.sload(selg2_t[:, 0:512], b.dr["selgQ"])
                        selg2_sd = b.sload(selg2_t[:, 512:1024],
                                           b.dr["selgK"])
                        tblA = [rsc.enter_context(
                            b.sb(f"tblA{i}", [128, 512], F32))
                            for i in range(2)]
                        tblB = rsc.enter_context(
                            b.sb("tblB", [128, 512], F32))
                        tb512c = rsc.enter_context(
                            b.sb("tb512c", [128, 512], BF16))
                        tf512 = [rsc.enter_context(b.sb(
                            f"tf512{i}", [128, 512], F32)) for i in range(2)]
                        gsums = rsc.enter_context(b.sb("gsums", [8, S], F32))
                        grb_bf = rsc.enter_context(b.sb("grb", [8, S], BF16))
                        srow = rsc.enter_context(b.sb("srow", [1, 1024], F32))
                        pexp = [rsc.enter_context(b.sb(
                            f"pexp{i}", [128, 512], BF16)) for i in range(4)]
                        with (b.ps("prb1", [128, 512], F32) as prb1,
                              b.ps("pR0", [128, 512], F32) as pR0,
                              b.ps("pR1", [128, 512], F32) as pR1,
                              b.ps("pqa0", [128, 512], F32) as pqa0,
                              b.ps("pqa1", [128, 512], F32) as pqa1,
                              b.ps("ppvg", [65, 512], F32) as ppva,
                              b.ps("ppvg2", [65, 512], F32) as ppvb):
                            qsc, ksc = ct[f"qs{l}"], ct[f"ks{l}"]
                            pRs = [pR0, pR1]
                            riL = [0]
                            vdep_r = [0, 0]
                            tbl_free = [0, 0]
                            tbl_sdd = [0, 0]
                            tseq = []
                            for (tA, tB) in (("tcs", "tss"),
                                             ("tcis", "tsis")):
                                for n in range(NB4):
                                    tseq.append(
                                        (tA, tB,
                                         slice(512 * n, 512 * (n + 1))))

                            def load_tbl(j, bb):
                                tA, tB, nsl = tseq[j % 8]
                                b.sload(tblA[bb][:], b.dr[tA][:, nsl],
                                        ("v", tbl_free[bb]))

                            def g1b_gen(hp):
                                for t_i, (dst, scol, selo) in enumerate((
                                        (qb, qsc, 0),
                                        (kb, ksc, 512))):
                                    for n in range(NB4):
                                        nsl = slice(512 * n, 512 * (n + 1))
                                        ri = riL[0]
                                        bb = ri % 2
                                        if ri == 0:
                                            load_tbl(0, 0)
                                        load_tbl(ri + 1, (ri + 1) % 2)
                                        tB = tseq[ri % 8][1]
                                        sdd = b.sload(
                                            tblB[:], b.dr[tB][:, nsl],
                                            ("v", vdep_r[(ri + 1) % 2]))
                                        b.wt(nc.tensor,
                                             ("v", vdep_r[(ri + 1) % 2]),
                                             ("sd", selg2_sd))
                                        td = b.inc("t", nc.tensor.matmul(
                                            prb1[:],
                                            selg2_t[:, selo + 128 * hp:
                                                    selo + 128 * (hp + 1)],
                                            rq16[0:16, nsl], start=True,
                                            stop=True))
                                        b.wt(nc.vector, ("t", td))
                                        vd = b.inc(
                                            "v",
                                            nc.vector.scalar_tensor_tensor(
                                                out=tb512c[:],
                                                in0=dst[hp][:, nsl],
                                                scalar=scol[:], in1=prb1[:],
                                                op0=ALU.mult, op1=ALU.mult))
                                        pR = pRs[ri % 2]
                                        b.wt(nc.tensor, ("v", vd),
                                             ("v", vdep_r[ri % 2]))
                                        td = b.inc("t", nc.tensor.matmul(
                                            pR[:], ct["rmat"][:], tb512c[:],
                                            start=True, stop=True))
                                        b.wt(nc.vector, ("sd", sdd))
                                        nc.vector.tensor_mul(
                                            out=tf512[0][:], in0=tb512c[:],
                                            in1=tblA[bb][:])
                                        b.wt(nc.vector, ("t", td))
                                        nc.vector.tensor_mul(
                                            out=tf512[1][:], in0=pR[:],
                                            in1=tblB[:])
                                        vdep_r[ri % 2] = b.inc(
                                            "v", nc.vector.tensor_add(
                                                out=dst[hp][:, nsl],
                                                in0=tf512[0][:],
                                                in1=tf512[1][:]))
                                        tbl_free[bb] = vdep_r[ri % 2]
                                        riL[0] += 1
                                        yield

                            ast = {"expA": 0, "expB": 0, "evacA": 0,
                                   "evacB": 0, "evacV": 0, "pv": 0,
                                   "pv2": [0, 0], "ep": 0}
                            att = None
                            for hp in range(4):
                                pg = g1b_gen(hp)
                                for _ in pg:
                                    for _i in range(4):
                                        if att is not None:
                                            try:
                                                next(att)
                                            except StopIteration:
                                                att = None
                                while att is not None:
                                    try:
                                        next(att)
                                    except StopIteration:
                                        att = None
                                rdy = ("v", vdep_r[(riL[0] + 1) % 2])
                                rdy2 = ("v", vdep_r[riL[0] % 2])
                                att = b.att_pair_gen(
                                    "g", qb[hp], kb[hp], vo[2 * hp],
                                    vo[2 * hp + 1],
                                    cbias[:, 2 * hp:2 * hp + 1],
                                    cbias[:, 2 * hp + 1:2 * hp + 2],
                                    8.0, oun_g[hp], gsums, 2 * hp, srow,
                                    pqa0, pqa1, ppva, ppvb,
                                    [pexp[0], pexp[1]], [pexp[2], pexp[3]],
                                    ast, entry_deps=(rdy, rdy2))
                            while att is not None:
                                try:
                                    next(att)
                                except StopIteration:
                                    att = None
                            b.barrier()
                            b.ckpt(f"g2{l}")

                        if l == 0 and "att0" in b.dumps:
                            dmp = b.dumps["att0"]
                            for hp in range(4):
                                b.gload(dmp[128 * hp:128 * (hp + 1), :],
                                        oun_g[hp][:])
                            b.sload(dmp[512:520, :], gsums[0:8, :])
                            b.barrier()

                        nc.scalar.activation(out=gsums[:], in_=gsums[:],
                                             func=AF.Ln)
                        ad = b.inc("a", nc.scalar.activation(
                            out=grb_bf[:], in_=gsums[:], func=AF.Exp,
                            scale=-1.0))
                        with (b.ps("prng0", [128, 512], F32) as prn0,
                              b.ps("prng1", [128, 512], F32) as prn1):
                            prns = [prn0, prn1]
                            od = [0, 0]
                            oi = 0
                            for hp in range(4):
                                for n in range(NB4):
                                    nsl = slice(512 * n, 512 * (n + 1))
                                    prn = prns[oi % 2]
                                    b.wt(nc.tensor, ("a", ad),
                                         ("v", od[oi % 2]))
                                    td = b.inc("t", nc.tensor.matmul(
                                        prn[:],
                                        ct["selg_bf"][:, 128 * hp:
                                                      128 * (hp + 1)],
                                        grb_bf[0:8, nsl], start=True,
                                        stop=True))
                                    b.wt(nc.vector, ("t", td))
                                    od[oi % 2] = b.inc(
                                        "v", nc.vector.tensor_mul(
                                            out=oun_g[hp][:, nsl],
                                            in0=oun_g[hp][:, nsl],
                                            in1=prn[:]))
                                    oi += 1
                        b.barrier()

            with contextlib.ExitStack() as wsc:
                wo4 = wsc.enter_context(b.sb("wo4", [128, 4096], BF16))
                gw = b.dr[f"gwo{l}"]
                gdw = 0
                for k in range(4):
                    gdw = b.gload(wo4[:, 1024 * k:1024 * (k + 1)],
                                  gw[128 * k:128 * (k + 1), :])
                with (b.ps("pW0", [128, 512], F32) as pW0,
                      b.ps("pW1", [128, 512], F32) as pW1):
                    pWs = [pW0, pW1]
                    vdep_w = [0, 0]
                    wi = 0
                    for m in range(NCH):
                        for n in range(NB4):
                            pW = pWs[wi % 2]
                            b.wt(nc.tensor, ("gd", gdw), ("v", vdep_w[wi % 2]))
                            i = None
                            for k in range(4):
                                i = nc.tensor.matmul(
                                    pW[:], wo4[:, 1024 * k + 128 * m:
                                                1024 * k + 128 * (m + 1)],
                                    oun_g[k][:, 512 * n:512 * (n + 1)],
                                    start=(k == 0), stop=(k == 3))
                            td = b.inc("t", i)
                            b.wt(nc.vector, ("t", td))
                            xsl = slice(1 + 512 * n, 1 + 512 * (n + 1))
                            nc.vector.tensor_add(out=xg[m][:, xsl],
                                                 in0=xg[m][:, xsl], in1=pW[:])
                            vdep_w[wi % 2] = b.inc(
                                "v", nc.vector.tensor_copy(
                                    out=xb[m][:, xsl], in_=xg[m][:, xsl]))
                            wi += 1
                    b.barrier()
            b.ckpt(f"g3{l}")

            ln1 = ct["lnones"]
            with contextlib.ExitStack() as lsc:
                sqb = [lsc.enter_context(b.sb(f"sqb2{i}", [128, SQ], BF16))
                       for i in range(2)]
                mrow = lsc.enter_context(b.sb("mrow2", [1, S], F32))
                rrow = lsc.enter_context(b.sb("rrow2", [1, S], F32))
                with (b.ps("pm2", [1, S], F32) as pm,
                      b.ps("pq2", [1, S], F32) as pq):
                    mm_dep = [0, 0]
                    for c in range(NCH):
                        bufi = c % 2
                        if bufi == 0:
                            b.wt(nc.scalar, ("t", mm_dep[0]))
                            sq_dep = ("a", b.inc("a", nc.scalar.activation(
                                out=sqb[0][:, 0:SQ], in_=xg[c][:, 0:SQ],
                                func=AF.Square)))
                        else:
                            b.wt(nc.vector, ("t", mm_dep[1]))
                            sq_dep = ("v", b.inc("v", nc.vector.tensor_mul(
                                out=sqb[1][:, 0:SQ], in0=xg[c][:, 0:SQ],
                                in1=xg[c][:, 0:SQ])))
                        off = 0 if c >= 4 else 1
                        b.wt(nc.tensor, sq_dep)
                        i = None
                        for n in range(NB4):
                            sl = slice(off + 512 * n, off + 512 * (n + 1))
                            nc.tensor.matmul(pm[0:1, 512 * n:512 * (n + 1)],
                                             ln1[:], xb[c][:, sl],
                                             start=(c == 0), stop=(c == 7))
                        for n in range(NB4):
                            sl = slice(off + 512 * n, off + 512 * (n + 1))
                            i = nc.tensor.matmul(
                                pq[0:1, 512 * n:512 * (n + 1)], ln1[:],
                                sqb[bufi][:, sl], start=(c == 0),
                                stop=(c == 7))
                        mm_dep[bufi] = b.inc("t", i)
                    scrB = b.ln_finalize(pm, pq, mrow, rrow)
                b.barrier()

        with contextlib.ExitStack() as sc:
            h2n = [sc.enter_context(b.sb(f"h2n{c}", [128, S], BF16))
                   for c in range(NCH)]
            lngc, lnbc = ct[f"lng{l}"], ct[f"lnb{l}"]

            with contextlib.ExitStack() as fsc:
                w1t = [fsc.enter_context(
                    b.sb(f"w1t{i}", [128, 8 * 2048], BF16))
                    for i in range(2)]
                w2c = [fsc.enter_context(
                    b.sb(f"w2c{i}", [128, HIDP], BF16)) for i in range(2)]
                h4all = fsc.enter_context(
                    b.sb("h4all", [128, NJ * 512], BF16))
                gel = [fsc.enter_context(
                    b.sb(f"gel{i}", [128, 512], BF16))
                    for i in range(2)]
                rowsB = fsc.enter_context(b.sb("rowsB", [1, 1024], F32))
                tfaB = fsc.enter_context(b.sb("tfaB", [128, 512], F32))
                xcol = fsc.enter_context(b.sb("xcolB", [128, 4], F32))
                w1d = b.dr[f"w1pp{l}"]
                w2d = b.dr[f"w2pt{l}"]
                b1c = ct[f"b1pp{l}"]
                b2c = ct[f"b2{l}"]
                thirds = [(0, 8), (8, 16), (16, 22)]

                def load_third(tix, buf, dep):
                    j0, j1 = thirds[tix]
                    w = 256 * (j1 - j0)
                    gd = 0
                    for k in range(NCH):
                        gd = b.gload(
                            w1t[buf][:, 2048 * k:2048 * k + w],
                            w1d[128 * k:128 * (k + 1), 256 * j0:256 * j1],
                            ("t", dep) if k == 0 else ("t", 0))
                    return gd

                with (b.ps("pag0", [128, 1024], F32) as pag0,
                      b.ps("pag1", [128, 1024], F32) as pag1,
                      b.ps("pW20", [128, 512], F32) as pW20,
                      b.ps("pW21", [128, 512], F32) as pW21,
                      b.ps("pmbB", [128, 512], F32) as pmbB,
                      b.ps("prbB", [128, 512], F32) as prbB):
                    pags = [pag0, pag1]
                    pW2s = [pW20, pW21]
                    gel_dep = [0, 0]
                    h4_dep = [0, 0]
                    w1gd = [0, 0]
                    w1_used = [0, 0]
                    gthird = 0
                    w1gd[0] = load_third(0, 0, 0)
                    gd_w2 = [0, 0]
                    w2_used = [0, 0]
                    gd_w2[0] = b.gload(w2c[0][:], w2d[0:128, :])
                    res_dep2 = [0, 0]
                    w2_done = 0

                    onesf = ct["onesf"]
                    ap_done = [0]
                    bc_dep = [0]
                    h2n_rdy = [0, 0, 0, 0]

                    def apply_begin(n):
                        nsl = slice(512 * n, 512 * (n + 1))
                        b.sload(rowsB[0:1, 0:512], b.lnmr[0:1, nsl],
                                ("sd", scrB), ("t", bc_dep[0]))
                        sd = b.sload(rowsB[0:1, 512:1024], b.lnrs[0:1, nsl])
                        b.wt(nc.tensor, ("sd", sd), ("v", ap_done[0]))
                        nc.tensor.matmul(
                            pmbB[:, 0:512], onesf[:], rowsB[0:1, 0:512],
                            start=True, stop=True)
                        bc_dep[0] = b.inc("t", nc.tensor.matmul(
                            prbB[:, 0:512], onesf[:], rowsB[0:1, 512:1024],
                            start=True, stop=True))

                    def apply_c(n, c, xc=None):
                        tfa = tfaB[:]
                        nsl = slice(512 * n, 512 * (n + 1))
                        off = 0 if c >= 4 else 1
                        hsl = slice(off + 512 * n, off + 512 * (n + 1))
                        b.wt(nc.vector, ("t", bc_dep[0]))
                        if xc is None:
                            nc.vector.tensor_mul(out=tfa, in0=xg[c][:, hsl],
                                                 in1=prbB[:, 0:512])
                        else:
                            nc.vector.tensor_mul(out=tfa[:, 0:1], in0=xc,
                                                 in1=prbB[:, 0:1])
                            nc.vector.tensor_mul(
                                out=tfa[:, 1:512],
                                in0=xg[c][:, hsl.start + 1:hsl.stop],
                                in1=prbB[:, 1:512])
                        nc.vector.tensor_sub(out=tfa, in0=tfa,
                                             in1=pmbB[:, 0:512])
                        tick = b.inc("v", nc.vector.tensor_scalar(
                            out=h2n[c][:, nsl], in0=tfa,
                            scalar1=lngc[:, c:c + 1],
                            scalar2=lnbc[:, c:c + 1],
                            op0=ALU.mult, op1=ALU.add))
                        ap_done[0] = tick
                        h2n_rdy[n] = tick

                    apply_begin(0)
                    for c in range(NCH):
                        apply_c(0, c)

                    for nb in range(4):
                        nsl = slice(512 * nb, 512 * (nb + 1))
                        for j in range(NJ):
                            jb = j % 2
                            tix = 0 if j < 8 else (1 if j < 16 else 2)
                            tbuf = gthird % 2
                            j0 = thirds[tix][0]
                            if j in (0, 8, 16):
                                if not (nb == 3 and j == 16):
                                    nbuf = (gthird + 1) % 2
                                    w1gd[nbuf] = load_third(
                                        (tix + 1) % 3, nbuf, w1_used[nbuf])
                            pa = pags[jb][:, 0:512]
                            pg = pags[jb][:, 512:1024]
                            b.wt(nc.tensor, ("gd", w1gd[tbuf]),
                                 ("a", gel_dep[jb]), ("v", h4_dep[jb]),
                                 ("v", h2n_rdy[nb] if j == 0 else 0))
                            jcol = 256 * (j - j0)
                            i = None
                            for k in range(NCH):
                                base = 2048 * k + jcol
                                nc.tensor.matmul(
                                    pa, w1t[tbuf][:, base:base + 128],
                                    h2n[k][:, nsl], start=(k == 0),
                                    stop=(k == 7))
                            for k in range(NCH):
                                base = 2048 * k + jcol + 128
                                i = nc.tensor.matmul(
                                    pg, w1t[tbuf][:, base:base + 128],
                                    h2n[k][:, nsl], start=(k == 0),
                                    stop=(k == 7))
                            td = b.inc("t", i)
                            w1_used[tbuf] = td
                            b.wt(nc.scalar, ("t", td), ("v", h4_dep[jb]))
                            gel_dep[jb] = b.inc("a", nc.scalar.activation(
                                out=gel[jb][:], in_=pg, func=AF.Gelu,
                                bias=b1c[:, 2 * j + 1:2 * j + 2]))
                            b.wt(nc.vector, ("a", gel_dep[jb]),
                                 ("t", w2_done))
                            h4_dep[jb] = b.inc(
                                "v", nc.vector.scalar_tensor_tensor(
                                    out=h4all[:, 512 * j:512 * (j + 1)],
                                    in0=pa, scalar=b1c[:, 2 * j:2 * j + 1],
                                    in1=gel[jb][:], op0=ALU.add, op1=ALU.mult))
                            if j in (7, 15, 21):
                                gthird += 1
                        if nb + 1 < 4:
                            for c in range(4, NCH):
                                nc.vector.tensor_copy(
                                    out=xcol[:, c - 4:c - 3],
                                    in_=xg[c][:, 512 * (nb + 1):
                                              512 * (nb + 1) + 1])
                            apply_begin(nb + 1)
                            for c in range(2):
                                apply_c(nb + 1, c, None)
                        last_w2 = 0
                        for m in range(NCH):
                            mb = m % 2
                            m_, nb_ = (m + 1, nb) if m + 1 < NCH else (0, nb + 1)
                            if nb_ < 4:
                                gd_w2[(m + 1) % 2] = b.gload(
                                    w2c[(m + 1) % 2][:],
                                    w2d[128 * m_:128 * (m_ + 1), :],
                                    ("t", w2_used[(m + 1) % 2]))
                            b.wt(nc.tensor, ("gd", gd_w2[mb]),
                                 ("v", h4_dep[0]), ("v", h4_dep[1]),
                                 ("v", res_dep2[mb]))
                            i = None
                            for j in range(NJ):
                                i = nc.tensor.matmul(
                                    pW2s[mb][:],
                                    w2c[mb][:, 128 * j:128 * (j + 1)],
                                    h4all[:, 512 * j:512 * (j + 1)],
                                    start=(j == 0), stop=(j == NJ - 1))
                            td = b.inc("t", i)
                            w2_used[mb] = td
                            last_w2 = td
                            xsl = slice(1 + 512 * nb, 1 + 512 * (nb + 1))
                            b.wt(nc.vector, ("t", td))
                            res_dep2[mb] = b.inc(
                                "v", nc.vector.scalar_tensor_tensor(
                                    out=xg[m][:, xsl], in0=pW2s[mb][:],
                                    scalar=b2c[:, m:m + 1], in1=xg[m][:, xsl],
                                    op0=ALU.add, op1=ALU.add))
                            if nb + 1 < 4 and m + 2 < NCH:
                                c = m + 2
                                apply_c(nb + 1, c,
                                        xcol[:, c - 4:c - 3] if c >= 4
                                        else None)
                        w2_done = last_w2
                    b.barrier()
                b.ckpt(f"ffn{l}")
        if l == 0 and "g0" in b.dumps:
            for c in range(NCH):
                b.sload(b.dumps["g0"][128 * c:128 * (c + 1), :], xg[c][:, 1:SQ])
            b.barrier()
    if "gpre" in b.dumps:
        for c in range(NCH):
            b.sload(b.dumps["gpre"][128 * c:128 * (c + 1), :], xg[c][:, 1:SQ])
        b.barrier()

    with contextlib.ExitStack() as sc:
        xbs = [sc.enter_context(b.sb(f"xbs{c}", [128, S], BF16))
               for c in range(NCH)]
        oun_s = [sc.enter_context(b.sb(f"osn{i}", [128, S], BF16))
                 for i in range(NCH)]
        ssums = sc.enter_context(b.sb("ssums", [16, S], F32))
        srow_s = sc.enter_context(b.sb("srow_s", [1, 1024], F32))
        srb_bf = sc.enter_context(b.sb("srb", [16, S], BF16))
        selg16_t = sc.enter_context(b.sb("selg16_t", [16, 1024], BF16))
        selg16_sd = b.sload(selg16_t[:], b.dr["selg16_bf"])
        cbias_s = sc.enter_context(b.sb("cbias_s", [128, 16], F32))

        for c in range(NCH):
            b.wt(nc.scalar, b.now("t"), b.now("v"))
            b.inc("a", nc.scalar.activation(out=xg[c][:, 1:SQ],
                                            in_=xg[c][:, 1:SQ], func=AF.Silu))

        with contextlib.ExitStack() as lsc:
            tmpf = [lsc.enter_context(b.sb(f"tmpf{i}", [128, S], F32))
                    for i in range(3)]
            mrow = lsc.enter_context(b.sb("mrowS", [1, S], F32))
            rrow = lsc.enter_context(b.sb("rrowS", [1, S], F32))
            with (b.ps("pmbs", [128, S], F32) as pmb,
                  b.ps("prbs", [128, S], F32) as prb):
                tdep = b.ln_bcast(pmb, prb, mrow, rrow, b.scr0_sd,
                                  src=(b.lnmr0, b.lnrs0))
                dep_v = [0, 0, 0]
                for c in range(NCH):
                    buf = tmpf[c % 3]
                    if c % 2 == 0:
                        dd = ("sd", b.sload(buf[:], xd[128 * c:128 * (c + 1), :],
                                            ("v", dep_v[c % 3])))
                    else:
                        dd = ("gd", b.gload(buf[:], xd[128 * c:128 * (c + 1), :],
                                            ("v", dep_v[c % 3])))
                    b.wt(nc.vector, dd, ("t", tdep))
                    nc.vector.tensor_mul(out=buf[:], in0=buf[:], in1=prb[:])
                    dep_v[c % 3] = b.inc("v", nc.vector.tensor_sub(
                        out=xbs[c][:], in0=buf[:], in1=pmb[:]))
            b.barrier()
        b.ckpt("sln")

        with contextlib.ExitStack() as asc:
            qh2 = [asc.enter_context(b.sb(f"qh{i}", [128, S], BF16))
                   for i in range(2)]
            kh2 = [asc.enter_context(b.sb(f"kh{i}", [128, S], BF16))
                   for i in range(2)]
            vo4 = [asc.enter_context(b.sb(f"vos{i}", [128, 16 * 65], BF16))
                   for i in range(4)]
            wqs = [asc.enter_context(b.sb(f"wqss{i}", [128, D], BF16))
                   for i in range(2)]
            wks = [asc.enter_context(b.sb(f"wkss{i}", [128, D], BF16))
                   for i in range(2)]
            wvs = [asc.enter_context(b.sb(f"wvss{i}", [128, D], BF16))
                   for i in range(2)]
            nmax = asc.enter_context(b.sb("nmax", [2, 8], F32))
            t21 = asc.enter_context(b.sb("t21", [2, 8], F32))
            vbts = asc.enter_context(b.sb("vbts", [128, 512], BF16))
            tbs2 = [asc.enter_context(b.sb(f"tbs2{i}", [128, 512], BF16))
                    for i in range(2)]
            pexp = [asc.enter_context(b.sb(
                f"pexps{i}", [128, 512], BF16)) for i in range(4)]

            for vv in vo4:
                oc = vv.ap().rearrange("p (u c) -> p u c", c=65)[:, :, 64:65]
                b.inc("gc", nc.gpsimd.memset(oc, 1.0))
            b.barrier()

            with (b.ps("pPs0", [128, 512], F32) as pP0,
                  b.ps("pPs1", [128, 512], F32) as pP1,
                  b.ps("pqka", [128, 512], F32) as pqka,
                  b.ps("pqkb", [128, 512], F32) as pqkb,
                  b.ps("pTs0", [128, 128], BF16) as pT0,
                  b.ps("pTs1", [128, 128], BF16) as pT1,
                  b.ps("spva", [65, 512], F32) as ppva,
                  b.ps("spvb", [65, 512], F32) as ppvb):
                pPs = [pP0, pP1]
                ast = {"expA": 0, "expB": 0, "evacA": 0, "evacB": 0,
                       "evacV": 0, "pv": 0, "pv2": [0, 0], "ep": 0}
                evac_dep = [0, 0]
                tr_dep = [0, 0]
                nrm_dep2 = [0, 0]
                vo_guard = [0, 0]
                nmax_rd = [0, 0]
                piL = [0]
                npend = [None]
                entry = {}

                def emit_norm(gcd_):
                    (pPb, pib, col, sq_) = npend[0]
                    npend[0] = None
                    b.wt(nc.tensor, ("a", sq_))
                    td = b.inc("t", nc.tensor.matmul(
                        pPb[0:2, 0:512], ct["sel2"][:], tbs2[pib][:],
                        start=True, stop=True))
                    b.wt(nc.vector, ("t", td), ("gc", gcd_))
                    nc.vector.tensor_reduce(
                        out=t21[0:2, col:col + 1], in_=pPb[0:2, 0:512],
                        axis=AX.X, op=ALU.max)
                    return b.inc("v", nc.vector.tensor_max(
                        out=nmax[0:2, col:col + 1],
                        in0=nmax[0:2, col:col + 1],
                        in1=t21[0:2, col:col + 1]))

                wall_free = [0, 0]
                swgd = [0, 0]

                def load_sw(hp):
                    wb = hp % 2
                    gd = 0
                    for wtile, wname in ((wqs[wb], "wq_t"), (wks[wb], "wk_t"),
                                         (wvs[wb], "wv_t")):
                        wsrc = b.dr[wname].rearrange("(k p) c -> p k c", p=128)
                        gd = b.gload(
                            wtile.ap().rearrange("p (k c) -> p k c", c=128),
                            wsrc[:, :, 128 * hp:128 * (hp + 1)],
                            ("t", wall_free[wb]) if wtile is wqs[wb]
                            else ("t", 0))
                    return gd

                def s_proj_gen(hp):
                    par = hp % 2
                    qh = qh2[par]
                    kh = kh2[par]
                    vo_a, vo_b = vo4[2 * par], vo4[2 * par + 1]
                    if hp + 1 < 8:
                        swgd[(hp + 1) % 2] = load_sw(hp + 1)
                    gddep = ("gd", swgd[hp % 2])
                    b.wt(nc.gpsimd, ("a", nmax_rd[par]), b.now("v"))
                    gcd = b.inc("gc", nc.gpsimd.memset(
                        nmax[:, 4 * par:4 * par + 2], 0.0))
                    proj_evac = 0
                    for tsel in range(3):
                        wtile = (wqs[par], wks[par], wvs[par])[tsel]
                        dsts = (qh, kh, None)[tsel]
                        bias_col = (ct["bq"], ct["bk"],
                                    ct["bv"])[tsel][:, hp:hp + 1]
                        for n in range(NB4):
                            pi = piL[0]
                            pP = pPs[pi % 2]
                            b.wt(nc.tensor, gddep, ("a", evac_dep[pi % 2]),
                                 ("v", nrm_dep2[pi % 2]))
                            i = None
                            for k in range(NCH):
                                sl = slice(512 * n, 512 * (n + 1))
                                i = nc.tensor.matmul(
                                    pP[:], wtile[:, 128 * k:128 * (k + 1)],
                                    xbs[k][:, sl], start=(k == 0),
                                    stop=(k == 7))
                                if k == 3:
                                    yield
                            mmdep = b.inc("t", i)
                            if npend[0] is not None:
                                nrm_dep2[(pi + 1) % 2] = emit_norm(gcd)
                            if tsel < 2:
                                nsl = slice(512 * n, 512 * (n + 1))
                                b.wt(nc.scalar, ("t", mmdep))
                                nc.scalar.activation(out=dsts[:, nsl],
                                                     in_=pP[:],
                                                     func=AF.Identity,
                                                     bias=bias_col)
                                evac_dep[pi % 2] = b.inc(
                                    "a", nc.scalar.activation(
                                        out=tbs2[pi % 2][:],
                                        in_=dsts[:, nsl], func=AF.Square))
                                proj_evac = evac_dep[pi % 2]
                                npend[0] = (pP, pi % 2, 4 * par + tsel,
                                            evac_dep[pi % 2])
                            else:
                                evac_dep[pi % 2] = b.v_block(
                                    pP, n, vbts, vo_a, vo_b, [pT0, pT1],
                                    tr_dep, bias=bias_col,
                                    vo_guard=vo_guard[par])
                            piL[0] += 1
                            yield
                    wall_free[par] = b.cnt["t"]
                    if npend[0] is not None:
                        nrm_dep2[(piL[0] + 1) % 2] = emit_norm(gcd)
                    c0 = 4 * par
                    b.wt(nc.scalar, ("v", nrm_dep2[0]), ("v", nrm_dep2[1]))
                    nc.scalar.activation(out=t21[0:2, c0:c0 + 1],
                                         in_=nmax[0:2, c0:c0 + 1],
                                         func=AF.Sqrt)
                    ad = b.inc("a", nc.scalar.activation(
                        out=t21[0:2, c0 + 1:c0 + 2],
                        in_=nmax[0:2, c0 + 1:c0 + 2], func=AF.Sqrt))
                    nmax_rd[par] = ad
                    b.wt(nc.vector, ("a", ad))
                    nc.vector.tensor_mul(out=t21[0:2, c0 + 2:c0 + 3],
                                         in0=t21[0:2, c0:c0 + 1],
                                         in1=t21[0:2, c0 + 1:c0 + 2])
                    vd = b.inc("v", nc.vector.tensor_scalar_mul(
                        out=t21[0:2, c0 + 2:c0 + 3],
                        in0=t21[0:2, c0 + 2:c0 + 3], scalar1=-0.125))
                    b.wt(nc.tensor, ("v", vd))
                    nc.tensor.matmul(pP0[:, 0:1], ct["sel2f"][:, 0:128],
                                     t21[0:2, c0 + 2:c0 + 3],
                                     start=True, stop=True)
                    td = b.inc("t", nc.tensor.matmul(
                        pP0[:, 1:2], ct["sel2f"][:, 128:256],
                        t21[0:2, c0 + 2:c0 + 3], start=True, stop=True))
                    b.wt(nc.vector, ("t", td))
                    cb_tick = b.inc("v", nc.vector.tensor_copy(
                        out=cbias_s[:, 2 * hp:2 * hp + 2], in_=pP0[:, 0:2]))
                    nrm_dep2[0] = max(nrm_dep2[0], cb_tick)
                    entry[hp] = (("a", proj_evac), ("v", cb_tick))

                att = None
                swgd[0] = load_sw(0)
                for hp in range(8):
                    pg = s_proj_gen(hp)
                    for _ in pg:
                        for _i in range(2):
                            if att is not None:
                                try:
                                    next(att)
                                except StopIteration:
                                    att = None
                    while att is not None:
                        try:
                            next(att)
                        except StopIteration:
                            att = None
                    par = hp % 2
                    att = b.att_pair_gen(
                        "s", qh2[par], kh2[par], vo4[2 * par], vo4[2 * par + 1],
                        cbias_s[:, 2 * hp:2 * hp + 1],
                        cbias_s[:, 2 * hp + 1:2 * hp + 2],
                        0.125, oun_s[hp], ssums, 2 * hp, srow_s,
                        pqka, pqkb, ppva, ppvb,
                        [pexp[0], pexp[1]], [pexp[2], pexp[3]], ast,
                        entry_deps=entry[hp])
                    vo_guard[par] = 0
                while att is not None:
                    try:
                        next(att)
                    except StopIteration:
                        att = None
                b.barrier()

        with contextlib.ExitStack() as fsc:
            wos = fsc.enter_context(b.sb("wos", [128, 8192], BF16))
            ostg = [fsc.enter_context(b.sb(
                f"ostg{i}", [128, 512], F32)) for i in range(2)]
            tmpf2 = [fsc.enter_context(b.sb(
                f"tmf2{i}", [128, S], F32)) for i in range(2)]
            gdw = 0
            for k in range(NCH):
                gdw = b.gload(wos[:, 1024 * k:1024 * (k + 1)],
                              b.dr["wo_t"][128 * k:128 * (k + 1), :])
            sdm = [0, 0]
            sdm[0] = b.sload(tmpf2[0][:], xd[0:128, :])
            sdm[1] = b.sload(tmpf2[1][:], xd[128:256, :])

            nc.scalar.activation(out=ssums[:], in_=ssums[:], func=AF.Ln)
            ad = b.inc("a", nc.scalar.activation(out=srb_bf[:], in_=ssums[:],
                                                 func=AF.Exp, scale=-1.0))
            with (b.ps("prns0", [128, 512], F32) as prn0,
                  b.ps("prns1", [128, 512], F32) as prn1):
                prns = [prn0, prn1]
                od = [0, 0]
                oi = 0
                for hp in range(8):
                    for n in range(NB4):
                        nsl = slice(512 * n, 512 * (n + 1))
                        prn = prns[oi % 2]
                        b.wt(nc.tensor, ("a", ad), ("v", od[oi % 2]),
                             ("sd", selg16_sd))
                        td = b.inc("t", nc.tensor.matmul(
                            prn[:], selg16_t[:, 128 * hp:128 * (hp + 1)],
                            srb_bf[0:16, nsl], start=True, stop=True))
                        b.wt(nc.vector, ("t", td))
                        od[oi % 2] = b.inc("v", nc.vector.tensor_mul(
                            out=oun_s[hp][:, nsl], in0=oun_s[hp][:, nsl],
                            in1=prn[:]))
                        oi += 1
                b.barrier()

            boc = ct["bo"]
            with (b.ps("pS0f", [128, 512], F32) as pS0,
                  b.ps("pS1f", [128, 512], F32) as pS1):
                pSs = [pS0, pS1]
                si = 0
                vdep_s = [0, 0]
                st_dep = [0, 0]
                for m in range(NCH):
                    sd = sdm[m % 2]
                    for n in range(NB4):
                        pS = pSs[si % 2]
                        b.wt(nc.tensor, ("gd", gdw), ("v", vdep_s[si % 2]))
                        i = None
                        for k in range(NCH):
                            i = nc.tensor.matmul(
                                pS[:], wos[:, 1024 * k + 128 * m:
                                            1024 * k + 128 * (m + 1)],
                                oun_s[k][:, 512 * n:512 * (n + 1)],
                                start=(k == 0), stop=(k == 7))
                        td = b.inc("t", i)
                        nsl = slice(512 * n, 512 * (n + 1))
                        xsl = slice(1 + 512 * n, 1 + 512 * (n + 1))
                        ob = ostg[si % 2]
                        b.wt(nc.vector, ("t", td), ("sd", sd),
                             ("sd", st_dep[si % 2]))
                        nc.vector.scalar_tensor_tensor(
                            out=ob[:], in0=pS[:], scalar=boc[:, m:m + 1],
                            in1=xg[m][:, xsl], op0=ALU.add, op1=ALU.mult)
                        vdep_s[si % 2] = b.inc("v", nc.vector.tensor_add(
                            out=ob[:], in0=ob[:], in1=tmpf2[m % 2][:, nsl]))
                        st_dep[si % 2] = b.sload(
                            b.out_d[128 * m:128 * (m + 1), nsl], ob[:],
                            ("v", vdep_s[si % 2]))
                        si += 1
                    if m + 2 < NCH:
                        sdm[m % 2] = b.sload(
                            tmpf2[m % 2][:],
                            xd[128 * (m + 2):128 * (m + 3), :],
                            ("v", vdep_s[0]), ("v", vdep_s[1]))
                b.barrier()


def kernel(**inputs):
    x = np.asarray(inputs["L0_attn_x"], np.float32)
    shared = _prep(inputs)
    nc = _build()
    in_maps = []
    for c in range(8):
        m = dict(shared)
        m["x"] = np.ascontiguousarray(x[c].T)
        in_maps.append(m)
    res = run_bass_kernel_spmd(nc, in_maps, core_ids=list(range(8)))
    out = np.stack([np.asarray(r["out"], np.float32).T for r in res.results], 0)
    return np.ascontiguousarray(out)
